# version 22
# baseline (speedup 1.0000x reference)
import numpy as np
import sys

sys.path.insert(0, "/opt/trn_rl_repo")
import ml_dtypes
import concourse.bass as bass
import concourse.mybir as mybir
from concourse.bass_utils import run_bass_kernel_spmd
from contextlib import ExitStack

F32 = mybir.dt.float32
BF16 = mybir.dt.bfloat16
AF = mybir.ActivationFunctionType
ALU = mybir.AluOpType
AX = mybir.AxisListType

D = 1024
H = 16
NKC = 8
NEG = -1e9
LN8 = float(np.log(0.125))


class Ctx:

    DMAQ = {"sp_c": "sp", "sp_x": "sp", "sp_f": "sp", "actq": "act",
            "poolq": "pool", "pq0": "pool", "pq1": "pool", "pq2": "pool",
            "pq3": "pool", "pq4": "pool",
            "sq0": "sp", "sq1": "sp", "sq2": "sp"}

    def __init__(self, nc, stack):
        self.nc = nc
        e = stack.enter_context
        names = ("pe", "dve", "act", "pool", "sp") + tuple(self.DMAQ)
        self.sems = {n: e(nc.semaphore(f"sem_{n}")) for n in names}
        self.count = {n: 0 for n in names}
        self.waited = {n: {} for n in ("pe", "dve", "act", "pool", "sp")}
        self.queues = {n: [] for n in ("pe", "dve", "act", "pool", "sp")}
        self.bufs = {}

    def _buf(self, name):
        return self.bufs.setdefault(name, {"w": None, "r": {}})

    def _wait(self, xeng, sem_name, val):
        if xeng == "pe" and sem_name == "pe":
            return
        hw = self.waited[xeng].get(sem_name, 0)
        if val <= hw:
            return
        self.waited[xeng][sem_name] = val
        sem = self.sems[sem_name]
        self.queues[xeng].append(lambda eng, s=sem, v=val: eng.wait_ge(s, v))

    def _dma_adj(self, sname, v):
        if sname in self.DMAQ:
            return self.count[sname]
        return v

    def _hazards(self, xeng, reads, writes):
        for b in reads:
            rec = self._buf(b)
            if rec["w"]:
                sname, v = rec["w"]
                self._wait(xeng, sname, self._dma_adj(sname, v))
        for b in writes:
            rec = self._buf(b)
            if rec["w"]:
                sname, v = rec["w"]
                self._wait(xeng, sname, self._dma_adj(sname, v))
            for sname, v in rec["r"].items():
                self._wait(xeng, sname, self._dma_adj(sname, v))

    def _record(self, key, reads, writes, val):
        for b in reads:
            self._buf(b)["r"][key] = max(self._buf(b)["r"].get(key, 0), val)
        for b in writes:
            rec = self._buf(b)
            rec["w"] = (key, val)
            rec["r"] = {}

    def op(self, eng, fn, reads=(), writes=()):
        self._hazards(eng, reads, writes)
        sem = self.sems[eng]
        self.count[eng] += 1
        val = self.count[eng]

        def emit(engine, fn=fn, sem=sem):
            fn(engine).then_inc(sem, 1)
        self.queues[eng].append(emit)
        self._record(eng, reads, writes, val)
        return val

    def dma(self, q, out_ap, in_ap, reads=(), writes=(), **kw):
        xeng = self.DMAQ[q]
        self._hazards(xeng, reads, writes)
        sem = self.sems[q]
        self.count[q] += 16
        val = self.count[q]

        def emit(engine, out_ap=out_ap, in_ap=in_ap, sem=sem, kw=kw):
            engine.dma_start(out_ap, in_ap, **kw).then_inc(sem, 16)
        self.queues[xeng].append(emit)
        self._record(q, reads, writes, val)
        return val

    def finish(self):
        for q in self.DMAQ:
            sem, v = self.sems[q], self.count[q]
            if v:
                self.queues["sp"].append(
                    lambda eng, s=sem, v=v: eng.wait_ge(s, v))

    def finalize(self, block):
        qs = self.queues

        @block.tensor
        def _(eng):
            for f in qs["pe"]:
                f(eng)

        @block.vector
        def _(eng):
            for f in qs["dve"]:
                f(eng)

        @block.scalar
        def _(eng):
            for f in qs["act"]:
                f(eng)

        @block.gpsimd
        def _(eng):
            for f in qs["pool"]:
                f(eng)

        @block.sync
        def _(eng):
            for f in qs["sp"]:
                f(eng)


def host_constants():
    c = {}
    c["c_ident_bf"] = np.eye(128, dtype=ml_dtypes.bfloat16)
    c["c_ident_f32"] = np.eye(128, dtype=np.float32)
    c["c_ones1"] = np.ones((1, 128), dtype=np.float32)
    tri = np.triu(np.ones((64, 64), np.float32))
    blk = np.zeros((128, 128), np.float32)
    blk[:64, :64] = tri
    blk[64:, 64:] = tri
    c["c_cumsum"] = blk.astype(ml_dtypes.bfloat16)
    er = np.zeros((16, 1024), np.float32)
    for h in range(16):
        er[h, h * 64:(h + 1) * 64] = 1.0
    c["c_erep"] = er
    c["c_erep_bf"] = er.astype(ml_dtypes.bfloat16)
    def aug(mat, row):
        m = np.zeros((67, 64), np.float32)
        m[0:64] = mat
        m[row] = 1.0
        return m
    c["c_tril"] = aug(np.tril(np.ones((64, 64), np.float32)), 64)
    c["c_triu"] = aug(np.triu(np.ones((64, 64), np.float32)), 65)
    c["c_triu1"] = aug(np.triu(np.ones((64, 64), np.float32), 1), 66)
    negI = np.zeros((64, 1024), np.float32)
    irep = np.zeros((64, 1024), np.float32)
    for h in range(16):
        negI[:, h * 64:(h + 1) * 64] = NEG * np.eye(64, dtype=np.float32)
        irep[:, h * 64:(h + 1) * 64] = np.eye(64)
    c["c_negI"] = negI
    c["c_irep_bf"] = irep.astype(ml_dtypes.bfloat16)
    return c


def layout_inputs(inp):
    d = {}
    d["w_qkvg"] = np.ascontiguousarray(np.concatenate(
        [inp["L0_attn_Wq"], inp["L0_attn_Wk"], inp["L0_attn_Wv"],
         inp["L0_attn_Wg"]], axis=1), dtype=np.float32)
    d["w_o"] = np.ascontiguousarray(inp["L0_attn_Wo"], dtype=np.float32)
    d["w_ab"] = np.ascontiguousarray(np.concatenate(
        [inp["L0_attn_Wb"], inp["L0_attn_Wa"]], axis=1), dtype=np.float32)
    d["dtb_row"] = np.concatenate(
        [np.zeros(16, np.float32),
         np.asarray(inp["L0_attn_dt_bias"], np.float32)]).reshape(1, 32)
    d["a_log"] = np.asarray(inp["L0_attn_A_log"], np.float32).reshape(16, 1)
    d["nw_rep"] = np.ascontiguousarray(np.broadcast_to(
        np.tile(np.asarray(inp["L0_attn_norm_w"], np.float32), 16),
        (128, 1024)))
    d["w_gate"] = np.ascontiguousarray(
        np.asarray(inp["L0_ffn_gate_w"], np.float32).T)
    d["b_gate"] = np.asarray(inp["L0_ffn_gate_b"], np.float32).reshape(1, 8)
    d["w_exp"] = np.ascontiguousarray(inp["L0_ffn_expert_w"], dtype=np.float32)
    d["b_exp"] = np.ascontiguousarray(inp["L0_ffn_expert_b"], dtype=np.float32)
    return d


def build_program(T):
    SEG = min(512, T)
    NSEG = T // SEG
    NTT = SEG // 128
    NCH = SEG // 64
    nc = bass.Bass("TRN2", target_bir_lowering=False)

    x_ap = nc.dram_tensor("x", [T, D], F32, kind="ExternalInput").ap()
    out_ap = nc.dram_tensor("out", [T, D], F32, kind="ExternalOutput").ap()
    wqkvg_ap = nc.dram_tensor("w_qkvg", [D, 4096], F32, kind="ExternalInput").ap()
    wo_ap = nc.dram_tensor("w_o", [D, D], F32, kind="ExternalInput").ap()
    wab_ap = nc.dram_tensor("w_ab", [D, 32], F32, kind="ExternalInput").ap()
    dtb_ap = nc.dram_tensor("dtb_row", [1, 32], F32, kind="ExternalInput").ap()
    alog_ap = nc.dram_tensor("a_log", [16, 1], F32, kind="ExternalInput").ap()
    nwrep_ap = nc.dram_tensor("nw_rep", [128, D], F32, kind="ExternalInput").ap()
    wgate_ap = nc.dram_tensor("w_gate", [D, 8], F32, kind="ExternalInput").ap()
    bgate_ap = nc.dram_tensor("b_gate", [1, 8], F32, kind="ExternalInput").ap()
    wexp_ap = nc.dram_tensor("w_exp", [8, D, D], F32, kind="ExternalInput").ap()
    bexp_ap = nc.dram_tensor("b_exp", [8, D], F32, kind="ExternalInput").ap()
    consts = host_constants()
    c_ap = {k: nc.dram_tensor(k, list(v.shape),
                              BF16 if v.dtype == ml_dtypes.bfloat16 else F32,
                              kind="ExternalInput").ap()
            for k, v in consts.items()}
    scr_ap = nc.dram_tensor("scr", [64, SEG], F32, kind="Internal").ap()

    stack = ExitStack()
    e = stack.enter_context
    ctx = Ctx(nc, stack)

    PS = [e(nc.psum_tensor(f"ps{i}", [128, 512], F32)) for i in range(7)]
    PB = [e(nc.psum_tensor(f"pb{i}", [128, 1024], BF16)) for i in range(1)]

    s_ident_bf = e(nc.sbuf_tensor("s_ident_bf", [128, 128], BF16))
    s_ident_f32 = e(nc.sbuf_tensor("s_ident_f32", [128, 128], F32))
    s_ones1 = e(nc.sbuf_tensor("s_ones1", [1, 128], F32))
    s_cumsum = e(nc.sbuf_tensor("s_cumsum", [128, 128], BF16))
    s_decb = e(nc.sbuf_tensor("s_decb", [128, 16], BF16))
    s_erep = e(nc.sbuf_tensor("s_erep", [16, 1024], F32))
    s_tril = e(nc.sbuf_tensor("s_tril", [67, 64], F32))
    s_triu = e(nc.sbuf_tensor("s_triu", [67, 64], F32))
    s_triu1 = e(nc.sbuf_tensor("s_triu1", [67, 64], F32))
    s_nega = e(nc.sbuf_tensor("s_nega", [67, 1024], F32))
    s_irep = e(nc.sbuf_tensor("s_irep", [64, 1024], BF16))
    s_nwrep = e(nc.sbuf_tensor("s_nwrep", [128, D], BF16))
    s_dtb = e(nc.sbuf_tensor("s_dtb", [1, 32], F32))
    s_alog = e(nc.sbuf_tensor("s_alog", [16, 1], F32))
    s_nAb = e(nc.sbuf_tensor("s_nAb", [128, 16], F32))
    s_wab = e(nc.sbuf_tensor("s_wab", [128, 256], BF16))
    s_wgate = e(nc.sbuf_tensor("s_wgate", [128, 64], BF16))
    s_bgate = e(nc.sbuf_tensor("s_bgate", [1, 8], F32))
    s_bexp = e(nc.sbuf_tensor("s_bexp", [8, D], BF16))
    s_tmprow = e(nc.sbuf_tensor("s_tmprow", [1, 16], F32))

    NSL = 3
    s_slab = [e(nc.sbuf_tensor(f"s_slab{j}", [128, 4096], BF16)) for j in range(NSL)]

    s_x = [e(nc.sbuf_tensor("s_x0", [128, D], F32))] * 2
    s_xc = e(nc.sbuf_tensor("s_xc", [128, D], F32))
    s_stat = e(nc.sbuf_tensor("s_stat", [128, 8], F32))
    s_xn_tm = e(nc.sbuf_tensor("s_xn_tm", [128, D], BF16))
    s_xn_fm = [e(nc.sbuf_tensor(f"s_xn_fm{k}", [128, SEG], BF16)) for k in range(NKC)]

    s_ab = e(nc.sbuf_tensor("s_ab", [128, NTT * 32], F32))
    s_beta = e(nc.sbuf_tensor("s_beta", [128, NTT * 16], F32))
    s_sp = e(nc.sbuf_tensor("s_sp", [128, NTT * 16], F32))
    s_dec = e(nc.sbuf_tensor("s_dec", [128, NTT * 16], F32))
    s_g_tm = e(nc.sbuf_tensor("s_g_tm", [128, NTT * 16], F32))
    s_lnb_tm = e(nc.sbuf_tensor("s_lnb_tm", [128, NTT * 16], F32))
    s_g_fm = e(nc.sbuf_tensor("s_g_fm", [16, SEG], F32))
    s_gnl_fm = e(nc.sbuf_tensor("s_gnl_fm", [16, SEG], F32))
    s_gneg_fm = e(nc.sbuf_tensor("s_gneg_fm", [16, SEG], F32))
    s_lnb_fm = e(nc.sbuf_tensor("s_lnb_fm", [16, SEG], F32))
    s_glast = e(nc.sbuf_tensor("s_glast", [1, max(NCH * 16, 16)], F32))

    def chunk_set(name, dt=F32, w=16):
        return [e(nc.sbuf_tensor(f"{name}{i}", [64, w], dt)) for i in range(NCH)]

    s_qp = [e(nc.sbuf_tensor(f"s_qp{t}", [128, D], BF16)) for t in range(NTT)]
    s_kp2 = [e(nc.sbuf_tensor(f"s_kp2{t}", [128, D], BF16)) for t in range(NTT)]
    s_vp = [e(nc.sbuf_tensor(f"s_vp{t}", [128, D], BF16)) for t in range(NTT)]
    s_eg = chunk_set("s_eg")
    s_bg = chunk_set("s_bg")
    s_el = chunk_set("s_el")
    s_egl = chunk_set("s_egl")

    def par(name, w=H * 64, dt=BF16):
        t = e(nc.sbuf_tensor(f"{name}0", [64, w], dt))
        return [t, t]

    def par2(name, w=H * 64, dt=BF16):
        return [e(nc.sbuf_tensor(f"{name}{p}", [64, w], dt)) for p in range(2)]

    s_vb = par("s_vb")
    s_bch = e(nc.sbuf_tensor("s_bch", [64, 16], F32))
    s_gch = e(nc.sbuf_tensor("s_gch", [64, 16], F32))
    s_qb = par("s_qb")
    s_kb = par("s_kb")
    s_q_fm = par2("s_q_fm")
    s_k_fm = par2("s_k_fm")
    s_u = par2("s_u")
    s_wT = par2("s_wT")
    s_L1 = par("s_L1_")
    s_L3 = par2("s_L3_")
    s_R = par("s_R", w=H * 128)
    s_M = par("s_M")
    s_MT = par("s_MT")
    s_Ah = par("s_Ah")
    s_X = par("s_X", w=H * 128)

    s_S = e(nc.sbuf_tensor("s_S", [64, H * 64], F32))
    s_Sb = e(nc.sbuf_tensor("s_Sb", [64, H * 64], BF16))
    s_aT = e(nc.sbuf_tensor("s_aT", [64, H * 64], BF16))
    s_wtm = e(nc.sbuf_tensor("s_wtm", [64, H * 64], BF16))
    s_vn = e(nc.sbuf_tensor("s_vn", [64, H * 64], BF16))
    s_kpp = [e(nc.sbuf_tensor(f"s_kpp{p}", [64, H * 64], BF16)) for p in range(2)]
    s_qe_tm = e(nc.sbuf_tensor("s_qe_tm", [64, H * 64], BF16))
    s_qe_fm = [e(nc.sbuf_tensor(f"s_qe_fm{p}", [64, H * 64], BF16)) for p in range(2)]

    s_o = [e(nc.sbuf_tensor(f"s_o{t}", [128, D], BF16)) for t in range(NTT)]
    s_gate = [e(nc.sbuf_tensor(f"s_gate{t}", [128, D], BF16)) for t in range(NTT)]
    s_og = e(nc.sbuf_tensor("s_og", [128, D], BF16))
    s_ofm = [e(nc.sbuf_tensor(f"s_ofm{p}", [128, 128], BF16)) for p in range(2)]
    s_sout = [e(nc.sbuf_tensor(f"s_sout{t}", [128, D], BF16)) for t in range(NTT)]
    s_logit = e(nc.sbuf_tensor("s_logit", [128, 8], F32))
    s_m12 = e(nc.sbuf_tensor("s_m12", [128, 8], F32))
    s_m13 = [e(nc.sbuf_tensor(f"s_m13_{j}", [128, 8], F32)) for j in range(4)]
    s_score = [e(nc.sbuf_tensor(f"s_score{t}", [128, 8], F32)) for t in range(NTT)]
    s_scT = [e(nc.sbuf_tensor(f"s_scT{t}", [8, 128], BF16)) for t in range(NTT)]
    s_acc = [e(nc.sbuf_tensor(f"s_acc{t}", [128, D], F32)) for t in range(NTT)]
    s_etmp = e(nc.sbuf_tensor("s_etmp", [128, D], BF16))
    s_gf = e(nc.sbuf_tensor("s_gf", [128, D], BF16))

    op, dma = ctx.op, ctx.dma

    for dst, src, name in (
            (s_ident_bf, c_ap["c_ident_bf"], "ident_bf"),
            (s_ident_f32, c_ap["c_ident_f32"], "ident_f32"),
            (s_ones1, c_ap["c_ones1"], "ones1"),
            (s_cumsum, c_ap["c_cumsum"], "cumsum"),
            (s_erep, c_ap["c_erep"], "erep"),
            (s_tril, c_ap["c_tril"], "tril"),
            (s_triu, c_ap["c_triu"], "triu"),
            (s_triu1, c_ap["c_triu1"], "triu1"),
            (s_irep, c_ap["c_irep_bf"], "irep"),
            (s_dtb, dtb_ap, "dtb"),
            (s_alog, alog_ap, "alog"),
            (s_bgate, bgate_ap, "bgate")):
        dma("sp_c", dst[:], src, writes=(name,))
    dma("sp_c", s_nega[0:64, :], c_ap["c_negI"], writes=("nega0",))
    dma("poolq", s_nwrep[:], nwrep_ap, writes=("nwrep",))
    dma("poolq", s_wab[:].rearrange("p (kc a) -> p kc a", a=32),
        wab_ap.rearrange("(kc p) a -> p kc a", p=128), writes=("wab",))
    dma("poolq", s_wgate[:].rearrange("p (kc a) -> p kc a", a=8),
        wgate_ap.rearrange("(kc p) a -> p kc a", p=128), writes=("wgate",))
    dma("poolq", s_bexp[:], bexp_ap, writes=("bexp",))

    op("act", lambda s: s.activation(s_alog[:], s_alog[:], AF.Exp),
       reads=("alog",), writes=("alog",))
    op("pe", lambda t: t.transpose(PS[4][0:1, 0:16], s_alog[:], s_ident_f32[0:16, 0:16]),
       reads=("alog", "ident_f32"), writes=("ps4",))
    op("dve", lambda v: v.tensor_copy(s_tmprow[:], PS[4][0:1, 0:16]),
       reads=("ps4",), writes=("tmprow",))
    op("pe", lambda t: t.matmul(PS[4][:, 0:16], s_ones1[0:1, :], s_tmprow[:],
                                start=True, stop=True),
       reads=("ones1", "tmprow"), writes=("ps4",))
    op("dve", lambda v: v.tensor_scalar(s_nAb[:], PS[4][:, 0:16], -1.0, None,
                                        ALU.mult),
       reads=("ps4",), writes=("nAb",))

    def slab_src(idx):
        if idx < 8:
            return wqkvg_ap[:, idx * 512:(idx + 1) * 512].rearrange(
                "(kc p) n -> p kc n", p=128)
        if idx < 24:
            ei, hh = divmod(idx - 8, 2)
            return wexp_ap[ei, :, hh * 512:(hh + 1) * 512].rearrange(
                "(kc p) n -> p kc n", p=128)
        h = idx - 24
        return wo_ap[:, h * 512:(h + 1) * 512].rearrange(
            "(kc p) n -> p kc n", p=128)

    scrw_ap = nc.dram_tensor("scr_w", [26, 128, 4096], BF16,
                             kind="Internal").ap()

    slab_issued = [0]

    def slab_name(gidx):
        return f"slab{gidx % NSL}"

    def issue_slabs(upto):
        while slab_issued[0] < upto:
            g = slab_issued[0]
            seg, idx = divmod(g, 26)
            if seg >= NSEG:
                break
            if seg == 0:
                dma(f"pq{g % NSL}", s_slab[g % NSL][:].rearrange(
                    "p (kc n) -> p kc n", n=512), slab_src(idx),
                    writes=(slab_name(g),))
                if NSEG > 1:
                    dma("pq4", scrw_ap[idx], s_slab[g % NSL][:],
                        reads=(slab_name(g),), writes=(f"scrw{idx}",))
            else:
                dma(f"sq{g % NSL}", s_slab[g % NSL][:], scrw_ap[idx],
                    reads=(f"scrw{idx}",), writes=(slab_name(g),))
            slab_issued[0] += 1

    def slab(seg, idx):
        g = seg * 26 + idx
        return s_slab[g % NSL], slab_name(g)

    issue_slabs(3)

    for seg in range(NSEG):
        t0 = seg * SEG
        for tt in range(NTT):
            xn = "x0"
            xt = s_x[0]
            mb = s_m13[tt % 4]
            mbn = f"m13_{tt % 4}"
            dma("sp_x", xt[:], x_ap[t0 + tt * 128:t0 + (tt + 1) * 128, :],
                writes=(xn,))
            op("dve", lambda v, xt=xt, mb=mb: v.reduce_sum(
                mb[:, 0:1], xt[:], axis=AX.X),
                reads=(xn,), writes=(mbn,))
            op("act", lambda s, xt=xt, mb=mb: s.activation(
                s_gf[:], xt[:], AF.Square, accum_out=mb[:, 2:3]),
                reads=(xn,), writes=("gf", mbn))
            op("dve", lambda v, mb=mb: v.tensor_scalar(
                mb[:, 1:2], mb[:, 0:1], 1.0 / D, None, ALU.mult),
                reads=(mbn,), writes=(mbn,))
            op("dve", lambda v, mb=mb: v.tensor_scalar(
                mb[:, 5:6], mb[:, 1:2], mb[:, 1:2], None, ALU.mult),
                reads=(mbn,), writes=(mbn,))
            op("dve", lambda v, mb=mb: v.tensor_scalar(
                mb[:, 3:4], mb[:, 2:3], 1.0 / D, 1e-5,
                ALU.mult, ALU.add), reads=(mbn,), writes=(mbn,))
            op("dve", lambda v, mb=mb: v.tensor_tensor(
                mb[:, 3:4], mb[:, 3:4], mb[:, 5:6], ALU.subtract),
                reads=(mbn,), writes=(mbn,))
            op("act", lambda s, mb=mb: s.activation(
                mb[:, 3:4], mb[:, 3:4], AF.Ln),
                reads=(mbn,), writes=(mbn,))
            op("dve", lambda v, mb=mb: v.tensor_scalar(
                mb[:, 4:5], mb[:, 3:4], -0.5, None, ALU.mult),
                reads=(mbn,), writes=(mbn,))
            op("act", lambda s, mb=mb: s.activation(
                mb[:, 4:5], mb[:, 4:5], AF.Exp),
                reads=(mbn,), writes=(mbn,))
            op("dve", lambda v, mb=mb: v.tensor_scalar(
                mb[:, 5:6], mb[:, 4:5], mb[:, 1:2], None, ALU.mult),
                reads=(mbn,), writes=(mbn,))
            op("dve", lambda v, xt=xt, mb=mb: v.tensor_scalar(
                s_xn_tm[:], xt[:], mb[:, 4:5], mb[:, 5:6],
                ALU.mult, ALU.subtract),
                reads=(xn, mbn), writes=("xn_tm",))
            for k in range(NKC):
                op("pe", lambda t, k=k: t.transpose(
                    PB[0][:, 0:128], s_xn_tm[:, k * 128:(k + 1) * 128],
                    s_ident_bf[:]),
                    reads=("xn_tm", "ident_bf"), writes=("pb0",))
                op("dve", lambda v, k=k, tt=tt: v.tensor_copy(
                    s_xn_fm[k][:, tt * 128:(tt + 1) * 128], PB[0][:, 0:128]),
                    reads=("pb0",), writes=(f"xnfm{k}",))
            for k in range(NKC):
                op("pe", lambda t, k=k, tt=tt: t.matmul(
                    PS[4][:, 0:32], s_xn_fm[k][:, tt * 128:(tt + 1) * 128],
                    s_wab[:, k * 32:(k + 1) * 32], start=(k == 0), stop=False),
                    reads=(f"xnfm{k}", "wab"), writes=("ps4",))
            op("pe", lambda t: t.matmul(
                PS[4][:, 0:32], s_ones1[0:1, :], s_dtb[0:1, :],
                start=False, stop=True),
                reads=("ones1", "dtb"), writes=("ps4",))
            op("dve", lambda v, tt=tt: v.tensor_copy(
                s_ab[:, tt * 32:(tt + 1) * 32], PS[4][:, 0:32]),
                reads=("ps4",), writes=("ab",))

        for tt in range(NTT):
            zb = s_ab[:, tt * 32:tt * 32 + 16]
            za = s_ab[:, tt * 32 + 16:tt * 32 + 32]
            bcol = s_beta[:, tt * 16:(tt + 1) * 16]
            spc = s_sp[:, tt * 16:(tt + 1) * 16]
            dcol = s_dec[:, tt * 16:(tt + 1) * 16]
            lcol = s_lnb_tm[:, tt * 16:(tt + 1) * 16]
            op("act", lambda s, zb=zb, bcol=bcol: s.activation(
                bcol, zb, AF.Sigmoid), reads=("ab",), writes=("beta",))
            op("act", lambda s, bcol=bcol, lcol=lcol: s.activation(
                lcol, bcol, AF.Ln), reads=("beta",), writes=("lnb_tm",))
            op("dve", lambda v, za=za, dcol=dcol: v.tensor_scalar(
                dcol, za, -1.0, None, ALU.mult), reads=("ab",), writes=("dec",))
            op("dve", lambda v, za=za, spc=spc, dcol=dcol: v.tensor_tensor(
                spc, za, dcol, ALU.min), reads=("ab", "dec"), writes=("spx",))
            op("act", lambda s, spc=spc: s.activation(
                spc, spc, AF.Exp), reads=("spx",), writes=("spx",))
            op("dve", lambda v, spc=spc: v.tensor_scalar(
                spc, spc, 1.0, None, ALU.add), reads=("spx",), writes=("spx",))
            op("act", lambda s, spc=spc: s.activation(
                spc, spc, AF.Ln), reads=("spx",), writes=("spx",))
            op("dve", lambda v, za=za, dcol=dcol: v.tensor_scalar(
                dcol, za, 0.0, None, ALU.max), reads=("ab",), writes=("dec",))
            op("dve", lambda v, spc=spc, dcol=dcol: v.tensor_tensor(
                dcol, dcol, spc, ALU.add), reads=("dec", "spx"), writes=("dec",))
            op("dve", lambda v, dcol=dcol: v.tensor_tensor(
                dcol, dcol, s_nAb[:, 0:16], ALU.mult),
                reads=("dec", "nAb"), writes=("dec",))
            op("dve", lambda v, tt=tt: v.tensor_copy(
                s_decb[:], s_dec[:, tt * 16:(tt + 1) * 16]),
                reads=("dec",), writes=("decb",))
            op("pe", lambda t: t.matmul(
                PS[4][:, 0:16], s_cumsum[:, 0:128],
                s_decb[:], start=True, stop=True),
                reads=("cumsum", "decb"), writes=("ps4",))
            op("dve", lambda v, tt=tt: v.tensor_copy(
                s_g_tm[:, tt * 16:(tt + 1) * 16], PS[4][:, 0:16]),
                reads=("ps4",), writes=("g_tm",))
            op("pe", lambda t, tt=tt: t.transpose(
                PS[4][0:16, 0:128], s_g_tm[:, tt * 16:(tt + 1) * 16],
                s_ident_f32[:]),
                reads=("g_tm", "ident_f32"), writes=("ps4",))
            op("dve", lambda v, tt=tt: v.tensor_copy(
                s_g_fm[:, tt * 128:(tt + 1) * 128], PS[4][0:16, 0:128]),
                reads=("ps4",), writes=("g_fm",))
            op("pe", lambda t, tt=tt: t.transpose(
                PS[4][0:16, 0:128], s_lnb_tm[:, tt * 16:(tt + 1) * 16],
                s_ident_f32[:]),
                reads=("lnb_tm", "ident_f32"), writes=("ps4",))
            op("dve", lambda v, tt=tt: v.tensor_copy(
                s_lnb_fm[:, tt * 128:(tt + 1) * 128], PS[4][0:16, 0:128]),
                reads=("ps4",), writes=("lnb_fm",))
        op("act", lambda s: s.activation(
            s_gneg_fm[:], s_g_fm[:], AF.Copy, scale=-1.0),
            reads=("g_fm",), writes=("gneg_fm",))
        op("pool", lambda g: g.tensor_add(s_gnl_fm[:], s_g_fm[:], s_lnb_fm[:]),
           reads=("g_fm", "lnb_fm"), writes=("gnl_fm",))
        dma("sp_f", scr_ap[0:16, :], s_gneg_fm[:], reads=("gneg_fm",),
            writes=("scr0",))
        dma("sp_f", scr_ap[16:32, :], s_g_fm[:], reads=("g_fm",),
            writes=("scr1",))
        dma("sp_f", scr_ap[32:48, :], s_gnl_fm[:], reads=("gnl_fm",),
            writes=("scr2",))

        dma("sp_f", s_glast[0:1, 0:NCH * 16].rearrange(
                "a (i h) -> a i h", h=16),
            scr_ap[16:32, 63::64].rearrange("h i -> () i h"),
            reads=("scr1",), writes=("glast",),
            allow_slow_non_contiguous=True)

        for i in range(NCH):
            tt, half = divmod(i, 2)
            grows = s_g_tm[half * 64:(half + 1) * 64, tt * 16:(tt + 1) * 16]
            brows = s_beta[half * 64:(half + 1) * 64, tt * 16:(tt + 1) * 16]
            op("dve", lambda v, grows=grows: v.tensor_copy(s_gch[:], grows),
               reads=("g_tm",), writes=("gch",))
            op("dve", lambda v, brows=brows: v.tensor_copy(s_bch[:], brows),
               reads=("beta",), writes=("bch",))
            op("act", lambda s, i=i: s.activation(
                s_eg[i][:], s_gch[:], AF.Exp), reads=("gch",), writes=(f"eg{i}",))
            op("pool", lambda g, i=i: g.tensor_mul(
                s_bg[i][:], s_eg[i][:], s_bch[:]),
                reads=(f"eg{i}", "bch"), writes=(f"bg{i}",))
            op("pe", lambda t, i=i: t.matmul(
                PS[4][0:64, 0:16], s_ones1[0:1, 0:64],
                s_glast[0:1, i * 16:(i + 1) * 16], start=True, stop=True),
                reads=("ones1", "glast"), writes=("ps4",))
            op("act", lambda s, i=i: s.activation(
                s_egl[i][:], PS[4][0:64, 0:16], AF.Exp),
                reads=("ps4",), writes=(f"egl{i}",))
            op("dve", lambda v, i=i: v.tensor_sub(
                s_el[i][:], PS[4][0:64, 0:16], s_gch[:]),
                reads=("ps4", "gch"), writes=(f"el{i}",))
            op("act", lambda s, i=i: s.activation(
                s_el[i][:], s_el[i][:], AF.Exp),
                reads=(f"el{i}",), writes=(f"el{i}",))

        for pp, pname in enumerate(("q", "k", "v", "gate")):
            issue_slabs(seg * 26 + 2 * pp + 3)
            for tt in range(NTT):
                pbase = 0 if tt % 2 == 0 else 5
                for half in range(2):
                    sl, sln = slab(seg, 2 * pp + half)
                    for k in range(NKC):
                        op("pe", lambda t, k=k, tt=tt, half=half, sl=sl,
                           pbase=pbase: t.matmul(
                            PS[pbase + half][:, 0:512],
                            s_xn_fm[k][:, tt * 128:(tt + 1) * 128],
                            sl[:, k * 512:(k + 1) * 512],
                            start=(k == 0), stop=(k == NKC - 1)),
                            reads=(f"xnfm{k}", sln), writes=(f"ps{pbase + half}",))
                if pp in (0, 1):
                    for half in range(2):
                        bi = 2 * (tt % 2) + half
                        mb = s_m13[bi]
                        mbn = f"m13_{bi}"
                        gsl = s_gf[:, half * 512:(half + 1) * 512]
                        op("act", lambda s, half=half, pbase=pbase, gsl=gsl:
                           s.activation(
                            gsl, PS[pbase + half][:, 0:512], AF.Square),
                            reads=(f"ps{pbase + half}",), writes=(f"gf{half}",))
                        op("dve", lambda v, mb=mb, gsl=gsl: v.reduce_sum(
                            mb[:, 0:8],
                            gsl.rearrange("p (h d) -> p h d", d=64), axis=AX.X),
                            reads=(f"gf{half}",), writes=(mbn,))
                        op("dve", lambda v, mb=mb: v.tensor_scalar(
                            mb[:, 0:8], mb[:, 0:8], 1e-6, None, ALU.add),
                            reads=(mbn,), writes=(mbn,))
                        op("act", lambda s, mb=mb: s.activation(
                            mb[:, 0:8], mb[:, 0:8], AF.Ln),
                            reads=(mbn,), writes=(mbn,))
                        op("dve", lambda v, pp=pp, mb=mb: v.tensor_scalar(
                            mb[:, 0:8], mb[:, 0:8], -0.5,
                            (LN8 if pp == 0 else 0.0), ALU.mult, ALU.add),
                            reads=(mbn,), writes=(mbn,))
                        op("act", lambda s, mb=mb: s.activation(
                            mb[:, 0:8], mb[:, 0:8], AF.Exp),
                            reads=(mbn,), writes=(mbn,))
                        dst = s_qp[tt] if pp == 0 else s_kp2[tt]
                        dn = f"{pname}p{tt}"
                        op("dve", lambda v, dst=dst, half=half, pbase=pbase,
                           mb=mb, mbn=mbn: v.tensor_tensor(
                            dst[:, half * 512:(half + 1) * 512].rearrange(
                                "p (h d) -> p h d", d=64),
                            PS[pbase + half][:, 0:512].rearrange(
                                "p (h d) -> p h d", d=64),
                            mb[:, 0:8].rearrange(
                                "p (h o) -> p h o", o=1).broadcast_to(
                                [128, 8, 64]), ALU.mult),
                            reads=(f"ps{pbase + half}", mbn), writes=(dn,))
                elif pp == 2:
                    for half in range(2):
                        op("dve", lambda v, tt=tt, half=half, pbase=pbase:
                           v.tensor_copy(
                            s_vp[tt][:, half * 512:(half + 1) * 512],
                            PS[pbase + half][:, 0:512]),
                            reads=(f"ps{pbase + half}",), writes=(f"vp{tt}",))
                else:
                    for half in range(2):
                        op("act", lambda s, tt=tt, half=half, pbase=pbase:
                           s.activation(
                            s_gate[tt][:, half * 512:(half + 1) * 512],
                            PS[pbase + half][:, 0:512], AF.Copy),
                            reads=(f"ps{pbase + half}",), writes=(f"gate{tt}",))

        if seg == 0:
            op("dve", lambda v: v.memset(s_S[:], 0.0), writes=("S",))
            op("pool", lambda g: g.memset(s_Sb[:], 0.0), writes=("Sb",))

        def emit_pre1(i, piece):
            p = i % 2
            tt, half2 = divmod(i, 2)
            rlo, rhi = half2 * 64, (half2 + 1) * 64
            if piece == 0:
                for r, arow in ((0, 64), (2, 65), (1, 66)):
                    dma("sp_f", s_nega[arow:arow + 1, :].rearrange(
                            "a (h c) -> a h c", c=64),
                        scr_ap[r * 16:(r + 1) * 16, i * 64:(i + 1) * 64],
                        reads=(f"scr{r}",), writes=("negaA",))
                op("dve", lambda v, tt=tt, rlo=rlo, rhi=rhi: v.tensor_copy(
                    s_qb[0][:], s_qp[tt][rlo:rhi, :]),
                    reads=(f"qp{tt}",), writes=("qb0",))
                op("act", lambda s, tt=tt, rlo=rlo, rhi=rhi: s.activation(
                    s_kb[0][:], s_kp2[tt][rlo:rhi, :], AF.Copy),
                    reads=(f"kp{tt}",), writes=("kb0",))
                op("act", lambda s, tt=tt, rlo=rlo, rhi=rhi: s.activation(
                    s_vb[0][:], s_vp[tt][rlo:rhi, :], AF.Copy),
                    reads=(f"vp{tt}",), writes=("vb0",))
                op("dve", lambda v, tt=tt, rlo=rlo, rhi=rhi: v.tensor_copy(
                    s_bch[:], s_beta[rlo:rhi, tt * 16:(tt + 1) * 16]),
                    reads=("beta",), writes=("bch",))
            elif piece in (1, 2):
                src2, srcn, dst, dn = (
                    (s_qb[0], "qb0", s_q_fm[p], f"q_fm{p}") if piece == 1 else
                    (s_kb[0], "kb0", s_k_fm[p], f"k_fm{p}"))
                for h in range(H):
                    op("pe", lambda t, src2=src2, h=h: t.transpose(
                        PB[0][0:64, h * 64:(h + 1) * 64],
                        src2[:, h * 64:(h + 1) * 64], s_ident_bf[0:64, 0:64]),
                        reads=(srcn, "ident_bf"), writes=("pb0",))
                op("dve", lambda v, dst=dst: v.tensor_copy(
                    dst[:], PB[0][0:64, :]), reads=("pb0",), writes=(dn,))
            elif piece in (3, 4):
                if piece == 3:
                    lset, lname, k16, k16n, tri, trin = (
                        s_L1[0], "L1_0", s_gnl_fm, "gnl_fm", s_tril, "tril")
                else:
                    lset, lname, k16, k16n, tri, trin = (
                        s_L3[p], f"L3_{p}", s_gneg_fm, "gneg_fm", s_triu1,
                        "triu1")
                for half in range(2):
                    op("pe", lambda t, i=i, half=half, k16=k16: t.matmul(
                        PS[2][0:64, 0:512], k16[0:16, i * 64:(i + 1) * 64],
                        s_erep[0:16, half * 512:(half + 1) * 512],
                        start=True, stop=False),
                        reads=(k16n, "erep"), writes=("ps2",))
                    op("pe", lambda t, half=half, tri=tri: t.matmul(
                        PS[2][0:64, 0:512], tri[:, 0:64],
                        s_nega[:, half * 512:(half + 1) * 512],
                        start=False, stop=True),
                        reads=(trin, "nega0", "negaA"), writes=("ps2",))
                    op("act", lambda s, half=half, lset=lset: s.activation(
                        lset[:, half * 512:(half + 1) * 512],
                        PS[2][0:64, 0:512], AF.Exp),
                        reads=("ps2",),
                        writes=((lname,) if lset is s_L1[0] else
                                (f"{lname}h{half}",)))

        def emit_pre2(i):
            p = i % 2
            op("dve", lambda v: v.tensor_tensor(
                s_R[0][:].rearrange("c (h w) -> c h w", w=128)[:, :, 0:64],
                s_vb[0][:].rearrange("c (h d) -> c h d", d=64),
                s_bch[:].rearrange("c (h o) -> c h o", o=1).broadcast_to(
                    [64, 16, 64]), ALU.mult),
                reads=("vb0", "bch"), writes=("R0",))
            op("dve", lambda v, i=i: v.tensor_tensor(
                s_R[0][:].rearrange("c (h w) -> c h w", w=128)[:, :, 64:128],
                s_kb[0][:].rearrange("c (h d) -> c h d", d=64),
                s_bg[i][:].rearrange("c (h o) -> c h o", o=1).broadcast_to(
                    [64, 16, 64]), ALU.mult),
                reads=("kb0", f"bg{i}"), writes=("R0",))
            for h in range(H):
                ps = PS[5 + h // 8]
                op("pe", lambda t, p=p, h=h, ps=ps: t.matmul(
                    ps[0:64, (h % 8) * 64:(h % 8) * 64 + 64],
                    s_k_fm[p][:, h * 64:(h + 1) * 64],
                    s_k_fm[p][:, h * 64:(h + 1) * 64],
                    start=True, stop=True),
                    reads=(f"k_fm{p}",), writes=(f"ps{5 + h // 8}",))
            for hh in range(2):
                op("dve", lambda v, hh=hh: v.tensor_tensor(
                    s_M[0][:, hh * 512:(hh + 1) * 512],
                    PS[5 + hh][0:64, 0:512],
                    s_L1[0][:, hh * 512:(hh + 1) * 512], ALU.mult),
                    reads=(f"ps{5 + hh}", "L1_0"),
                    writes=(f"M0q{2 * hh}", f"M0q{2 * hh + 1}"))
            for h in range(H):
                op("pe", lambda t, h=h: t.transpose(
                    PB[0][0:64, h * 64:(h + 1) * 64],
                    s_M[0][:, h * 64:(h + 1) * 64], s_ident_bf[0:64, 0:64]),
                    reads=(f"M0q{h // 4}", "ident_bf"), writes=("pb0",))
            op("act", lambda s: s.activation(
                s_MT[0][:], PB[0][0:64, :], AF.Copy),
                reads=("pb0",), writes=("MT0q0", "MT0q1", "MT0q2", "MT0q3"))
            for q in range(4):
                op("pool", lambda g, q=q: g.tensor_add(
                    s_Ah[0][:, q * 256:(q + 1) * 256],
                    s_MT[0][:, q * 256:(q + 1) * 256],
                    s_irep[:, q * 256:(q + 1) * 256]),
                    reads=(f"MT0q{q}", "irep"), writes=(f"Ah0q{q}",))
            op("dve", lambda v, i=i: v.tensor_tensor(
                s_wtm[:].rearrange("c (h d) -> c h d", d=64),
                s_qb[0][:].rearrange("c (h d) -> c h d", d=64),
                s_eg[i][:].rearrange("c (h o) -> c h o", o=1).broadcast_to(
                    [64, 16, 64]), ALU.mult),
                reads=("qb0", f"eg{i}"), writes=("wtm",))
            for h in range(H):
                op("pe", lambda t, h=h: t.transpose(
                    PB[0][0:64, h * 64:(h + 1) * 64],
                    s_wtm[:, h * 64:(h + 1) * 64], s_ident_bf[0:64, 0:64]),
                    reads=("wtm", "ident_bf"), writes=("pb0",))
            op("dve", lambda v, p=p: v.tensor_copy(s_qe_fm[p][:], PB[0][0:64, :]),
               reads=("pb0",), writes=(f"qe_fm{p}",))
            op("pool", lambda g, i=i, p=p: g.tensor_tensor(
                s_kpp[p][:].rearrange("c (h d) -> c h d", d=64),
                s_kb[0][:].rearrange("c (h d) -> c h d", d=64),
                s_el[i][:].rearrange("c (h o) -> c h o", o=1).broadcast_to(
                    [64, 16, 64]), ALU.mult),
                reads=("kb0", f"el{i}"), writes=(f"kpp{p}",))

        def emitA_iter(i, j):
            for grp in range(4):
                ps = PS[3 + grp % 2]
                pn = f"ps{3 + grp % 2}"
                xsrcn = ("R0" if j == 0 else f"X0g{grp}",)
                xsrc = s_R[0] if j == 0 else s_X[0]
                for hh in range(4):
                    h = grp * 4 + hh
                    op("pe", lambda t, h=h, hh=hh, xsrc=xsrc, ps=ps, grp=grp:
                       t.matmul(
                        ps[0:64, hh * 128:(hh + 1) * 128],
                        s_Ah[0][:, h * 64:(h + 1) * 64],
                        xsrc[:, h * 128:(h + 1) * 128],
                        start=True, stop=True),
                        reads=(f"Ah0q{grp}",) + xsrcn, writes=(pn,))
                if grp % 2 == 0:
                    op("dve", lambda v, grp=grp, ps=ps: v.tensor_copy(
                        s_X[0][:, grp * 512:(grp + 1) * 512], ps[0:64, 0:512]),
                        reads=(pn,), writes=(f"X0g{grp}",))
                else:
                    op("act", lambda s, grp=grp, ps=ps: s.activation(
                        s_X[0][:, grp * 512:(grp + 1) * 512], ps[0:64, 0:512],
                        AF.Copy), reads=(pn,), writes=(f"X0g{grp}",))
            if j == 5:
                return
            for h in range(H):
                ps = PS[5 + h // 8]
                op("pe", lambda t, h=h, ps=ps: t.matmul(
                    ps[0:64, (h % 8) * 64:(h % 8) * 64 + 64],
                    s_MT[0][:, h * 64:(h + 1) * 64],
                    s_M[0][:, h * 64:(h + 1) * 64],
                    start=True, stop=True),
                    reads=(f"MT0q{h // 4}", f"M0q{h // 4}"),
                    writes=(f"ps{5 + h // 8}",))
            for h in range(H):
                ps = PS[3 + h // 8]
                op("pe", lambda t, h=h, ps=ps: t.matmul(
                    ps[0:64, (h % 8) * 64:(h % 8) * 64 + 64],
                    s_M[0][:, h * 64:(h + 1) * 64],
                    s_MT[0][:, h * 64:(h + 1) * 64],
                    start=True, stop=True),
                    reads=(f"M0q{h // 4}", f"MT0q{h // 4}"),
                    writes=(f"ps{3 + h // 8}",))
            for q in range(4):
                op("act", lambda s, q=q: s.activation(
                    s_MT[0][:, q * 256:(q + 1) * 256],
                    PS[3 + q // 2][0:64, (q % 2) * 256:(q % 2) * 256 + 256],
                    AF.Copy),
                    reads=(f"ps{3 + q // 2}",), writes=(f"MT0q{q}",))
                op("pool", lambda g, q=q: g.tensor_add(
                    s_Ah[0][:, q * 256:(q + 1) * 256],
                    s_MT[0][:, q * 256:(q + 1) * 256],
                    s_irep[:, q * 256:(q + 1) * 256]),
                    reads=(f"MT0q{q}", "irep"), writes=(f"Ah0q{q}",))
                op("dve", lambda v, q=q: v.tensor_copy(
                    s_M[0][:, q * 256:(q + 1) * 256],
                    PS[5 + q // 2][0:64, (q % 2) * 256:(q % 2) * 256 + 256]),
                    reads=(f"ps{5 + q // 2}",), writes=(f"M0q{q}",))

        def emitA_post(i):
            p = i % 2
            for pi in range(2):
                op("dve", lambda v, p=p, pi=pi: v.tensor_copy(
                    s_u[p][:, pi * 512:(pi + 1) * 512].rearrange(
                        "c (h d) -> c h d", d=64),
                    s_X[0][:, pi * 1024:(pi + 1) * 1024].rearrange(
                        "c (h w) -> c h w", w=128)[:, :, 0:64]),
                    reads=(f"X0g{2 * pi}", f"X0g{2 * pi + 1}"),
                    writes=(f"u{p}h{pi}",))
            op("dve", lambda v: v.tensor_copy(
                s_wtm[:].rearrange("c (h d) -> c h d", d=64),
                s_X[0][:].rearrange("c (h w) -> c h w", w=128)[:, :, 64:128]),
                reads=("X0g0", "X0g1", "X0g2", "X0g3"), writes=("wtm",))
            for h in range(H):
                op("pe", lambda t, h=h: t.transpose(
                    PB[0][0:64, h * 64:(h + 1) * 64],
                    s_wtm[:, h * 64:(h + 1) * 64], s_ident_bf[0:64, 0:64]),
                    reads=("wtm", "ident_bf"), writes=("pb0",))
            op("dve", lambda v, p=p: v.tensor_copy(s_wT[p][:], PB[0][0:64, :]),
               reads=("pb0",), writes=(f"wT{p}",))

        def emitB(i, piece):
            p = i % 2
            tt, half2 = divmod(i, 2)
            if piece == 0:
                for h in range(H):
                    ps = PS[h // 8]
                    op("pe", lambda t, p=p, h=h, ps=ps: t.matmul(
                        ps[0:64, (h % 8) * 64:(h % 8) * 64 + 64],
                        s_k_fm[p][:, h * 64:(h + 1) * 64],
                        s_q_fm[p][:, h * 64:(h + 1) * 64],
                        start=True, stop=True),
                        reads=(f"k_fm{p}", f"q_fm{p}"), writes=(f"ps{h // 8}",))
                for hh in range(2):
                    op("dve", lambda v, p=p, hh=hh: v.tensor_tensor(
                        s_aT[:, hh * 512:(hh + 1) * 512],
                        PS[hh][0:64, 0:512],
                        s_L3[p][:, hh * 512:(hh + 1) * 512], ALU.mult),
                        reads=(f"ps{hh}", f"L3_{p}h{hh}"), writes=(f"aTh{hh}",))
            elif piece == 1:
                for h in range(H):
                    ps = PS[h // 8]
                    op("pe", lambda t, p=p, h=h, ps=ps: t.matmul(
                        ps[0:64, (h % 8) * 64:(h % 8) * 64 + 64],
                        s_wT[p][:, h * 64:(h + 1) * 64],
                        s_Sb[:, h * 64:(h + 1) * 64],
                        start=True, stop=True),
                        reads=(f"wT{p}", "Sb"), writes=(f"ps{h // 8}",))
                for pi in range(2):
                    op("dve", lambda v, p=p, pi=pi: v.tensor_sub(
                        s_vn[:, pi * 512:(pi + 1) * 512],
                        s_u[p][:, pi * 512:(pi + 1) * 512],
                        PS[pi][0:64, 0:512]),
                        reads=(f"u{p}h{pi}", f"ps{pi}"), writes=(f"vnh{pi}",))
            elif piece == 2:
                for h in range(H):
                    ps = PS[h // 8]
                    op("pe", lambda t, p=p, h=h, ps=ps: t.matmul(
                        ps[0:64, (h % 8) * 64:(h % 8) * 64 + 64],
                        s_qe_fm[p][:, h * 64:(h + 1) * 64],
                        s_Sb[:, h * 64:(h + 1) * 64],
                        start=True, stop=False),
                        reads=(f"qe_fm{p}", "Sb"), writes=(f"ps{h // 8}",))
                    op("pe", lambda t, h=h, ps=ps: t.matmul(
                        ps[0:64, (h % 8) * 64:(h % 8) * 64 + 64],
                        s_aT[:, h * 64:(h + 1) * 64],
                        s_vn[:, h * 64:(h + 1) * 64],
                        start=False, stop=True),
                        reads=(f"aTh{h // 8}", f"vnh{h // 8}"),
                        writes=(f"ps{h // 8}",))
                for pi in range(2):
                    op("act", lambda s, tt=tt, half2=half2, pi=pi: s.activation(
                        s_o[tt][half2 * 64:(half2 + 1) * 64,
                                pi * 512:(pi + 1) * 512],
                        PS[pi][0:64, 0:512], AF.Copy),
                        reads=(f"ps{pi}",), writes=(f"o{tt}",))
            elif piece == 3:
                for h in range(H):
                    ps = PS[h // 8]
                    op("pe", lambda t, p=p, h=h, ps=ps: t.matmul(
                        ps[0:64, (h % 8) * 64:(h % 8) * 64 + 64],
                        s_kpp[p][:, h * 64:(h + 1) * 64],
                        s_vn[:, h * 64:(h + 1) * 64],
                        start=True, stop=True),
                        reads=(f"kpp{p}", f"vnh{h // 8}"), writes=(f"ps{h // 8}",))
            else:
                op("pool", lambda g, i=i: g.tensor_tensor(
                    s_S[:].rearrange("c (h d) -> c h d", d=64),
                    s_S[:].rearrange("c (h d) -> c h d", d=64),
                    s_egl[i][:].rearrange("c (h o) -> c h o", o=1).broadcast_to(
                        [64, 16, 64]), ALU.mult),
                    reads=("S", f"egl{i}"), writes=("S",))
                for hh in range(2):
                    op("dve", lambda v, hh=hh: v.tensor_add(
                        s_S[:, hh * 512:(hh + 1) * 512],
                        s_S[:, hh * 512:(hh + 1) * 512],
                        PS[hh][0:64, 0:512]),
                        reads=("S", f"ps{hh}"), writes=("S",))
                op("dve", lambda v: v.tensor_copy(s_Sb[:], s_S[:]),
                   reads=("S",), writes=("Sb",))

        for pc in range(5):
            emit_pre1(0, pc)
        emit_pre2(0)
        for i in range(NCH):
            for j in range(6):
                emitA_iter(i, j)
                if j < 5 and i + 1 < NCH:
                    emit_pre1(i + 1, j)
                if j < 5 and i >= 1:
                    emitB(i - 1, j)
            emitA_post(i)
            if i + 1 < NCH:
                emit_pre2(i + 1)
        for j in range(5):
            emitB(NCH - 1, j)

        def emit_oepi(tt):
            op("act", lambda s, tt=tt: s.activation(
                s_gf[:], s_o[tt][:], AF.Square), reads=(f"o{tt}",), writes=("gf",))
            op("dve", lambda v: v.reduce_sum(
                s_stat[:, 0:8],
                s_gf[:, 0:512].rearrange("p (h d) -> p h d", d=64), axis=AX.X),
                reads=("gf",), writes=("stat",))
            op("dve", lambda v: v.reduce_sum(
                s_m12[:, 0:8],
                s_gf[:, 512:1024].rearrange("p (h d) -> p h d", d=64), axis=AX.X),
                reads=("gf",), writes=("m12",))
            for half in range(2):
                src = s_stat[:, 0:8] if half == 0 else s_m12[:, 0:8]
                sn = "stat" if half == 0 else "m12"
                op("dve", lambda v, src=src: v.tensor_scalar(
                    src, src, 1.0 / 64, 1e-5, ALU.mult, ALU.add),
                    reads=(sn,), writes=(sn,))
                op("act", lambda s, src=src: s.activation(
                    src, src, AF.Ln), reads=(sn,), writes=(sn,))
                op("dve", lambda v, src=src: v.tensor_scalar(
                    src, src, -0.5, None, ALU.mult), reads=(sn,), writes=(sn,))
                op("act", lambda s, src=src: s.activation(
                    src, src, AF.Exp), reads=(sn,), writes=(sn,))
            op("act", lambda s, tt=tt: s.activation(
                s_gf[:], s_gate[tt][:], AF.Silu), reads=(f"gate{tt}",),
                writes=("gf",))
            for half in range(2):
                src = s_stat[:, 0:8] if half == 0 else s_m12[:, 0:8]
                sn = "stat" if half == 0 else "m12"
                op("dve", lambda v, tt=tt, half=half, src=src: v.tensor_tensor(
                    s_sout[tt][:, half * 512:(half + 1) * 512].rearrange(
                        "p (h d) -> p h d", d=64),
                    s_o[tt][:, half * 512:(half + 1) * 512].rearrange(
                        "p (h d) -> p h d", d=64),
                    src.rearrange("p (h o) -> p h o", o=1).broadcast_to(
                        [128, 8, 64]), ALU.mult),
                    reads=(f"o{tt}", sn), writes=(f"sout{tt}",))
            op("dve", lambda v, tt=tt: v.tensor_mul(
                s_sout[tt][:], s_sout[tt][:], s_nwrep[:]),
                reads=(f"sout{tt}", "nwrep"), writes=(f"sout{tt}",))
            op("dve", lambda v, tt=tt: v.tensor_mul(
                s_sout[tt][:], s_sout[tt][:], s_gf[:]),
                reads=(f"sout{tt}", "gf"), writes=(f"sout{tt}",))

        def emit_wo(tt):
            pbase = 0 if tt % 2 == 0 else 5

            def tpose(k):
                pr = k % 2
                op("pe", lambda t, k=k, tt=tt: t.transpose(
                    PB[0][:, 0:128], s_sout[tt][:, k * 128:(k + 1) * 128],
                    s_ident_bf[:]), reads=(f"sout{tt}", "ident_bf"),
                    writes=("pb0",))
                op("dve", lambda v, pr=pr: v.tensor_copy(
                    s_ofm[pr][:], PB[0][:, 0:128]), reads=("pb0",),
                    writes=(f"ofm{pr}",))

            tpose(0)
            tpose(1)
            for k in range(NKC):
                pr = k % 2
                for half in range(2):
                    sl, sln = slab(seg, 24 + half)
                    op("pe", lambda t, k=k, pr=pr, half=half, sl=sl,
                       pbase=pbase: t.matmul(
                        PS[pbase + half][:, 0:512], s_ofm[pr][:],
                        sl[:, k * 512:(k + 1) * 512],
                        start=(k == 0), stop=(k == NKC - 1)),
                        reads=(f"ofm{pr}", sln), writes=(f"ps{pbase + half}",))
                if k + 2 < NKC:
                    tpose(k + 2)
            for half in range(2):
                op("act", lambda s, tt=tt, half=half, pbase=pbase: s.activation(
                    s_sout[tt][:, half * 512:(half + 1) * 512],
                    PS[pbase + half][:, 0:512], AF.Copy),
                    reads=(f"ps{pbase + half}",), writes=(f"sout{tt}",))

        def emit_combine(tt):
            op("act", lambda s, tt=tt: s.activation(
                s_gf[:], s_acc[tt][:], AF.Silu),
                reads=(f"acc{tt}",), writes=("gf",))
            op("pool", lambda g, tt=tt: g.tensor_mul(
                s_og[:], s_sout[tt][:], s_gf[:]),
                reads=(f"sout{tt}", "gf"), writes=("og",))
            dma("sp_c", s_acc[tt][:],
                x_ap[t0 + tt * 128:t0 + (tt + 1) * 128, :],
                writes=(f"acc{tt}",))
            op("dve", lambda v, tt=tt: v.tensor_add(
                s_xc[:], s_acc[tt][:], s_og[:]),
                reads=(f"acc{tt}", "og"), writes=("xc",))
            dma("actq", out_ap[t0 + tt * 128:t0 + (tt + 1) * 128, :],
                s_xc[:], reads=("xc",))

        issue_slabs(seg * 26 + 11)
        for tt in range(NTT):
            for k in range(NKC):
                op("pe", lambda t, k=k, tt=tt: t.matmul(
                    PS[4][:, 0:8], s_xn_fm[k][:, tt * 128:(tt + 1) * 128],
                    s_wgate[:, k * 8:(k + 1) * 8], start=(k == 0), stop=False),
                    reads=(f"xnfm{k}", "wgate"), writes=("ps4",))
            op("pe", lambda t: t.matmul(
                PS[4][:, 0:8], s_ones1[0:1, :], s_bgate[0:1, :],
                start=False, stop=True),
                reads=("ones1", "bgate"), writes=("ps4",))
            op("dve", lambda v: v.tensor_copy(s_logit[:], PS[4][:, 0:8]),
               reads=("ps4",), writes=("logit",))
            op("dve", lambda v: v.reduce_max(s_m12[:, 0:1], s_logit[:], axis=AX.X),
               reads=("logit",), writes=("m12",))
            op("dve", lambda v: v.tensor_scalar(
                s_m12[:, 2:3], s_m12[:, 0:1], -1.0, None, ALU.mult),
                reads=("m12",), writes=("m12",))
            op("dve", lambda v: v.tensor_scalar(
                s_stat[:, 0:8], s_logit[:], s_m12[:, 0:1], None, ALU.is_ge),
                reads=("logit", "m12"), writes=("stat",))
            op("dve", lambda v: v.scalar_tensor_tensor(
                s_stat[:, 0:8], s_stat[:, 0:8], NEG, s_logit[:],
                ALU.mult, ALU.add), reads=("stat", "logit"), writes=("stat",))
            op("dve", lambda v: v.reduce_max(s_m12[:, 1:2], s_stat[:, 0:8],
                                             axis=AX.X),
               reads=("stat",), writes=("m12",))
            op("act", lambda s, tt=tt: s.activation(
                s_score[tt][:], s_logit[:], AF.Exp, bias=s_m12[:, 2:3]),
                reads=("logit", "m12"), writes=(f"score{tt}",))
            op("dve", lambda v: v.tensor_scalar(
                s_stat[:, 0:8], s_logit[:], s_m12[:, 1:2], None, ALU.is_ge),
                reads=("logit", "m12"), writes=("stat",))
            op("dve", lambda v, tt=tt: v.tensor_tensor(
                s_score[tt][:], s_score[tt][:], s_stat[:, 0:8], ALU.mult),
                reads=(f"score{tt}", "stat"), writes=(f"score{tt}",))
            op("dve", lambda v, tt=tt: v.reduce_sum(
                s_m12[:, 3:4], s_score[tt][:], axis=AX.X),
                reads=(f"score{tt}",), writes=("m12",))
            op("dve", lambda v: v.reciprocal(s_m12[:, 4:5], s_m12[:, 3:4]),
               reads=("m12",), writes=("m12",))
            op("dve", lambda v, tt=tt: v.tensor_scalar(
                s_score[tt][:], s_score[tt][:], s_m12[:, 4:5], None, ALU.mult),
                reads=(f"score{tt}", "m12"), writes=(f"score{tt}",))
        for ei in range(8):
            issue_slabs(seg * 26 + 11 + 2 * ei)
            if ei == 0:
                for tt in range(NTT):
                    op("dve", lambda v, tt=tt: v.tensor_copy(
                        s_etmp[:, 0:8], s_score[tt][:]),
                        reads=(f"score{tt}",), writes=("etmp",))
                    op("pe", lambda t: t.transpose(
                        PB[0][0:8, 0:128], s_etmp[:, 0:8], s_ident_bf[:]),
                        reads=("etmp", "ident_bf"), writes=("pb0",))
                    op("dve", lambda v, tt=tt: v.tensor_copy(
                        s_scT[tt][:], PB[0][0:8, 0:128]), reads=("pb0",),
                        writes=(f"scT{tt}",))
            for half in range(2):
                if half == 1:
                    issue_slabs(seg * 26 + 12 + 2 * ei)
                sl, sln = slab(seg, 8 + 2 * ei + half)
                for tt in range(NTT):
                    pbase = 0 if tt % 2 == 0 else 5
                    for k in range(NKC):
                        op("pe", lambda t, k=k, tt=tt, half=half, sl=sl,
                           pbase=pbase: t.matmul(
                            PS[pbase + half][:, 0:512],
                            s_xn_fm[k][:, tt * 128:(tt + 1) * 128],
                            sl[:, k * 512:(k + 1) * 512],
                            start=(k == 0), stop=(k == NKC - 1)),
                            reads=(f"xnfm{k}", sln), writes=(f"ps{pbase + half}",))
                    if ei == 0:
                        op("pe", lambda t, tt=tt, half=half: t.matmul(
                            PS[2][:, 0:512], s_scT[tt][:],
                            s_bexp[:, half * 512:(half + 1) * 512],
                            start=True, stop=True),
                            reads=(f"scT{tt}", "bexp"), writes=("ps2",))
                        op("dve", lambda v, tt=tt, half=half: v.tensor_copy(
                            s_acc[tt][:, half * 512:(half + 1) * 512],
                            PS[2][:, 0:512]), reads=("ps2",),
                            writes=(f"acc{tt}",))
                    op("act", lambda s, tt=tt, half=half, ei=ei, pbase=pbase:
                       s.activation(
                        s_etmp[:, half * 512:(half + 1) * 512],
                        PS[pbase + half][:, 0:512], AF.Copy,
                        scale=s_score[tt][:, ei:ei + 1]),
                        reads=(f"ps{pbase + half}", f"score{tt}"), writes=("etmp",))
                    op("dve", lambda v, tt=tt, half=half: v.tensor_add(
                        s_acc[tt][:, half * 512:(half + 1) * 512],
                        s_acc[tt][:, half * 512:(half + 1) * 512],
                        s_etmp[:, half * 512:(half + 1) * 512]),
                        reads=(f"acc{tt}", "etmp"), writes=(f"acc{tt}",))
            if ei < NTT:
                emit_oepi(ei)

        issue_slabs(seg * 26 + 26)
        for tt in range(NTT):
            emit_wo(tt)
            if tt >= 1:
                emit_combine(tt - 1)
        emit_combine(NTT - 1)

    ctx.finish()
    block = e(nc.Block())
    ctx.finalize(block)
    stack.close()
    return nc


_NC_CACHE = {}


def kernel(L0_attn_x, **kw):
    inp = {"L0_attn_x": L0_attn_x, **kw}
    T = L0_attn_x.shape[1]
    if T not in _NC_CACHE:
        _NC_CACHE[T] = build_program(T)
    nc = _NC_CACHE[T]
    lay = layout_inputs(inp)
    consts = host_constants()
    base = {**lay, **consts}
    x = np.ascontiguousarray(np.asarray(L0_attn_x, np.float32))
    in_maps = []
    for b in range(x.shape[0]):
        m = dict(base)
        m["x"] = x[b]
        in_maps.append(m)
    res = run_bass_kernel_spmd(nc, in_maps, core_ids=list(range(x.shape[0])))
    return np.stack([r["out"] for r in res.results], axis=0).astype(np.float32)


# version 23 (final)
# speedup vs baseline: 1.0080x; 1.0080x over previous
import numpy as np
import sys

sys.path.insert(0, "/opt/trn_rl_repo")
import ml_dtypes
import concourse.bass as bass
import concourse.mybir as mybir
from concourse.bass_utils import run_bass_kernel_spmd
from contextlib import ExitStack

F32 = mybir.dt.float32
BF16 = mybir.dt.bfloat16
AF = mybir.ActivationFunctionType
ALU = mybir.AluOpType
AX = mybir.AxisListType

D = 1024
H = 16
NKC = 8
NEG = -1e9
LN8 = float(np.log(0.125))


class Ctx:

    DMAQ = {"sp_c": "sp", "sp_x": "sp", "sp_f": "sp", "actq": "act",
            "poolq": "pool", "pq0": "pool", "pq1": "pool", "pq2": "pool",
            "pq3": "pool", "pq4": "pool",
            "sq0": "sp", "sq1": "sp", "sq2": "sp"}

    def __init__(self, nc, stack):
        self.nc = nc
        e = stack.enter_context
        names = ("pe", "dve", "act", "pool", "sp") + tuple(self.DMAQ)
        self.sems = {n: e(nc.semaphore(f"sem_{n}")) for n in names}
        self.count = {n: 0 for n in names}
        self.waited = {n: {} for n in ("pe", "dve", "act", "pool", "sp")}
        self.queues = {n: [] for n in ("pe", "dve", "act", "pool", "sp")}
        self.bufs = {}

    def _buf(self, name):
        return self.bufs.setdefault(name, {"w": None, "r": {}})

    def _wait(self, xeng, sem_name, val):
        if xeng == "pe" and sem_name == "pe":
            return
        hw = self.waited[xeng].get(sem_name, 0)
        if val <= hw:
            return
        self.waited[xeng][sem_name] = val
        sem = self.sems[sem_name]
        self.queues[xeng].append(lambda eng, s=sem, v=val: eng.wait_ge(s, v))

    def _dma_adj(self, sname, v):
        if sname in self.DMAQ:
            return self.count[sname]
        return v

    def _hazards(self, xeng, reads, writes):
        for b in reads:
            rec = self._buf(b)
            if rec["w"]:
                sname, v = rec["w"]
                self._wait(xeng, sname, self._dma_adj(sname, v))
        for b in writes:
            rec = self._buf(b)
            if rec["w"]:
                sname, v = rec["w"]
                self._wait(xeng, sname, self._dma_adj(sname, v))
            for sname, v in rec["r"].items():
                self._wait(xeng, sname, self._dma_adj(sname, v))

    def _record(self, key, reads, writes, val):
        for b in reads:
            self._buf(b)["r"][key] = max(self._buf(b)["r"].get(key, 0), val)
        for b in writes:
            rec = self._buf(b)
            rec["w"] = (key, val)
            rec["r"] = {}

    def op(self, eng, fn, reads=(), writes=()):
        self._hazards(eng, reads, writes)
        sem = self.sems[eng]
        self.count[eng] += 1
        val = self.count[eng]

        def emit(engine, fn=fn, sem=sem):
            fn(engine).then_inc(sem, 1)
        self.queues[eng].append(emit)
        self._record(eng, reads, writes, val)
        return val

    def dma(self, q, out_ap, in_ap, reads=(), writes=(), **kw):
        xeng = self.DMAQ[q]
        self._hazards(xeng, reads, writes)
        sem = self.sems[q]
        self.count[q] += 16
        val = self.count[q]

        def emit(engine, out_ap=out_ap, in_ap=in_ap, sem=sem, kw=kw):
            engine.dma_start(out_ap, in_ap, **kw).then_inc(sem, 16)
        self.queues[xeng].append(emit)
        self._record(q, reads, writes, val)
        return val

    def finish(self):
        for q in self.DMAQ:
            sem, v = self.sems[q], self.count[q]
            if v:
                self.queues["sp"].append(
                    lambda eng, s=sem, v=v: eng.wait_ge(s, v))

    def finalize(self, block):
        qs = self.queues

        @block.tensor
        def _(eng):
            for f in qs["pe"]:
                f(eng)

        @block.vector
        def _(eng):
            for f in qs["dve"]:
                f(eng)

        @block.scalar
        def _(eng):
            for f in qs["act"]:
                f(eng)

        @block.gpsimd
        def _(eng):
            for f in qs["pool"]:
                f(eng)

        @block.sync
        def _(eng):
            for f in qs["sp"]:
                f(eng)


def host_constants():
    c = {}
    c["c_ident_bf"] = np.eye(128, dtype=ml_dtypes.bfloat16)
    c["c_ident_f32"] = np.eye(128, dtype=np.float32)
    c["c_ones1"] = np.ones((1, 128), dtype=np.float32)
    tri = np.triu(np.ones((64, 64), np.float32))
    blk = np.zeros((128, 128), np.float32)
    blk[:64, :64] = tri
    blk[64:, 64:] = tri
    c["c_cumsum"] = blk.astype(ml_dtypes.bfloat16)
    er = np.zeros((16, 1024), np.float32)
    for h in range(16):
        er[h, h * 64:(h + 1) * 64] = 1.0
    c["c_erep"] = er
    c["c_erep_bf"] = er.astype(ml_dtypes.bfloat16)
    def aug(mat, row):
        m = np.zeros((67, 64), np.float32)
        m[0:64] = mat
        m[row] = 1.0
        return m
    c["c_tril"] = aug(np.tril(np.ones((64, 64), np.float32)), 64)
    c["c_triu"] = aug(np.triu(np.ones((64, 64), np.float32)), 65)
    c["c_triu1"] = aug(np.triu(np.ones((64, 64), np.float32), 1), 66)
    negI = np.zeros((64, 1024), np.float32)
    irep = np.zeros((64, 1024), np.float32)
    for h in range(16):
        negI[:, h * 64:(h + 1) * 64] = NEG * np.eye(64, dtype=np.float32)
        irep[:, h * 64:(h + 1) * 64] = np.eye(64)
    c["c_negI"] = negI
    c["c_irep_bf"] = irep.astype(ml_dtypes.bfloat16)
    return c


def layout_inputs(inp):
    d = {}
    d["w_qkvg"] = np.ascontiguousarray(np.concatenate(
        [inp["L0_attn_Wq"], inp["L0_attn_Wk"], inp["L0_attn_Wv"],
         inp["L0_attn_Wg"]], axis=1), dtype=np.float32)
    d["w_o"] = np.ascontiguousarray(inp["L0_attn_Wo"], dtype=np.float32)
    d["w_ab"] = np.ascontiguousarray(np.concatenate(
        [inp["L0_attn_Wb"], inp["L0_attn_Wa"]], axis=1), dtype=np.float32)
    d["dtb_row"] = np.concatenate(
        [np.zeros(16, np.float32),
         np.asarray(inp["L0_attn_dt_bias"], np.float32)]).reshape(1, 32)
    d["a_log"] = np.asarray(inp["L0_attn_A_log"], np.float32).reshape(16, 1)
    d["nw_rep"] = np.ascontiguousarray(np.broadcast_to(
        np.tile(np.asarray(inp["L0_attn_norm_w"], np.float32), 16),
        (128, 1024)))
    d["w_gate"] = np.ascontiguousarray(
        np.asarray(inp["L0_ffn_gate_w"], np.float32).T)
    d["b_gate"] = np.asarray(inp["L0_ffn_gate_b"], np.float32).reshape(1, 8)
    d["w_exp"] = np.ascontiguousarray(inp["L0_ffn_expert_w"], dtype=np.float32)
    d["b_exp"] = np.ascontiguousarray(inp["L0_ffn_expert_b"], dtype=np.float32)
    return d


def build_program(T):
    SEG = min(512, T)
    NSEG = T // SEG
    NTT = SEG // 128
    NCH = SEG // 64
    nc = bass.Bass("TRN2", target_bir_lowering=False)

    x_ap = nc.dram_tensor("x", [T, D], F32, kind="ExternalInput").ap()
    out_ap = nc.dram_tensor("out", [T, D], F32, kind="ExternalOutput").ap()
    wqkvg_ap = nc.dram_tensor("w_qkvg", [D, 4096], F32, kind="ExternalInput").ap()
    wo_ap = nc.dram_tensor("w_o", [D, D], F32, kind="ExternalInput").ap()
    wab_ap = nc.dram_tensor("w_ab", [D, 32], F32, kind="ExternalInput").ap()
    dtb_ap = nc.dram_tensor("dtb_row", [1, 32], F32, kind="ExternalInput").ap()
    alog_ap = nc.dram_tensor("a_log", [16, 1], F32, kind="ExternalInput").ap()
    nwrep_ap = nc.dram_tensor("nw_rep", [128, D], F32, kind="ExternalInput").ap()
    wgate_ap = nc.dram_tensor("w_gate", [D, 8], F32, kind="ExternalInput").ap()
    bgate_ap = nc.dram_tensor("b_gate", [1, 8], F32, kind="ExternalInput").ap()
    wexp_ap = nc.dram_tensor("w_exp", [8, D, D], F32, kind="ExternalInput").ap()
    bexp_ap = nc.dram_tensor("b_exp", [8, D], F32, kind="ExternalInput").ap()
    consts = host_constants()
    c_ap = {k: nc.dram_tensor(k, list(v.shape),
                              BF16 if v.dtype == ml_dtypes.bfloat16 else F32,
                              kind="ExternalInput").ap()
            for k, v in consts.items()}
    scr_ap = nc.dram_tensor("scr", [64, SEG], F32, kind="Internal").ap()

    stack = ExitStack()
    e = stack.enter_context
    ctx = Ctx(nc, stack)

    PS = [e(nc.psum_tensor(f"ps{i}", [128, 512], F32)) for i in range(7)]
    PB = [e(nc.psum_tensor(f"pb{i}", [128, 1024], BF16)) for i in range(1)]

    s_ident_bf = e(nc.sbuf_tensor("s_ident_bf", [128, 128], BF16))
    s_ident_f32 = e(nc.sbuf_tensor("s_ident_f32", [128, 128], F32))
    s_ones1 = e(nc.sbuf_tensor("s_ones1", [1, 128], F32))
    s_cumsum = e(nc.sbuf_tensor("s_cumsum", [128, 128], BF16))
    s_decb = e(nc.sbuf_tensor("s_decb", [128, 16], BF16))
    s_erep = e(nc.sbuf_tensor("s_erep", [16, 1024], F32))
    s_tril = e(nc.sbuf_tensor("s_tril", [67, 64], F32))
    s_triu = e(nc.sbuf_tensor("s_triu", [67, 64], F32))
    s_triu1 = e(nc.sbuf_tensor("s_triu1", [67, 64], F32))
    s_nega = e(nc.sbuf_tensor("s_nega", [67, 1024], F32))
    s_irep = e(nc.sbuf_tensor("s_irep", [64, 1024], BF16))
    s_nwrep = e(nc.sbuf_tensor("s_nwrep", [128, D], BF16))
    s_dtb = e(nc.sbuf_tensor("s_dtb", [1, 32], F32))
    s_alog = e(nc.sbuf_tensor("s_alog", [16, 1], F32))
    s_nAb = e(nc.sbuf_tensor("s_nAb", [128, 16], F32))
    s_wab = e(nc.sbuf_tensor("s_wab", [128, 256], BF16))
    s_wgate = e(nc.sbuf_tensor("s_wgate", [128, 64], BF16))
    s_bgate = e(nc.sbuf_tensor("s_bgate", [1, 8], F32))
    s_bexp = e(nc.sbuf_tensor("s_bexp", [8, D], BF16))
    s_tmprow = e(nc.sbuf_tensor("s_tmprow", [1, 16], F32))

    NSL = 3
    s_slab = [e(nc.sbuf_tensor(f"s_slab{j}", [128, 4096], BF16)) for j in range(NSL)]

    s_x = [e(nc.sbuf_tensor("s_x0", [128, D], F32))] * 2
    s_xc = e(nc.sbuf_tensor("s_xc", [128, D], F32))
    s_stat = e(nc.sbuf_tensor("s_stat", [128, 8], F32))
    s_xn_tm = e(nc.sbuf_tensor("s_xn_tm", [128, D], BF16))
    s_xn_fm = [e(nc.sbuf_tensor(f"s_xn_fm{k}", [128, SEG], BF16)) for k in range(NKC)]

    s_ab = e(nc.sbuf_tensor("s_ab", [128, NTT * 32], F32))
    s_beta = e(nc.sbuf_tensor("s_beta", [128, NTT * 16], F32))
    s_sp = e(nc.sbuf_tensor("s_sp", [128, NTT * 16], F32))
    s_dec = e(nc.sbuf_tensor("s_dec", [128, NTT * 16], F32))
    s_g_tm = e(nc.sbuf_tensor("s_g_tm", [128, NTT * 16], F32))
    s_lnb_tm = e(nc.sbuf_tensor("s_lnb_tm", [128, NTT * 16], F32))
    s_g_fm = e(nc.sbuf_tensor("s_g_fm", [16, SEG], F32))
    s_gnl_fm = e(nc.sbuf_tensor("s_gnl_fm", [16, SEG], F32))
    s_gneg_fm = e(nc.sbuf_tensor("s_gneg_fm", [16, SEG], F32))
    s_lnb_fm = e(nc.sbuf_tensor("s_lnb_fm", [16, SEG], F32))
    s_glast = e(nc.sbuf_tensor("s_glast", [1, max(NCH * 16, 16)], F32))

    def chunk_set(name, dt=F32, w=16):
        return [e(nc.sbuf_tensor(f"{name}{i}", [64, w], dt)) for i in range(NCH)]

    s_qp = [e(nc.sbuf_tensor(f"s_qp{t}", [128, D], BF16)) for t in range(NTT)]
    s_kp2 = [e(nc.sbuf_tensor(f"s_kp2{t}", [128, D], BF16)) for t in range(NTT)]
    s_vp = [e(nc.sbuf_tensor(f"s_vp{t}", [128, D], BF16)) for t in range(NTT)]
    s_eg = chunk_set("s_eg")
    s_bg = chunk_set("s_bg")
    s_el = chunk_set("s_el")
    s_egl = chunk_set("s_egl")

    def par(name, w=H * 64, dt=BF16):
        t = e(nc.sbuf_tensor(f"{name}0", [64, w], dt))
        return [t, t]

    def par2(name, w=H * 64, dt=BF16):
        return [e(nc.sbuf_tensor(f"{name}{p}", [64, w], dt)) for p in range(2)]

    s_vb = par("s_vb")
    s_bch = e(nc.sbuf_tensor("s_bch", [64, 16], F32))
    s_gch = e(nc.sbuf_tensor("s_gch", [64, 16], F32))
    s_qb = par("s_qb")
    s_kb = par("s_kb")
    s_q_fm = par2("s_q_fm")
    s_k_fm = par2("s_k_fm")
    s_u = par2("s_u")
    s_wT = par2("s_wT")
    s_L1 = par("s_L1_")
    s_L3 = par2("s_L3_")
    s_R = par("s_R", w=H * 128)
    s_M = par("s_M")
    s_MT = par("s_MT")
    s_Ah = par("s_Ah")
    s_X = par("s_X", w=H * 128)

    s_S = e(nc.sbuf_tensor("s_S", [64, H * 64], F32))
    s_Sb = e(nc.sbuf_tensor("s_Sb", [64, H * 64], BF16))
    s_aT = e(nc.sbuf_tensor("s_aT", [64, H * 64], BF16))
    s_wtm = e(nc.sbuf_tensor("s_wtm", [64, H * 64], BF16))
    s_vn = e(nc.sbuf_tensor("s_vn", [64, H * 64], BF16))
    s_kpp = [e(nc.sbuf_tensor(f"s_kpp{p}", [64, H * 64], BF16)) for p in range(2)]
    s_qe_tm = e(nc.sbuf_tensor("s_qe_tm", [64, H * 64], BF16))
    s_qe_fm = [e(nc.sbuf_tensor(f"s_qe_fm{p}", [64, H * 64], BF16)) for p in range(2)]

    s_o = [e(nc.sbuf_tensor(f"s_o{t}", [128, D], BF16)) for t in range(NTT)]
    s_gate = [e(nc.sbuf_tensor(f"s_gate{t}", [128, D], BF16)) for t in range(NTT)]
    s_og = e(nc.sbuf_tensor("s_og", [128, D], BF16))
    s_ofm = [e(nc.sbuf_tensor(f"s_ofm{p}", [128, 128], BF16)) for p in range(2)]
    s_sout = [e(nc.sbuf_tensor(f"s_sout{t}", [128, D], BF16)) for t in range(NTT)]
    s_logit = e(nc.sbuf_tensor("s_logit", [128, 8], F32))
    s_m12 = e(nc.sbuf_tensor("s_m12", [128, 8], F32))
    s_m13 = [e(nc.sbuf_tensor(f"s_m13_{j}", [128, 8], F32)) for j in range(4)]
    s_score = [e(nc.sbuf_tensor(f"s_score{t}", [128, 8], F32)) for t in range(NTT)]
    s_scT = [e(nc.sbuf_tensor(f"s_scT{t}", [8, 128], BF16)) for t in range(NTT)]
    s_acc = [e(nc.sbuf_tensor(f"s_acc{t}", [128, D], F32)) for t in range(NTT)]
    s_etmp = e(nc.sbuf_tensor("s_etmp", [128, D], BF16))
    s_gf = e(nc.sbuf_tensor("s_gf", [128, D], BF16))

    op, dma = ctx.op, ctx.dma

    for dst, src, name in (
            (s_ident_bf, c_ap["c_ident_bf"], "ident_bf"),
            (s_ident_f32, c_ap["c_ident_f32"], "ident_f32"),
            (s_ones1, c_ap["c_ones1"], "ones1"),
            (s_cumsum, c_ap["c_cumsum"], "cumsum"),
            (s_erep, c_ap["c_erep"], "erep"),
            (s_tril, c_ap["c_tril"], "tril"),
            (s_triu, c_ap["c_triu"], "triu"),
            (s_triu1, c_ap["c_triu1"], "triu1"),
            (s_irep, c_ap["c_irep_bf"], "irep"),
            (s_dtb, dtb_ap, "dtb"),
            (s_alog, alog_ap, "alog"),
            (s_bgate, bgate_ap, "bgate")):
        dma("sp_c", dst[:], src, writes=(name,))
    dma("sp_c", s_nega[0:64, :], c_ap["c_negI"], writes=("nega0",))
    dma("poolq", s_nwrep[:], nwrep_ap, writes=("nwrep",))
    dma("poolq", s_wab[:].rearrange("p (kc a) -> p kc a", a=32),
        wab_ap.rearrange("(kc p) a -> p kc a", p=128), writes=("wab",))
    dma("poolq", s_wgate[:].rearrange("p (kc a) -> p kc a", a=8),
        wgate_ap.rearrange("(kc p) a -> p kc a", p=128), writes=("wgate",))
    dma("poolq", s_bexp[:], bexp_ap, writes=("bexp",))

    op("act", lambda s: s.activation(s_alog[:], s_alog[:], AF.Exp),
       reads=("alog",), writes=("alog",))
    op("pe", lambda t: t.transpose(PS[4][0:1, 0:16], s_alog[:], s_ident_f32[0:16, 0:16]),
       reads=("alog", "ident_f32"), writes=("ps4",))
    op("dve", lambda v: v.tensor_copy(s_tmprow[:], PS[4][0:1, 0:16]),
       reads=("ps4",), writes=("tmprow",))
    op("pe", lambda t: t.matmul(PS[4][:, 0:16], s_ones1[0:1, :], s_tmprow[:],
                                start=True, stop=True),
       reads=("ones1", "tmprow"), writes=("ps4",))
    op("dve", lambda v: v.tensor_scalar(s_nAb[:], PS[4][:, 0:16], -1.0, None,
                                        ALU.mult),
       reads=("ps4",), writes=("nAb",))

    def slab_src(idx):
        if idx < 8:
            return wqkvg_ap[:, idx * 512:(idx + 1) * 512].rearrange(
                "(kc p) n -> p kc n", p=128)
        if idx < 24:
            ei, hh = divmod(idx - 8, 2)
            return wexp_ap[ei, :, hh * 512:(hh + 1) * 512].rearrange(
                "(kc p) n -> p kc n", p=128)
        h = idx - 24
        return wo_ap[:, h * 512:(h + 1) * 512].rearrange(
            "(kc p) n -> p kc n", p=128)

    scrw_ap = nc.dram_tensor("scr_w", [26, 128, 4096], BF16,
                             kind="Internal").ap()

    slab_issued = [0]

    def slab_name(gidx):
        return f"slab{gidx % NSL}"

    def issue_slabs(upto):
        while slab_issued[0] < upto:
            g = slab_issued[0]
            seg, idx = divmod(g, 26)
            if seg >= NSEG:
                break
            if seg == 0:
                dma(f"pq{g % NSL}", s_slab[g % NSL][:].rearrange(
                    "p (kc n) -> p kc n", n=512), slab_src(idx),
                    writes=(slab_name(g),))
                if NSEG > 1:
                    dma("pq4", scrw_ap[idx], s_slab[g % NSL][:],
                        reads=(slab_name(g),), writes=(f"scrw{idx}",))
            else:
                dma(f"sq{g % NSL}", s_slab[g % NSL][:], scrw_ap[idx],
                    reads=(f"scrw{idx}",), writes=(slab_name(g),))
            slab_issued[0] += 1

    def slab(seg, idx):
        g = seg * 26 + idx
        return s_slab[g % NSL], slab_name(g)

    issue_slabs(3)

    for seg in range(NSEG):
        t0 = seg * SEG
        for tt in range(NTT):
            xn = "x0"
            xt = s_x[0]
            mb = s_m13[tt % 4]
            mbn = f"m13_{tt % 4}"
            dma("sp_x", xt[:], x_ap[t0 + tt * 128:t0 + (tt + 1) * 128, :],
                writes=(xn,))
            op("dve", lambda v, xt=xt, mb=mb: v.reduce_sum(
                mb[:, 0:1], xt[:], axis=AX.X),
                reads=(xn,), writes=(mbn,))
            op("act", lambda s, xt=xt, mb=mb: s.activation(
                s_gf[:], xt[:], AF.Square, accum_out=mb[:, 2:3]),
                reads=(xn,), writes=("gf", mbn))
            op("dve", lambda v, mb=mb: v.tensor_scalar(
                mb[:, 1:2], mb[:, 0:1], 1.0 / D, None, ALU.mult),
                reads=(mbn,), writes=(mbn,))
            op("dve", lambda v, mb=mb: v.tensor_scalar(
                mb[:, 5:6], mb[:, 1:2], mb[:, 1:2], None, ALU.mult),
                reads=(mbn,), writes=(mbn,))
            op("dve", lambda v, mb=mb: v.tensor_scalar(
                mb[:, 3:4], mb[:, 2:3], 1.0 / D, 1e-5,
                ALU.mult, ALU.add), reads=(mbn,), writes=(mbn,))
            op("dve", lambda v, mb=mb: v.tensor_tensor(
                mb[:, 3:4], mb[:, 3:4], mb[:, 5:6], ALU.subtract),
                reads=(mbn,), writes=(mbn,))
            op("act", lambda s, mb=mb: s.activation(
                mb[:, 3:4], mb[:, 3:4], AF.Ln),
                reads=(mbn,), writes=(mbn,))
            op("dve", lambda v, mb=mb: v.tensor_scalar(
                mb[:, 4:5], mb[:, 3:4], -0.5, None, ALU.mult),
                reads=(mbn,), writes=(mbn,))
            op("act", lambda s, mb=mb: s.activation(
                mb[:, 4:5], mb[:, 4:5], AF.Exp),
                reads=(mbn,), writes=(mbn,))
            op("dve", lambda v, mb=mb: v.tensor_scalar(
                mb[:, 5:6], mb[:, 4:5], mb[:, 1:2], None, ALU.mult),
                reads=(mbn,), writes=(mbn,))
            op("dve", lambda v, xt=xt, mb=mb: v.tensor_scalar(
                s_xn_tm[:], xt[:], mb[:, 4:5], mb[:, 5:6],
                ALU.mult, ALU.subtract),
                reads=(xn, mbn), writes=("xn_tm",))
            for k in range(NKC):
                op("pe", lambda t, k=k: t.transpose(
                    PB[0][:, 0:128], s_xn_tm[:, k * 128:(k + 1) * 128],
                    s_ident_bf[:]),
                    reads=("xn_tm", "ident_bf"), writes=("pb0",))
                op("dve", lambda v, k=k, tt=tt: v.tensor_copy(
                    s_xn_fm[k][:, tt * 128:(tt + 1) * 128], PB[0][:, 0:128]),
                    reads=("pb0",), writes=(f"xnfm{k}",))
            for k in range(NKC):
                op("pe", lambda t, k=k, tt=tt: t.matmul(
                    PS[4][:, 0:32], s_xn_fm[k][:, tt * 128:(tt + 1) * 128],
                    s_wab[:, k * 32:(k + 1) * 32], start=(k == 0), stop=False),
                    reads=(f"xnfm{k}", "wab"), writes=("ps4",))
            op("pe", lambda t: t.matmul(
                PS[4][:, 0:32], s_ones1[0:1, :], s_dtb[0:1, :],
                start=False, stop=True),
                reads=("ones1", "dtb"), writes=("ps4",))
            op("dve", lambda v, tt=tt: v.tensor_copy(
                s_ab[:, tt * 32:(tt + 1) * 32], PS[4][:, 0:32]),
                reads=("ps4",), writes=("ab",))

        for tt in range(NTT):
            zb = s_ab[:, tt * 32:tt * 32 + 16]
            za = s_ab[:, tt * 32 + 16:tt * 32 + 32]
            bcol = s_beta[:, tt * 16:(tt + 1) * 16]
            spc = s_sp[:, tt * 16:(tt + 1) * 16]
            dcol = s_dec[:, tt * 16:(tt + 1) * 16]
            lcol = s_lnb_tm[:, tt * 16:(tt + 1) * 16]
            op("act", lambda s, zb=zb, bcol=bcol: s.activation(
                bcol, zb, AF.Sigmoid), reads=("ab",), writes=("beta",))
            op("act", lambda s, bcol=bcol, lcol=lcol: s.activation(
                lcol, bcol, AF.Ln), reads=("beta",), writes=("lnb_tm",))
            op("dve", lambda v, za=za, dcol=dcol: v.tensor_scalar(
                dcol, za, -1.0, None, ALU.mult), reads=("ab",), writes=("dec",))
            op("dve", lambda v, za=za, spc=spc, dcol=dcol: v.tensor_tensor(
                spc, za, dcol, ALU.min), reads=("ab", "dec"), writes=("spx",))
            op("act", lambda s, spc=spc: s.activation(
                spc, spc, AF.Exp), reads=("spx",), writes=("spx",))
            op("dve", lambda v, spc=spc: v.tensor_scalar(
                spc, spc, 1.0, None, ALU.add), reads=("spx",), writes=("spx",))
            op("act", lambda s, spc=spc: s.activation(
                spc, spc, AF.Ln), reads=("spx",), writes=("spx",))
            op("dve", lambda v, za=za, dcol=dcol: v.tensor_scalar(
                dcol, za, 0.0, None, ALU.max), reads=("ab",), writes=("dec",))
            op("dve", lambda v, spc=spc, dcol=dcol: v.tensor_tensor(
                dcol, dcol, spc, ALU.add), reads=("dec", "spx"), writes=("dec",))
            op("dve", lambda v, dcol=dcol: v.tensor_tensor(
                dcol, dcol, s_nAb[:, 0:16], ALU.mult),
                reads=("dec", "nAb"), writes=("dec",))
            op("dve", lambda v, tt=tt: v.tensor_copy(
                s_decb[:], s_dec[:, tt * 16:(tt + 1) * 16]),
                reads=("dec",), writes=("decb",))
            op("pe", lambda t: t.matmul(
                PS[4][:, 0:16], s_cumsum[:, 0:128],
                s_decb[:], start=True, stop=True),
                reads=("cumsum", "decb"), writes=("ps4",))
            op("dve", lambda v, tt=tt: v.tensor_copy(
                s_g_tm[:, tt * 16:(tt + 1) * 16], PS[4][:, 0:16]),
                reads=("ps4",), writes=("g_tm",))
            op("pe", lambda t, tt=tt: t.transpose(
                PS[4][0:16, 0:128], s_g_tm[:, tt * 16:(tt + 1) * 16],
                s_ident_f32[:]),
                reads=("g_tm", "ident_f32"), writes=("ps4",))
            op("dve", lambda v, tt=tt: v.tensor_copy(
                s_g_fm[:, tt * 128:(tt + 1) * 128], PS[4][0:16, 0:128]),
                reads=("ps4",), writes=("g_fm",))
            op("pe", lambda t, tt=tt: t.transpose(
                PS[4][0:16, 0:128], s_lnb_tm[:, tt * 16:(tt + 1) * 16],
                s_ident_f32[:]),
                reads=("lnb_tm", "ident_f32"), writes=("ps4",))
            op("dve", lambda v, tt=tt: v.tensor_copy(
                s_lnb_fm[:, tt * 128:(tt + 1) * 128], PS[4][0:16, 0:128]),
                reads=("ps4",), writes=("lnb_fm",))
        op("act", lambda s: s.activation(
            s_gneg_fm[:], s_g_fm[:], AF.Copy, scale=-1.0),
            reads=("g_fm",), writes=("gneg_fm",))
        op("pool", lambda g: g.tensor_add(s_gnl_fm[:], s_g_fm[:], s_lnb_fm[:]),
           reads=("g_fm", "lnb_fm"), writes=("gnl_fm",))
        dma("sp_f", scr_ap[0:16, :], s_gneg_fm[:], reads=("gneg_fm",),
            writes=("scr0",))
        dma("sp_f", scr_ap[16:32, :], s_g_fm[:], reads=("g_fm",),
            writes=("scr1",))
        dma("sp_f", scr_ap[32:48, :], s_gnl_fm[:], reads=("gnl_fm",),
            writes=("scr2",))

        dma("sp_f", s_glast[0:1, 0:NCH * 16].rearrange(
                "a (i h) -> a i h", h=16),
            scr_ap[16:32, 63::64].rearrange("h i -> () i h"),
            reads=("scr1",), writes=("glast",),
            allow_slow_non_contiguous=True)

        for i in range(NCH):
            tt, half = divmod(i, 2)
            grows = s_g_tm[half * 64:(half + 1) * 64, tt * 16:(tt + 1) * 16]
            brows = s_beta[half * 64:(half + 1) * 64, tt * 16:(tt + 1) * 16]
            op("dve", lambda v, grows=grows: v.tensor_copy(s_gch[:], grows),
               reads=("g_tm",), writes=("gch",))
            op("dve", lambda v, brows=brows: v.tensor_copy(s_bch[:], brows),
               reads=("beta",), writes=("bch",))
            op("act", lambda s, i=i: s.activation(
                s_eg[i][:], s_gch[:], AF.Exp), reads=("gch",), writes=(f"eg{i}",))
            op("pool", lambda g, i=i: g.tensor_mul(
                s_bg[i][:], s_eg[i][:], s_bch[:]),
                reads=(f"eg{i}", "bch"), writes=(f"bg{i}",))
            op("pe", lambda t, i=i: t.matmul(
                PS[4][0:64, 0:16], s_ones1[0:1, 0:64],
                s_glast[0:1, i * 16:(i + 1) * 16], start=True, stop=True),
                reads=("ones1", "glast"), writes=("ps4",))
            op("act", lambda s, i=i: s.activation(
                s_egl[i][:], PS[4][0:64, 0:16], AF.Exp),
                reads=("ps4",), writes=(f"egl{i}",))
            op("dve", lambda v, i=i: v.tensor_sub(
                s_el[i][:], PS[4][0:64, 0:16], s_gch[:]),
                reads=("ps4", "gch"), writes=(f"el{i}",))
            op("act", lambda s, i=i: s.activation(
                s_el[i][:], s_el[i][:], AF.Exp),
                reads=(f"el{i}",), writes=(f"el{i}",))

        for pp, pname in enumerate(("q", "k", "v", "gate")):
            issue_slabs(seg * 26 + 2 * pp + 3)
            for tt in range(NTT):
                pbase = 0 if tt % 2 == 0 else 5
                for half in range(2):
                    sl, sln = slab(seg, 2 * pp + half)
                    for k in range(NKC):
                        op("pe", lambda t, k=k, tt=tt, half=half, sl=sl,
                           pbase=pbase: t.matmul(
                            PS[pbase + half][:, 0:512],
                            s_xn_fm[k][:, tt * 128:(tt + 1) * 128],
                            sl[:, k * 512:(k + 1) * 512],
                            start=(k == 0), stop=(k == NKC - 1)),
                            reads=(f"xnfm{k}", sln), writes=(f"ps{pbase + half}",))
                if pp in (0, 1):
                    for half in range(2):
                        bi = 2 * (tt % 2) + half
                        mb = s_m13[bi]
                        mbn = f"m13_{bi}"
                        gsl = s_gf[:, half * 512:(half + 1) * 512]
                        op("act", lambda s, half=half, pbase=pbase, gsl=gsl:
                           s.activation(
                            gsl, PS[pbase + half][:, 0:512], AF.Square),
                            reads=(f"ps{pbase + half}",), writes=(f"gf{half}",))
                        op("dve", lambda v, mb=mb, gsl=gsl: v.reduce_sum(
                            mb[:, 0:8],
                            gsl.rearrange("p (h d) -> p h d", d=64), axis=AX.X),
                            reads=(f"gf{half}",), writes=(mbn,))
                        op("dve", lambda v, mb=mb: v.tensor_scalar(
                            mb[:, 0:8], mb[:, 0:8], 1e-6, None, ALU.add),
                            reads=(mbn,), writes=(mbn,))
                        op("act", lambda s, mb=mb: s.activation(
                            mb[:, 0:8], mb[:, 0:8], AF.Ln),
                            reads=(mbn,), writes=(mbn,))
                        op("dve", lambda v, pp=pp, mb=mb: v.tensor_scalar(
                            mb[:, 0:8], mb[:, 0:8], -0.5,
                            (LN8 if pp == 0 else 0.0), ALU.mult, ALU.add),
                            reads=(mbn,), writes=(mbn,))
                        op("act", lambda s, mb=mb: s.activation(
                            mb[:, 0:8], mb[:, 0:8], AF.Exp),
                            reads=(mbn,), writes=(mbn,))
                        dst = s_qp[tt] if pp == 0 else s_kp2[tt]
                        dn = f"{pname}p{tt}"
                        op("dve", lambda v, dst=dst, half=half, pbase=pbase,
                           mb=mb, mbn=mbn: v.tensor_tensor(
                            dst[:, half * 512:(half + 1) * 512].rearrange(
                                "p (h d) -> p h d", d=64),
                            PS[pbase + half][:, 0:512].rearrange(
                                "p (h d) -> p h d", d=64),
                            mb[:, 0:8].rearrange(
                                "p (h o) -> p h o", o=1).broadcast_to(
                                [128, 8, 64]), ALU.mult),
                            reads=(f"ps{pbase + half}", mbn), writes=(dn,))
                elif pp == 2:
                    for half in range(2):
                        op("dve", lambda v, tt=tt, half=half, pbase=pbase:
                           v.tensor_copy(
                            s_vp[tt][:, half * 512:(half + 1) * 512],
                            PS[pbase + half][:, 0:512]),
                            reads=(f"ps{pbase + half}",), writes=(f"vp{tt}",))
                else:
                    for half in range(2):
                        op("act", lambda s, tt=tt, half=half, pbase=pbase:
                           s.activation(
                            s_gate[tt][:, half * 512:(half + 1) * 512],
                            PS[pbase + half][:, 0:512], AF.Copy),
                            reads=(f"ps{pbase + half}",), writes=(f"gate{tt}",))

        if seg == 0:
            op("dve", lambda v: v.memset(s_S[:], 0.0), writes=("S",))
            op("pool", lambda g: g.memset(s_Sb[:], 0.0), writes=("Sb",))

        def emit_pre1(i, piece):
            p = i % 2
            tt, half2 = divmod(i, 2)
            rlo, rhi = half2 * 64, (half2 + 1) * 64
            if piece == 0:
                for r, arow in ((0, 64), (2, 65), (1, 66)):
                    dma("sp_f", s_nega[arow:arow + 1, :].rearrange(
                            "a (h c) -> a h c", c=64),
                        scr_ap[r * 16:(r + 1) * 16, i * 64:(i + 1) * 64],
                        reads=(f"scr{r}",), writes=("negaA",))
                op("dve", lambda v, tt=tt, rlo=rlo, rhi=rhi: v.tensor_copy(
                    s_qb[0][:], s_qp[tt][rlo:rhi, :]),
                    reads=(f"qp{tt}",), writes=("qb0",))
                op("act", lambda s, tt=tt, rlo=rlo, rhi=rhi: s.activation(
                    s_kb[0][:], s_kp2[tt][rlo:rhi, :], AF.Copy),
                    reads=(f"kp{tt}",), writes=("kb0",))
                op("act", lambda s, tt=tt, rlo=rlo, rhi=rhi: s.activation(
                    s_vb[0][:], s_vp[tt][rlo:rhi, :], AF.Copy),
                    reads=(f"vp{tt}",), writes=("vb0",))
                op("dve", lambda v, tt=tt, rlo=rlo, rhi=rhi: v.tensor_copy(
                    s_bch[:], s_beta[rlo:rhi, tt * 16:(tt + 1) * 16]),
                    reads=("beta",), writes=("bch",))
            elif piece in (1, 2):
                src2, srcn, dst, dn = (
                    (s_qb[0], "qb0", s_q_fm[p], f"q_fm{p}") if piece == 1 else
                    (s_kb[0], "kb0", s_k_fm[p], f"k_fm{p}"))
                for h in range(H):
                    op("pe", lambda t, src2=src2, h=h: t.transpose(
                        PB[0][0:64, h * 64:(h + 1) * 64],
                        src2[:, h * 64:(h + 1) * 64], s_ident_bf[0:64, 0:64]),
                        reads=(srcn, "ident_bf"), writes=("pb0",))
                op("dve", lambda v, dst=dst: v.tensor_copy(
                    dst[:], PB[0][0:64, :]), reads=("pb0",), writes=(dn,))
            elif piece in (3, 4):
                if piece == 3:
                    lset, lname, k16, k16n, tri, trin = (
                        s_L1[0], "L1_0", s_gnl_fm, "gnl_fm", s_tril, "tril")
                else:
                    lset, lname, k16, k16n, tri, trin = (
                        s_L3[p], f"L3_{p}", s_gneg_fm, "gneg_fm", s_triu1,
                        "triu1")
                for half in range(2):
                    op("pe", lambda t, i=i, half=half, k16=k16: t.matmul(
                        PS[2][0:64, 0:512], k16[0:16, i * 64:(i + 1) * 64],
                        s_erep[0:16, half * 512:(half + 1) * 512],
                        start=True, stop=False),
                        reads=(k16n, "erep"), writes=("ps2",))
                    op("pe", lambda t, half=half, tri=tri: t.matmul(
                        PS[2][0:64, 0:512], tri[:, 0:64],
                        s_nega[:, half * 512:(half + 1) * 512],
                        start=False, stop=True),
                        reads=(trin, "nega0", "negaA"), writes=("ps2",))
                    op("act", lambda s, half=half, lset=lset: s.activation(
                        lset[:, half * 512:(half + 1) * 512],
                        PS[2][0:64, 0:512], AF.Exp),
                        reads=("ps2",),
                        writes=((lname,) if lset is s_L1[0] else
                                (f"{lname}h{half}",)))

        def emit_pre2(i):
            p = i % 2
            op("dve", lambda v: v.tensor_tensor(
                s_R[0][:].rearrange("c (h w) -> c h w", w=128)[:, :, 0:64],
                s_vb[0][:].rearrange("c (h d) -> c h d", d=64),
                s_bch[:].rearrange("c (h o) -> c h o", o=1).broadcast_to(
                    [64, 16, 64]), ALU.mult),
                reads=("vb0", "bch"), writes=("R0",))
            op("dve", lambda v, i=i: v.tensor_tensor(
                s_R[0][:].rearrange("c (h w) -> c h w", w=128)[:, :, 64:128],
                s_kb[0][:].rearrange("c (h d) -> c h d", d=64),
                s_bg[i][:].rearrange("c (h o) -> c h o", o=1).broadcast_to(
                    [64, 16, 64]), ALU.mult),
                reads=("kb0", f"bg{i}"), writes=("R0",))
            for h in range(H):
                ps = PS[5 + h // 8]
                op("pe", lambda t, p=p, h=h, ps=ps: t.matmul(
                    ps[0:64, (h % 8) * 64:(h % 8) * 64 + 64],
                    s_k_fm[p][:, h * 64:(h + 1) * 64],
                    s_k_fm[p][:, h * 64:(h + 1) * 64],
                    start=True, stop=True),
                    reads=(f"k_fm{p}",), writes=(f"ps{5 + h // 8}",))
            for hh in range(2):
                op("dve", lambda v, hh=hh: v.tensor_tensor(
                    s_M[0][:, hh * 512:(hh + 1) * 512],
                    PS[5 + hh][0:64, 0:512],
                    s_L1[0][:, hh * 512:(hh + 1) * 512], ALU.mult),
                    reads=(f"ps{5 + hh}", "L1_0"),
                    writes=(f"M0q{2 * hh}", f"M0q{2 * hh + 1}"))
            for h in range(H):
                op("pe", lambda t, h=h: t.transpose(
                    PB[0][0:64, h * 64:(h + 1) * 64],
                    s_M[0][:, h * 64:(h + 1) * 64], s_ident_bf[0:64, 0:64]),
                    reads=(f"M0q{h // 4}", "ident_bf"), writes=("pb0",))
            op("act", lambda s: s.activation(
                s_MT[0][:], PB[0][0:64, :], AF.Copy),
                reads=("pb0",), writes=("MT0q0", "MT0q1", "MT0q2", "MT0q3"))
            for q in range(4):
                op("pool", lambda g, q=q: g.tensor_add(
                    s_Ah[0][:, q * 256:(q + 1) * 256],
                    s_MT[0][:, q * 256:(q + 1) * 256],
                    s_irep[:, q * 256:(q + 1) * 256]),
                    reads=(f"MT0q{q}", "irep"), writes=(f"Ah0q{q}",))
            op("dve", lambda v, i=i: v.tensor_tensor(
                s_wtm[:].rearrange("c (h d) -> c h d", d=64),
                s_qb[0][:].rearrange("c (h d) -> c h d", d=64),
                s_eg[i][:].rearrange("c (h o) -> c h o", o=1).broadcast_to(
                    [64, 16, 64]), ALU.mult),
                reads=("qb0", f"eg{i}"), writes=("wtm",))
            for h in range(H):
                op("pe", lambda t, h=h: t.transpose(
                    PB[0][0:64, h * 64:(h + 1) * 64],
                    s_wtm[:, h * 64:(h + 1) * 64], s_ident_bf[0:64, 0:64]),
                    reads=("wtm", "ident_bf"), writes=("pb0",))
            op("dve", lambda v, p=p: v.tensor_copy(s_qe_fm[p][:], PB[0][0:64, :]),
               reads=("pb0",), writes=(f"qe_fm{p}",))
            op("pool", lambda g, i=i, p=p: g.tensor_tensor(
                s_kpp[p][:].rearrange("c (h d) -> c h d", d=64),
                s_kb[0][:].rearrange("c (h d) -> c h d", d=64),
                s_el[i][:].rearrange("c (h o) -> c h o", o=1).broadcast_to(
                    [64, 16, 64]), ALU.mult),
                reads=("kb0", f"el{i}"), writes=(f"kpp{p}",))

        def emitA_iter(i, j):
            for grp in range(4):
                ps = PS[3 + grp % 2]
                pn = f"ps{3 + grp % 2}"
                xsrcn = ("R0" if j == 0 else f"X0g{grp}",)
                xsrc = s_R[0] if j == 0 else s_X[0]
                for hh in range(4):
                    h = grp * 4 + hh
                    op("pe", lambda t, h=h, hh=hh, xsrc=xsrc, ps=ps, grp=grp:
                       t.matmul(
                        ps[0:64, hh * 128:(hh + 1) * 128],
                        s_Ah[0][:, h * 64:(h + 1) * 64],
                        xsrc[:, h * 128:(h + 1) * 128],
                        start=True, stop=True),
                        reads=(f"Ah0q{grp}",) + xsrcn, writes=(pn,))
                if grp % 2 == 0:
                    op("dve", lambda v, grp=grp, ps=ps: v.tensor_copy(
                        s_X[0][:, grp * 512:(grp + 1) * 512], ps[0:64, 0:512]),
                        reads=(pn,), writes=(f"X0g{grp}",))
                else:
                    op("act", lambda s, grp=grp, ps=ps: s.activation(
                        s_X[0][:, grp * 512:(grp + 1) * 512], ps[0:64, 0:512],
                        AF.Copy), reads=(pn,), writes=(f"X0g{grp}",))
            if j == 5:
                return
            for h in range(H):
                ps = PS[5 + h // 8]
                op("pe", lambda t, h=h, ps=ps: t.matmul(
                    ps[0:64, (h % 8) * 64:(h % 8) * 64 + 64],
                    s_MT[0][:, h * 64:(h + 1) * 64],
                    s_M[0][:, h * 64:(h + 1) * 64],
                    start=True, stop=True),
                    reads=(f"MT0q{h // 4}", f"M0q{h // 4}"),
                    writes=(f"ps{5 + h // 8}",))
            for h in range(H):
                ps = PS[3 + h // 8]
                op("pe", lambda t, h=h, ps=ps: t.matmul(
                    ps[0:64, (h % 8) * 64:(h % 8) * 64 + 64],
                    s_M[0][:, h * 64:(h + 1) * 64],
                    s_MT[0][:, h * 64:(h + 1) * 64],
                    start=True, stop=True),
                    reads=(f"M0q{h // 4}", f"MT0q{h // 4}"),
                    writes=(f"ps{3 + h // 8}",))
            for q in range(4):
                op("act", lambda s, q=q: s.activation(
                    s_MT[0][:, q * 256:(q + 1) * 256],
                    PS[3 + q // 2][0:64, (q % 2) * 256:(q % 2) * 256 + 256],
                    AF.Copy),
                    reads=(f"ps{3 + q // 2}",), writes=(f"MT0q{q}",))
                op("pool", lambda g, q=q: g.tensor_add(
                    s_Ah[0][:, q * 256:(q + 1) * 256],
                    s_MT[0][:, q * 256:(q + 1) * 256],
                    s_irep[:, q * 256:(q + 1) * 256]),
                    reads=(f"MT0q{q}", "irep"), writes=(f"Ah0q{q}",))
                op("dve", lambda v, q=q: v.tensor_copy(
                    s_M[0][:, q * 256:(q + 1) * 256],
                    PS[5 + q // 2][0:64, (q % 2) * 256:(q % 2) * 256 + 256]),
                    reads=(f"ps{5 + q // 2}",), writes=(f"M0q{q}",))

        def emitA_post(i):
            p = i % 2
            for pi in range(2):
                op("dve", lambda v, p=p, pi=pi: v.tensor_copy(
                    s_u[p][:, pi * 512:(pi + 1) * 512].rearrange(
                        "c (h d) -> c h d", d=64),
                    s_X[0][:, pi * 1024:(pi + 1) * 1024].rearrange(
                        "c (h w) -> c h w", w=128)[:, :, 0:64]),
                    reads=(f"X0g{2 * pi}", f"X0g{2 * pi + 1}"),
                    writes=(f"u{p}h{pi}",))
            op("dve", lambda v: v.tensor_copy(
                s_wtm[:].rearrange("c (h d) -> c h d", d=64),
                s_X[0][:].rearrange("c (h w) -> c h w", w=128)[:, :, 64:128]),
                reads=("X0g0", "X0g1", "X0g2", "X0g3"), writes=("wtm",))
            for h in range(H):
                op("pe", lambda t, h=h: t.transpose(
                    PB[0][0:64, h * 64:(h + 1) * 64],
                    s_wtm[:, h * 64:(h + 1) * 64], s_ident_bf[0:64, 0:64]),
                    reads=("wtm", "ident_bf"), writes=("pb0",))
            op("dve", lambda v, p=p: v.tensor_copy(s_wT[p][:], PB[0][0:64, :]),
               reads=("pb0",), writes=(f"wT{p}",))

        def emitB(i, piece):
            p = i % 2
            tt, half2 = divmod(i, 2)
            if piece == 0:
                for h in range(H):
                    ps = PS[h // 8]
                    op("pe", lambda t, p=p, h=h, ps=ps: t.matmul(
                        ps[0:64, (h % 8) * 64:(h % 8) * 64 + 64],
                        s_k_fm[p][:, h * 64:(h + 1) * 64],
                        s_q_fm[p][:, h * 64:(h + 1) * 64],
                        start=True, stop=True),
                        reads=(f"k_fm{p}", f"q_fm{p}"), writes=(f"ps{h // 8}",))
                for hh in range(2):
                    op("dve", lambda v, p=p, hh=hh: v.tensor_tensor(
                        s_aT[:, hh * 512:(hh + 1) * 512],
                        PS[hh][0:64, 0:512],
                        s_L3[p][:, hh * 512:(hh + 1) * 512], ALU.mult),
                        reads=(f"ps{hh}", f"L3_{p}h{hh}"), writes=(f"aTh{hh}",))
            elif piece == 1:
                for h in range(H):
                    ps = PS[h // 8]
                    op("pe", lambda t, p=p, h=h, ps=ps: t.matmul(
                        ps[0:64, (h % 8) * 64:(h % 8) * 64 + 64],
                        s_wT[p][:, h * 64:(h + 1) * 64],
                        s_Sb[:, h * 64:(h + 1) * 64],
                        start=True, stop=True),
                        reads=(f"wT{p}", "Sb"), writes=(f"ps{h // 8}",))
                for pi in range(2):
                    op("dve", lambda v, p=p, pi=pi: v.tensor_sub(
                        s_vn[:, pi * 512:(pi + 1) * 512],
                        s_u[p][:, pi * 512:(pi + 1) * 512],
                        PS[pi][0:64, 0:512]),
                        reads=(f"u{p}h{pi}", f"ps{pi}"), writes=(f"vnh{pi}",))
            elif piece == 2:
                for h in range(H):
                    ps = PS[h // 8]
                    op("pe", lambda t, p=p, h=h, ps=ps: t.matmul(
                        ps[0:64, (h % 8) * 64:(h % 8) * 64 + 64],
                        s_qe_fm[p][:, h * 64:(h + 1) * 64],
                        s_Sb[:, h * 64:(h + 1) * 64],
                        start=True, stop=False),
                        reads=(f"qe_fm{p}", "Sb"), writes=(f"ps{h // 8}",))
                    op("pe", lambda t, h=h, ps=ps: t.matmul(
                        ps[0:64, (h % 8) * 64:(h % 8) * 64 + 64],
                        s_aT[:, h * 64:(h + 1) * 64],
                        s_vn[:, h * 64:(h + 1) * 64],
                        start=False, stop=True),
                        reads=(f"aTh{h // 8}", f"vnh{h // 8}"),
                        writes=(f"ps{h // 8}",))
                for pi in range(2):
                    op("act", lambda s, tt=tt, half2=half2, pi=pi: s.activation(
                        s_o[tt][half2 * 64:(half2 + 1) * 64,
                                pi * 512:(pi + 1) * 512],
                        PS[pi][0:64, 0:512], AF.Copy),
                        reads=(f"ps{pi}",), writes=(f"o{tt}",))
            elif piece == 3:
                for h in range(H):
                    ps = PS[h // 8]
                    op("pe", lambda t, p=p, h=h, ps=ps: t.matmul(
                        ps[0:64, (h % 8) * 64:(h % 8) * 64 + 64],
                        s_kpp[p][:, h * 64:(h + 1) * 64],
                        s_vn[:, h * 64:(h + 1) * 64],
                        start=True, stop=True),
                        reads=(f"kpp{p}", f"vnh{h // 8}"), writes=(f"ps{h // 8}",))
            else:
                op("pool", lambda g, i=i: g.tensor_tensor(
                    s_S[:].rearrange("c (h d) -> c h d", d=64),
                    s_S[:].rearrange("c (h d) -> c h d", d=64),
                    s_egl[i][:].rearrange("c (h o) -> c h o", o=1).broadcast_to(
                        [64, 16, 64]), ALU.mult),
                    reads=("S", f"egl{i}"), writes=("S",))
                for hh in range(2):
                    op("dve", lambda v, hh=hh: v.tensor_add(
                        s_S[:, hh * 512:(hh + 1) * 512],
                        s_S[:, hh * 512:(hh + 1) * 512],
                        PS[hh][0:64, 0:512]),
                        reads=("S", f"ps{hh}"), writes=("S",))
                op("dve", lambda v: v.tensor_copy(s_Sb[:], s_S[:]),
                   reads=("S",), writes=("Sb",))

        for pc in range(5):
            emit_pre1(0, pc)
        emit_pre2(0)
        for i in range(NCH):
            for j in range(6):
                emitA_iter(i, j)
                if j < 5 and i + 1 < NCH:
                    emit_pre1(i + 1, j)
                if j < 5 and i >= 1:
                    emitB(i - 1, j)
            emitA_post(i)
            if i + 1 < NCH:
                emit_pre2(i + 1)
        for j in range(5):
            emitB(NCH - 1, j)

        def emit_oepi(tt):
            op("act", lambda s, tt=tt: s.activation(
                s_gf[:], s_o[tt][:], AF.Square), reads=(f"o{tt}",), writes=("gf",))
            op("dve", lambda v: v.reduce_sum(
                s_stat[:, 0:8],
                s_gf[:, 0:512].rearrange("p (h d) -> p h d", d=64), axis=AX.X),
                reads=("gf",), writes=("stat",))
            op("dve", lambda v: v.reduce_sum(
                s_m12[:, 0:8],
                s_gf[:, 512:1024].rearrange("p (h d) -> p h d", d=64), axis=AX.X),
                reads=("gf",), writes=("m12",))
            for half in range(2):
                src = s_stat[:, 0:8] if half == 0 else s_m12[:, 0:8]
                sn = "stat" if half == 0 else "m12"
                op("dve", lambda v, src=src: v.tensor_scalar(
                    src, src, 1.0 / 64, 1e-5, ALU.mult, ALU.add),
                    reads=(sn,), writes=(sn,))
                op("act", lambda s, src=src: s.activation(
                    src, src, AF.Ln), reads=(sn,), writes=(sn,))
                op("dve", lambda v, src=src: v.tensor_scalar(
                    src, src, -0.5, None, ALU.mult), reads=(sn,), writes=(sn,))
                op("act", lambda s, src=src: s.activation(
                    src, src, AF.Exp), reads=(sn,), writes=(sn,))
            op("act", lambda s, tt=tt: s.activation(
                s_gf[:], s_gate[tt][:], AF.Silu), reads=(f"gate{tt}",),
                writes=("gf",))
            for half in range(2):
                src = s_stat[:, 0:8] if half == 0 else s_m12[:, 0:8]
                sn = "stat" if half == 0 else "m12"
                op("dve", lambda v, tt=tt, half=half, src=src: v.tensor_tensor(
                    s_sout[tt][:, half * 512:(half + 1) * 512].rearrange(
                        "p (h d) -> p h d", d=64),
                    s_o[tt][:, half * 512:(half + 1) * 512].rearrange(
                        "p (h d) -> p h d", d=64),
                    src.rearrange("p (h o) -> p h o", o=1).broadcast_to(
                        [128, 8, 64]), ALU.mult),
                    reads=(f"o{tt}", sn), writes=(f"sout{tt}",))
            op("dve", lambda v, tt=tt: v.tensor_mul(
                s_sout[tt][:], s_sout[tt][:], s_nwrep[:]),
                reads=(f"sout{tt}", "nwrep"), writes=(f"sout{tt}",))
            op("dve", lambda v, tt=tt: v.tensor_mul(
                s_sout[tt][:], s_sout[tt][:], s_gf[:]),
                reads=(f"sout{tt}", "gf"), writes=(f"sout{tt}",))

        def emit_wo(tt):
            pbase = (0, 5, 3)[tt % 3]

            def tpose(k):
                pr = k % 2
                op("pe", lambda t, k=k, tt=tt: t.transpose(
                    PB[0][:, 0:128], s_sout[tt][:, k * 128:(k + 1) * 128],
                    s_ident_bf[:]), reads=(f"sout{tt}", "ident_bf"),
                    writes=("pb0",))
                op("dve", lambda v, pr=pr: v.tensor_copy(
                    s_ofm[pr][:], PB[0][:, 0:128]), reads=("pb0",),
                    writes=(f"ofm{pr}",))

            tpose(0)
            tpose(1)
            for k in range(NKC):
                pr = k % 2
                for half in range(2):
                    sl, sln = slab(seg, 24 + half)
                    op("pe", lambda t, k=k, pr=pr, half=half, sl=sl,
                       pbase=pbase: t.matmul(
                        PS[pbase + half][:, 0:512], s_ofm[pr][:],
                        sl[:, k * 512:(k + 1) * 512],
                        start=(k == 0), stop=(k == NKC - 1)),
                        reads=(f"ofm{pr}", sln), writes=(f"ps{pbase + half}",))
                if k + 2 < NKC:
                    tpose(k + 2)
            for half in range(2):
                op("act", lambda s, tt=tt, half=half, pbase=pbase: s.activation(
                    s_sout[tt][:, half * 512:(half + 1) * 512],
                    PS[pbase + half][:, 0:512], AF.Copy),
                    reads=(f"ps{pbase + half}",), writes=(f"sout{tt}",))

        def emit_combine(tt):
            op("act", lambda s, tt=tt: s.activation(
                s_gf[:], s_acc[tt][:], AF.Silu),
                reads=(f"acc{tt}",), writes=("gf",))
            op("pool", lambda g, tt=tt: g.tensor_mul(
                s_og[:], s_sout[tt][:], s_gf[:]),
                reads=(f"sout{tt}", "gf"), writes=("og",))
            dma("sp_c", s_acc[tt][:],
                x_ap[t0 + tt * 128:t0 + (tt + 1) * 128, :],
                writes=(f"acc{tt}",))
            op("dve", lambda v, tt=tt: v.tensor_add(
                s_xc[:], s_acc[tt][:], s_og[:]),
                reads=(f"acc{tt}", "og"), writes=("xc",))
            dma("actq", out_ap[t0 + tt * 128:t0 + (tt + 1) * 128, :],
                s_xc[:], reads=("xc",))

        issue_slabs(seg * 26 + 11)
        for tt in range(NTT):
            for k in range(NKC):
                op("pe", lambda t, k=k, tt=tt: t.matmul(
                    PS[4][:, 0:8], s_xn_fm[k][:, tt * 128:(tt + 1) * 128],
                    s_wgate[:, k * 8:(k + 1) * 8], start=(k == 0), stop=False),
                    reads=(f"xnfm{k}", "wgate"), writes=("ps4",))
            op("pe", lambda t: t.matmul(
                PS[4][:, 0:8], s_ones1[0:1, :], s_bgate[0:1, :],
                start=False, stop=True),
                reads=("ones1", "bgate"), writes=("ps4",))
            op("dve", lambda v: v.tensor_copy(s_logit[:], PS[4][:, 0:8]),
               reads=("ps4",), writes=("logit",))
            op("dve", lambda v: v.reduce_max(s_m12[:, 0:1], s_logit[:], axis=AX.X),
               reads=("logit",), writes=("m12",))
            op("dve", lambda v: v.tensor_scalar(
                s_m12[:, 2:3], s_m12[:, 0:1], -1.0, None, ALU.mult),
                reads=("m12",), writes=("m12",))
            op("dve", lambda v: v.tensor_scalar(
                s_stat[:, 0:8], s_logit[:], s_m12[:, 0:1], None, ALU.is_ge),
                reads=("logit", "m12"), writes=("stat",))
            op("dve", lambda v: v.scalar_tensor_tensor(
                s_stat[:, 0:8], s_stat[:, 0:8], NEG, s_logit[:],
                ALU.mult, ALU.add), reads=("stat", "logit"), writes=("stat",))
            op("dve", lambda v: v.reduce_max(s_m12[:, 1:2], s_stat[:, 0:8],
                                             axis=AX.X),
               reads=("stat",), writes=("m12",))
            op("act", lambda s, tt=tt: s.activation(
                s_score[tt][:], s_logit[:], AF.Exp, bias=s_m12[:, 2:3]),
                reads=("logit", "m12"), writes=(f"score{tt}",))
            op("dve", lambda v: v.tensor_scalar(
                s_stat[:, 0:8], s_logit[:], s_m12[:, 1:2], None, ALU.is_ge),
                reads=("logit", "m12"), writes=("stat",))
            op("dve", lambda v, tt=tt: v.tensor_tensor(
                s_score[tt][:], s_score[tt][:], s_stat[:, 0:8], ALU.mult),
                reads=(f"score{tt}", "stat"), writes=(f"score{tt}",))
            op("dve", lambda v, tt=tt: v.reduce_sum(
                s_m12[:, 3:4], s_score[tt][:], axis=AX.X),
                reads=(f"score{tt}",), writes=("m12",))
            op("dve", lambda v: v.reciprocal(s_m12[:, 4:5], s_m12[:, 3:4]),
               reads=("m12",), writes=("m12",))
            op("dve", lambda v, tt=tt: v.tensor_scalar(
                s_score[tt][:], s_score[tt][:], s_m12[:, 4:5], None, ALU.mult),
                reads=(f"score{tt}", "m12"), writes=(f"score{tt}",))
        for ei in range(8):
            issue_slabs(seg * 26 + 11 + 2 * ei)
            if ei == 0:
                for tt in range(NTT):
                    op("dve", lambda v, tt=tt: v.tensor_copy(
                        s_etmp[:, 0:8], s_score[tt][:]),
                        reads=(f"score{tt}",), writes=("etmp",))
                    op("pe", lambda t: t.transpose(
                        PB[0][0:8, 0:128], s_etmp[:, 0:8], s_ident_bf[:]),
                        reads=("etmp", "ident_bf"), writes=("pb0",))
                    op("dve", lambda v, tt=tt: v.tensor_copy(
                        s_scT[tt][:], PB[0][0:8, 0:128]), reads=("pb0",),
                        writes=(f"scT{tt}",))
            for half in range(2):
                sl, sln = slab(seg, 8 + 2 * ei + half)
                for tt in range(NTT):
                    pbase = (0, 5, 3)[tt % 3]
                    for k in range(NKC):
                        op("pe", lambda t, k=k, tt=tt, half=half, sl=sl,
                           pbase=pbase: t.matmul(
                            PS[pbase + half][:, 0:512],
                            s_xn_fm[k][:, tt * 128:(tt + 1) * 128],
                            sl[:, k * 512:(k + 1) * 512],
                            start=(k == 0), stop=(k == NKC - 1)),
                            reads=(f"xnfm{k}", sln), writes=(f"ps{pbase + half}",))
                    if ei == 0:
                        op("pe", lambda t, tt=tt, half=half: t.matmul(
                            PS[2][:, 0:512], s_scT[tt][:],
                            s_bexp[:, half * 512:(half + 1) * 512],
                            start=True, stop=True),
                            reads=(f"scT{tt}", "bexp"), writes=("ps2",))
                        op("dve", lambda v, tt=tt, half=half: v.tensor_copy(
                            s_acc[tt][:, half * 512:(half + 1) * 512],
                            PS[2][:, 0:512]), reads=("ps2",),
                            writes=(f"acc{tt}",))
                    op("act", lambda s, tt=tt, half=half, ei=ei, pbase=pbase:
                       s.activation(
                        s_etmp[:, half * 512:(half + 1) * 512],
                        PS[pbase + half][:, 0:512], AF.Copy,
                        scale=s_score[tt][:, ei:ei + 1]),
                        reads=(f"ps{pbase + half}", f"score{tt}"), writes=("etmp",))
                    op("dve", lambda v, tt=tt, half=half: v.tensor_add(
                        s_acc[tt][:, half * 512:(half + 1) * 512],
                        s_acc[tt][:, half * 512:(half + 1) * 512],
                        s_etmp[:, half * 512:(half + 1) * 512]),
                        reads=(f"acc{tt}", "etmp"), writes=(f"acc{tt}",))
            if ei < NTT:
                emit_oepi(ei)

        issue_slabs(seg * 26 + 26)
        for tt in range(NTT):
            emit_wo(tt)
            if tt >= 1:
                emit_combine(tt - 1)
        emit_combine(NTT - 1)

    ctx.finish()
    block = e(nc.Block())
    ctx.finalize(block)
    stack.close()
    return nc


_NC_CACHE = {}


def kernel(L0_attn_x, **kw):
    inp = {"L0_attn_x": L0_attn_x, **kw}
    T = L0_attn_x.shape[1]
    if T not in _NC_CACHE:
        _NC_CACHE[T] = build_program(T)
    nc = _NC_CACHE[T]
    lay = layout_inputs(inp)
    consts = host_constants()
    base = {**lay, **consts}
    x = np.ascontiguousarray(np.asarray(L0_attn_x, np.float32))
    in_maps = []
    for b in range(x.shape[0]):
        m = dict(base)
        m["x"] = x[b]
        in_maps.append(m)
    res = run_bass_kernel_spmd(nc, in_maps, core_ids=list(range(x.shape[0])))
    return np.stack([r["out"] for r in res.results], axis=0).astype(np.float32)


# version 24
# speedup vs baseline: 1.0089x; 1.0009x over previous
import numpy as np
import sys

sys.path.insert(0, "/opt/trn_rl_repo")
import ml_dtypes
import concourse.bass as bass
import concourse.mybir as mybir
from concourse.bass_utils import run_bass_kernel_spmd
from contextlib import ExitStack

F32 = mybir.dt.float32
BF16 = mybir.dt.bfloat16
AF = mybir.ActivationFunctionType
ALU = mybir.AluOpType
AX = mybir.AxisListType

D = 1024
H = 16
NKC = 8
NEG = -1e9
LN8 = float(np.log(0.125))


class Ctx:

    DMAQ = {"sp_c": "sp", "sp_x": "sp", "sp_f": "sp", "actq": "act",
            "poolq": "pool", "pq0": "pool", "pq1": "pool", "pq2": "pool",
            "pq3": "pool", "pq4": "pool",
            "sq0": "sp", "sq1": "sp", "sq2": "sp"}

    def __init__(self, nc, stack):
        self.nc = nc
        e = stack.enter_context
        names = ("pe", "dve", "act", "pool", "sp") + tuple(self.DMAQ)
        self.sems = {n: e(nc.semaphore(f"sem_{n}")) for n in names}
        self.count = {n: 0 for n in names}
        self.waited = {n: {} for n in ("pe", "dve", "act", "pool", "sp")}
        self.queues = {n: [] for n in ("pe", "dve", "act", "pool", "sp")}
        self.bufs = {}

    def _buf(self, name):
        return self.bufs.setdefault(name, {"w": None, "r": {}})

    def _wait(self, xeng, sem_name, val):
        if xeng == "pe" and sem_name == "pe":
            return
        hw = self.waited[xeng].get(sem_name, 0)
        if val <= hw:
            return
        self.waited[xeng][sem_name] = val
        sem = self.sems[sem_name]
        self.queues[xeng].append(lambda eng, s=sem, v=val: eng.wait_ge(s, v))

    def _dma_adj(self, sname, v):
        if sname in self.DMAQ:
            return self.count[sname]
        return v

    def _hazards(self, xeng, reads, writes):
        for b in reads:
            rec = self._buf(b)
            if rec["w"]:
                sname, v = rec["w"]
                self._wait(xeng, sname, self._dma_adj(sname, v))
        for b in writes:
            rec = self._buf(b)
            if rec["w"]:
                sname, v = rec["w"]
                self._wait(xeng, sname, self._dma_adj(sname, v))
            for sname, v in rec["r"].items():
                self._wait(xeng, sname, self._dma_adj(sname, v))

    def _record(self, key, reads, writes, val):
        for b in reads:
            self._buf(b)["r"][key] = max(self._buf(b)["r"].get(key, 0), val)
        for b in writes:
            rec = self._buf(b)
            rec["w"] = (key, val)
            rec["r"] = {}

    def op(self, eng, fn, reads=(), writes=()):
        self._hazards(eng, reads, writes)
        sem = self.sems[eng]
        self.count[eng] += 1
        val = self.count[eng]

        def emit(engine, fn=fn, sem=sem):
            fn(engine).then_inc(sem, 1)
        self.queues[eng].append(emit)
        self._record(eng, reads, writes, val)
        return val

    def dma(self, q, out_ap, in_ap, reads=(), writes=(), **kw):
        xeng = self.DMAQ[q]
        self._hazards(xeng, reads, writes)
        sem = self.sems[q]
        self.count[q] += 16
        val = self.count[q]

        def emit(engine, out_ap=out_ap, in_ap=in_ap, sem=sem, kw=kw):
            engine.dma_start(out_ap, in_ap, **kw).then_inc(sem, 16)
        self.queues[xeng].append(emit)
        self._record(q, reads, writes, val)
        return val

    def finish(self):
        for q in self.DMAQ:
            sem, v = self.sems[q], self.count[q]
            if v:
                self.queues["sp"].append(
                    lambda eng, s=sem, v=v: eng.wait_ge(s, v))

    def finalize(self, block):
        qs = self.queues

        @block.tensor
        def _(eng):
            for f in qs["pe"]:
                f(eng)

        @block.vector
        def _(eng):
            for f in qs["dve"]:
                f(eng)

        @block.scalar
        def _(eng):
            for f in qs["act"]:
                f(eng)

        @block.gpsimd
        def _(eng):
            for f in qs["pool"]:
                f(eng)

        @block.sync
        def _(eng):
            for f in qs["sp"]:
                f(eng)


def host_constants():
    c = {}
    c["c_ident_bf"] = np.eye(128, dtype=ml_dtypes.bfloat16)
    c["c_ident_f32"] = np.eye(128, dtype=np.float32)
    c["c_ones1"] = np.ones((1, 128), dtype=np.float32)
    tri = np.triu(np.ones((64, 64), np.float32))
    blk = np.zeros((128, 128), np.float32)
    blk[:64, :64] = tri
    blk[64:, 64:] = tri
    c["c_cumsum"] = blk.astype(ml_dtypes.bfloat16)
    er = np.zeros((16, 1024), np.float32)
    for h in range(16):
        er[h, h * 64:(h + 1) * 64] = 1.0
    c["c_erep"] = er
    c["c_erep_bf"] = er.astype(ml_dtypes.bfloat16)
    def aug(mat, row):
        m = np.zeros((67, 64), np.float32)
        m[0:64] = mat
        m[row] = 1.0
        return m
    c["c_tril"] = aug(np.tril(np.ones((64, 64), np.float32)), 64)
    c["c_triu"] = aug(np.triu(np.ones((64, 64), np.float32)), 65)
    c["c_triu1"] = aug(np.triu(np.ones((64, 64), np.float32), 1), 66)
    negI = np.zeros((64, 1024), np.float32)
    irep = np.zeros((64, 1024), np.float32)
    for h in range(16):
        negI[:, h * 64:(h + 1) * 64] = NEG * np.eye(64, dtype=np.float32)
        irep[:, h * 64:(h + 1) * 64] = np.eye(64)
    c["c_negI"] = negI
    c["c_irep_bf"] = irep.astype(ml_dtypes.bfloat16)
    return c


def layout_inputs(inp):
    d = {}
    d["w_qkvg"] = np.ascontiguousarray(np.concatenate(
        [inp["L0_attn_Wq"], inp["L0_attn_Wk"], inp["L0_attn_Wv"],
         inp["L0_attn_Wg"]], axis=1), dtype=np.float32)
    d["w_o"] = np.ascontiguousarray(inp["L0_attn_Wo"], dtype=np.float32)
    d["w_ab"] = np.ascontiguousarray(np.concatenate(
        [inp["L0_attn_Wb"], inp["L0_attn_Wa"]], axis=1), dtype=np.float32)
    d["dtb_row"] = np.concatenate(
        [np.zeros(16, np.float32),
         np.asarray(inp["L0_attn_dt_bias"], np.float32)]).reshape(1, 32)
    d["a_log"] = np.asarray(inp["L0_attn_A_log"], np.float32).reshape(16, 1)
    d["nw_rep"] = np.ascontiguousarray(np.broadcast_to(
        np.tile(np.asarray(inp["L0_attn_norm_w"], np.float32), 16),
        (128, 1024)))
    d["w_gate"] = np.ascontiguousarray(
        np.asarray(inp["L0_ffn_gate_w"], np.float32).T)
    d["b_gate"] = np.asarray(inp["L0_ffn_gate_b"], np.float32).reshape(1, 8)
    d["w_exp"] = np.ascontiguousarray(inp["L0_ffn_expert_w"], dtype=np.float32)
    d["b_exp"] = np.ascontiguousarray(inp["L0_ffn_expert_b"], dtype=np.float32)
    return d


def build_program(T):
    SEG = min(512, T)
    NSEG = T // SEG
    NTT = SEG // 128
    NCH = SEG // 64
    nc = bass.Bass("TRN2", target_bir_lowering=False)

    x_ap = nc.dram_tensor("x", [T, D], F32, kind="ExternalInput").ap()
    out_ap = nc.dram_tensor("out", [T, D], F32, kind="ExternalOutput").ap()
    wqkvg_ap = nc.dram_tensor("w_qkvg", [D, 4096], F32, kind="ExternalInput").ap()
    wo_ap = nc.dram_tensor("w_o", [D, D], F32, kind="ExternalInput").ap()
    wab_ap = nc.dram_tensor("w_ab", [D, 32], F32, kind="ExternalInput").ap()
    dtb_ap = nc.dram_tensor("dtb_row", [1, 32], F32, kind="ExternalInput").ap()
    alog_ap = nc.dram_tensor("a_log", [16, 1], F32, kind="ExternalInput").ap()
    nwrep_ap = nc.dram_tensor("nw_rep", [128, D], F32, kind="ExternalInput").ap()
    wgate_ap = nc.dram_tensor("w_gate", [D, 8], F32, kind="ExternalInput").ap()
    bgate_ap = nc.dram_tensor("b_gate", [1, 8], F32, kind="ExternalInput").ap()
    wexp_ap = nc.dram_tensor("w_exp", [8, D, D], F32, kind="ExternalInput").ap()
    bexp_ap = nc.dram_tensor("b_exp", [8, D], F32, kind="ExternalInput").ap()
    consts = host_constants()
    c_ap = {k: nc.dram_tensor(k, list(v.shape),
                              BF16 if v.dtype == ml_dtypes.bfloat16 else F32,
                              kind="ExternalInput").ap()
            for k, v in consts.items()}
    scr_ap = nc.dram_tensor("scr", [64, SEG], F32, kind="Internal").ap()

    stack = ExitStack()
    e = stack.enter_context
    ctx = Ctx(nc, stack)

    PS = [e(nc.psum_tensor(f"ps{i}", [128, 512], F32)) for i in range(7)]
    PB = [e(nc.psum_tensor(f"pb{i}", [128, 1024], BF16)) for i in range(1)]

    s_ident_bf = e(nc.sbuf_tensor("s_ident_bf", [128, 128], BF16))
    s_ident_f32 = e(nc.sbuf_tensor("s_ident_f32", [128, 128], F32))
    s_ones1 = e(nc.sbuf_tensor("s_ones1", [1, 128], F32))
    s_cumsum = e(nc.sbuf_tensor("s_cumsum", [128, 128], BF16))
    s_decb = e(nc.sbuf_tensor("s_decb", [128, 16], BF16))
    s_erep = e(nc.sbuf_tensor("s_erep", [16, 1024], F32))
    s_tril = e(nc.sbuf_tensor("s_tril", [67, 64], F32))
    s_triu = e(nc.sbuf_tensor("s_triu", [67, 64], F32))
    s_triu1 = e(nc.sbuf_tensor("s_triu1", [67, 64], F32))
    s_nega = e(nc.sbuf_tensor("s_nega", [67, 1024], F32))
    s_irep = e(nc.sbuf_tensor("s_irep", [64, 1024], BF16))
    s_nwrep = e(nc.sbuf_tensor("s_nwrep", [128, D], BF16))
    s_dtb = e(nc.sbuf_tensor("s_dtb", [1, 32], F32))
    s_alog = e(nc.sbuf_tensor("s_alog", [16, 1], F32))
    s_nAb = e(nc.sbuf_tensor("s_nAb", [128, 16], F32))
    s_wab = e(nc.sbuf_tensor("s_wab", [128, 256], BF16))
    s_wgate = e(nc.sbuf_tensor("s_wgate", [128, 64], BF16))
    s_bgate = e(nc.sbuf_tensor("s_bgate", [1, 8], F32))
    s_bexp = e(nc.sbuf_tensor("s_bexp", [8, D], BF16))
    s_tmprow = e(nc.sbuf_tensor("s_tmprow", [1, 16], F32))

    NSL = 3
    s_slab = [e(nc.sbuf_tensor(f"s_slab{j}", [128, 4096], BF16)) for j in range(NSL)]

    s_x = [e(nc.sbuf_tensor("s_x0", [128, D], F32))] * 2
    s_xc = e(nc.sbuf_tensor("s_xc", [128, D], F32))
    s_stat = e(nc.sbuf_tensor("s_stat", [128, 8], F32))
    s_xn_tm = e(nc.sbuf_tensor("s_xn_tm", [128, D], BF16))
    s_xn_fm = [e(nc.sbuf_tensor(f"s_xn_fm{k}", [128, SEG], BF16)) for k in range(NKC)]

    s_ab = e(nc.sbuf_tensor("s_ab", [128, NTT * 32], F32))
    s_beta = e(nc.sbuf_tensor("s_beta", [128, NTT * 16], F32))
    s_sp = e(nc.sbuf_tensor("s_sp", [128, NTT * 16], F32))
    s_dec = e(nc.sbuf_tensor("s_dec", [128, NTT * 16], F32))
    s_g_tm = e(nc.sbuf_tensor("s_g_tm", [128, NTT * 16], F32))
    s_lnb_tm = e(nc.sbuf_tensor("s_lnb_tm", [128, NTT * 16], F32))
    s_g_fm = e(nc.sbuf_tensor("s_g_fm", [16, SEG], F32))
    s_gnl_fm = e(nc.sbuf_tensor("s_gnl_fm", [16, SEG], F32))
    s_gneg_fm = e(nc.sbuf_tensor("s_gneg_fm", [16, SEG], F32))
    s_lnb_fm = e(nc.sbuf_tensor("s_lnb_fm", [16, SEG], F32))
    s_glast = e(nc.sbuf_tensor("s_glast", [1, max(NCH * 16, 16)], F32))

    def chunk_set(name, dt=F32, w=16):
        return [e(nc.sbuf_tensor(f"{name}{i}", [64, w], dt)) for i in range(NCH)]

    s_qp = [e(nc.sbuf_tensor(f"s_qp{t}", [128, D], BF16)) for t in range(NTT)]
    s_kp2 = [e(nc.sbuf_tensor(f"s_kp2{t}", [128, D], BF16)) for t in range(NTT)]
    s_vp = [e(nc.sbuf_tensor(f"s_vp{t}", [128, D], BF16)) for t in range(NTT)]
    s_eg = chunk_set("s_eg")
    s_bg = chunk_set("s_bg")
    s_el = chunk_set("s_el")
    s_egl = chunk_set("s_egl")

    def par(name, w=H * 64, dt=BF16):
        t = e(nc.sbuf_tensor(f"{name}0", [64, w], dt))
        return [t, t]

    def par2(name, w=H * 64, dt=BF16):
        return [e(nc.sbuf_tensor(f"{name}{p}", [64, w], dt)) for p in range(2)]

    s_vb = par("s_vb")
    s_bch = e(nc.sbuf_tensor("s_bch", [64, 16], F32))
    s_gch = e(nc.sbuf_tensor("s_gch", [64, 16], F32))
    s_qb = par("s_qb")
    s_kb = par("s_kb")
    s_q_fm = par2("s_q_fm")
    s_k_fm = par2("s_k_fm")
    s_u = par2("s_u")
    s_wT = par2("s_wT")
    s_L1 = par("s_L1_")
    s_L3 = par2("s_L3_")
    s_R = par("s_R", w=H * 128)
    s_M = par("s_M")
    s_MT = par("s_MT")
    s_Ah = par("s_Ah")
    s_X = par("s_X", w=H * 128)

    s_S = e(nc.sbuf_tensor("s_S", [64, H * 64], F32))
    s_Sb = e(nc.sbuf_tensor("s_Sb", [64, H * 64], BF16))
    s_aT = e(nc.sbuf_tensor("s_aT", [64, H * 64], BF16))
    s_wtm = e(nc.sbuf_tensor("s_wtm", [64, H * 64], BF16))
    s_vn = e(nc.sbuf_tensor("s_vn", [64, H * 64], BF16))
    s_kpp = [e(nc.sbuf_tensor(f"s_kpp{p}", [64, H * 64], BF16)) for p in range(2)]
    s_qe_tm = e(nc.sbuf_tensor("s_qe_tm", [64, H * 64], BF16))
    s_qe_fm = [e(nc.sbuf_tensor(f"s_qe_fm{p}", [64, H * 64], BF16)) for p in range(2)]

    s_o = [e(nc.sbuf_tensor(f"s_o{t}", [128, D], BF16)) for t in range(NTT)]
    s_gate = [e(nc.sbuf_tensor(f"s_gate{t}", [128, D], BF16)) for t in range(NTT)]
    s_og = e(nc.sbuf_tensor("s_og", [128, D], BF16))
    s_ofm = [e(nc.sbuf_tensor(f"s_ofm{p}", [128, 128], BF16)) for p in range(2)]
    s_sout = [e(nc.sbuf_tensor(f"s_sout{t}", [128, D], BF16)) for t in range(NTT)]
    s_logit = e(nc.sbuf_tensor("s_logit", [128, 8], F32))
    s_m12 = e(nc.sbuf_tensor("s_m12", [128, 8], F32))
    s_m13 = [e(nc.sbuf_tensor(f"s_m13_{j}", [128, 8], F32)) for j in range(4)]
    s_score = [e(nc.sbuf_tensor(f"s_score{t}", [128, 8], F32)) for t in range(NTT)]
    s_scT = [e(nc.sbuf_tensor(f"s_scT{t}", [8, 128], BF16)) for t in range(NTT)]
    s_acc = [e(nc.sbuf_tensor(f"s_acc{t}", [128, D], F32)) for t in range(NTT)]
    s_etmp = e(nc.sbuf_tensor("s_etmp", [128, D], BF16))
    s_gf = e(nc.sbuf_tensor("s_gf", [128, D], BF16))

    op, dma = ctx.op, ctx.dma

    for dst, src, name in (
            (s_ident_bf, c_ap["c_ident_bf"], "ident_bf"),
            (s_ident_f32, c_ap["c_ident_f32"], "ident_f32"),
            (s_ones1, c_ap["c_ones1"], "ones1"),
            (s_cumsum, c_ap["c_cumsum"], "cumsum"),
            (s_erep, c_ap["c_erep"], "erep"),
            (s_tril, c_ap["c_tril"], "tril"),
            (s_triu, c_ap["c_triu"], "triu"),
            (s_triu1, c_ap["c_triu1"], "triu1"),
            (s_irep, c_ap["c_irep_bf"], "irep"),
            (s_dtb, dtb_ap, "dtb"),
            (s_alog, alog_ap, "alog"),
            (s_bgate, bgate_ap, "bgate")):
        dma("sp_c", dst[:], src, writes=(name,))
    dma("sp_c", s_nega[0:64, :], c_ap["c_negI"], writes=("nega0",))
    dma("poolq", s_nwrep[:], nwrep_ap, writes=("nwrep",))
    dma("poolq", s_wab[:].rearrange("p (kc a) -> p kc a", a=32),
        wab_ap.rearrange("(kc p) a -> p kc a", p=128), writes=("wab",))
    dma("poolq", s_wgate[:].rearrange("p (kc a) -> p kc a", a=8),
        wgate_ap.rearrange("(kc p) a -> p kc a", p=128), writes=("wgate",))
    dma("poolq", s_bexp[:], bexp_ap, writes=("bexp",))

    op("act", lambda s: s.activation(s_alog[:], s_alog[:], AF.Exp),
       reads=("alog",), writes=("alog",))
    op("pe", lambda t: t.transpose(PS[4][0:1, 0:16], s_alog[:], s_ident_f32[0:16, 0:16]),
       reads=("alog", "ident_f32"), writes=("ps4",))
    op("dve", lambda v: v.tensor_copy(s_tmprow[:], PS[4][0:1, 0:16]),
       reads=("ps4",), writes=("tmprow",))
    op("pe", lambda t: t.matmul(PS[4][:, 0:16], s_ones1[0:1, :], s_tmprow[:],
                                start=True, stop=True),
       reads=("ones1", "tmprow"), writes=("ps4",))
    op("dve", lambda v: v.tensor_scalar(s_nAb[:], PS[4][:, 0:16], -1.0, None,
                                        ALU.mult),
       reads=("ps4",), writes=("nAb",))

    def slab_src(idx):
        if idx < 8:
            return wqkvg_ap[:, idx * 512:(idx + 1) * 512].rearrange(
                "(kc p) n -> p kc n", p=128)
        if idx < 24:
            ei, hh = divmod(idx - 8, 2)
            return wexp_ap[ei, :, hh * 512:(hh + 1) * 512].rearrange(
                "(kc p) n -> p kc n", p=128)
        h = idx - 24
        return wo_ap[:, h * 512:(h + 1) * 512].rearrange(
            "(kc p) n -> p kc n", p=128)

    scrw_ap = nc.dram_tensor("scr_w", [26, 128, 4096], BF16,
                             kind="Internal").ap()

    slab_issued = [0]

    def slab_name(gidx):
        return f"slab{gidx % NSL}"

    def issue_slabs(upto):
        while slab_issued[0] < upto:
            g = slab_issued[0]
            seg, idx = divmod(g, 26)
            if seg >= NSEG:
                break
            if seg == 0:
                dma(f"pq{g % NSL}", s_slab[g % NSL][:].rearrange(
                    "p (kc n) -> p kc n", n=512), slab_src(idx),
                    writes=(slab_name(g),))
                if NSEG > 1:
                    dma("pq4", scrw_ap[idx], s_slab[g % NSL][:],
                        reads=(slab_name(g),), writes=(f"scrw{idx}",))
            else:
                dma(f"sq{g % NSL}", s_slab[g % NSL][:], scrw_ap[idx],
                    reads=(f"scrw{idx}",), writes=(slab_name(g),))
            slab_issued[0] += 1

    def slab(seg, idx):
        g = seg * 26 + idx
        return s_slab[g % NSL], slab_name(g)

    issue_slabs(3)

    for seg in range(NSEG):
        t0 = seg * SEG
        for tt in range(NTT):
            xn = "x0"
            xt = s_x[0]
            mb = s_m13[tt % 4]
            mbn = f"m13_{tt % 4}"
            dma("sp_x", xt[:], x_ap[t0 + tt * 128:t0 + (tt + 1) * 128, :],
                writes=(xn,))
            op("dve", lambda v, xt=xt, mb=mb: v.reduce_sum(
                mb[:, 0:1], xt[:], axis=AX.X),
                reads=(xn,), writes=(mbn,))
            op("act", lambda s, xt=xt, mb=mb: s.activation(
                s_gf[:], xt[:], AF.Square, accum_out=mb[:, 2:3]),
                reads=(xn,), writes=("gf", mbn))
            op("dve", lambda v, mb=mb: v.tensor_scalar(
                mb[:, 1:2], mb[:, 0:1], 1.0 / D, None, ALU.mult),
                reads=(mbn,), writes=(mbn,))
            op("dve", lambda v, mb=mb: v.tensor_scalar(
                mb[:, 5:6], mb[:, 1:2], mb[:, 1:2], None, ALU.mult),
                reads=(mbn,), writes=(mbn,))
            op("dve", lambda v, mb=mb: v.tensor_scalar(
                mb[:, 3:4], mb[:, 2:3], 1.0 / D, 1e-5,
                ALU.mult, ALU.add), reads=(mbn,), writes=(mbn,))
            op("dve", lambda v, mb=mb: v.tensor_tensor(
                mb[:, 3:4], mb[:, 3:4], mb[:, 5:6], ALU.subtract),
                reads=(mbn,), writes=(mbn,))
            op("act", lambda s, mb=mb: s.activation(
                mb[:, 3:4], mb[:, 3:4], AF.Ln),
                reads=(mbn,), writes=(mbn,))
            op("dve", lambda v, mb=mb: v.tensor_scalar(
                mb[:, 4:5], mb[:, 3:4], -0.5, None, ALU.mult),
                reads=(mbn,), writes=(mbn,))
            op("act", lambda s, mb=mb: s.activation(
                mb[:, 4:5], mb[:, 4:5], AF.Exp),
                reads=(mbn,), writes=(mbn,))
            op("dve", lambda v, mb=mb: v.tensor_scalar(
                mb[:, 5:6], mb[:, 4:5], mb[:, 1:2], None, ALU.mult),
                reads=(mbn,), writes=(mbn,))
            op("dve", lambda v, xt=xt, mb=mb: v.tensor_scalar(
                s_xn_tm[:], xt[:], mb[:, 4:5], mb[:, 5:6],
                ALU.mult, ALU.subtract),
                reads=(xn, mbn), writes=("xn_tm",))
            for k in range(NKC):
                op("pe", lambda t, k=k: t.transpose(
                    PB[0][:, 0:128], s_xn_tm[:, k * 128:(k + 1) * 128],
                    s_ident_bf[:]),
                    reads=("xn_tm", "ident_bf"), writes=("pb0",))
                op("dve", lambda v, k=k, tt=tt: v.tensor_copy(
                    s_xn_fm[k][:, tt * 128:(tt + 1) * 128], PB[0][:, 0:128]),
                    reads=("pb0",), writes=(f"xnfm{k}",))
            for k in range(NKC):
                op("pe", lambda t, k=k, tt=tt: t.matmul(
                    PS[4][:, 0:32], s_xn_fm[k][:, tt * 128:(tt + 1) * 128],
                    s_wab[:, k * 32:(k + 1) * 32], start=(k == 0), stop=False),
                    reads=(f"xnfm{k}", "wab"), writes=("ps4",))
            op("pe", lambda t: t.matmul(
                PS[4][:, 0:32], s_ones1[0:1, :], s_dtb[0:1, :],
                start=False, stop=True),
                reads=("ones1", "dtb"), writes=("ps4",))
            op("dve", lambda v, tt=tt: v.tensor_copy(
                s_ab[:, tt * 32:(tt + 1) * 32], PS[4][:, 0:32]),
                reads=("ps4",), writes=("ab",))

        for tt in range(NTT):
            zb = s_ab[:, tt * 32:tt * 32 + 16]
            za = s_ab[:, tt * 32 + 16:tt * 32 + 32]
            bcol = s_beta[:, tt * 16:(tt + 1) * 16]
            spc = s_sp[:, tt * 16:(tt + 1) * 16]
            dcol = s_dec[:, tt * 16:(tt + 1) * 16]
            lcol = s_lnb_tm[:, tt * 16:(tt + 1) * 16]
            op("act", lambda s, zb=zb, bcol=bcol: s.activation(
                bcol, zb, AF.Sigmoid), reads=("ab",), writes=("beta",))
            op("act", lambda s, bcol=bcol, lcol=lcol: s.activation(
                lcol, bcol, AF.Ln), reads=("beta",), writes=("lnb_tm",))
            op("dve", lambda v, za=za, dcol=dcol: v.tensor_scalar(
                dcol, za, -1.0, None, ALU.mult), reads=("ab",), writes=("dec",))
            op("dve", lambda v, za=za, spc=spc, dcol=dcol: v.tensor_tensor(
                spc, za, dcol, ALU.min), reads=("ab", "dec"), writes=("spx",))
            op("act", lambda s, spc=spc: s.activation(
                spc, spc, AF.Exp), reads=("spx",), writes=("spx",))
            op("dve", lambda v, spc=spc: v.tensor_scalar(
                spc, spc, 1.0, None, ALU.add), reads=("spx",), writes=("spx",))
            op("act", lambda s, spc=spc: s.activation(
                spc, spc, AF.Ln), reads=("spx",), writes=("spx",))
            op("dve", lambda v, za=za, dcol=dcol: v.tensor_scalar(
                dcol, za, 0.0, None, ALU.max), reads=("ab",), writes=("dec",))
            op("dve", lambda v, spc=spc, dcol=dcol: v.tensor_tensor(
                dcol, dcol, spc, ALU.add), reads=("dec", "spx"), writes=("dec",))
            op("dve", lambda v, dcol=dcol: v.tensor_tensor(
                dcol, dcol, s_nAb[:, 0:16], ALU.mult),
                reads=("dec", "nAb"), writes=("dec",))
            op("dve", lambda v, tt=tt: v.tensor_copy(
                s_decb[:], s_dec[:, tt * 16:(tt + 1) * 16]),
                reads=("dec",), writes=("decb",))
            op("pe", lambda t: t.matmul(
                PS[4][:, 0:16], s_cumsum[:, 0:128],
                s_decb[:], start=True, stop=True),
                reads=("cumsum", "decb"), writes=("ps4",))
            op("dve", lambda v, tt=tt: v.tensor_copy(
                s_g_tm[:, tt * 16:(tt + 1) * 16], PS[4][:, 0:16]),
                reads=("ps4",), writes=("g_tm",))
            op("pe", lambda t, tt=tt: t.transpose(
                PS[4][0:16, 0:128], s_g_tm[:, tt * 16:(tt + 1) * 16],
                s_ident_f32[:]),
                reads=("g_tm", "ident_f32"), writes=("ps4",))
            op("dve", lambda v, tt=tt: v.tensor_copy(
                s_g_fm[:, tt * 128:(tt + 1) * 128], PS[4][0:16, 0:128]),
                reads=("ps4",), writes=("g_fm",))
            op("pe", lambda t, tt=tt: t.transpose(
                PS[4][0:16, 0:128], s_lnb_tm[:, tt * 16:(tt + 1) * 16],
                s_ident_f32[:]),
                reads=("lnb_tm", "ident_f32"), writes=("ps4",))
            op("dve", lambda v, tt=tt: v.tensor_copy(
                s_lnb_fm[:, tt * 128:(tt + 1) * 128], PS[4][0:16, 0:128]),
                reads=("ps4",), writes=("lnb_fm",))
        op("act", lambda s: s.activation(
            s_gneg_fm[:], s_g_fm[:], AF.Copy, scale=-1.0),
            reads=("g_fm",), writes=("gneg_fm",))
        op("pool", lambda g: g.tensor_add(s_gnl_fm[:], s_g_fm[:], s_lnb_fm[:]),
           reads=("g_fm", "lnb_fm"), writes=("gnl_fm",))
        dma("sp_f", scr_ap[0:16, :], s_gneg_fm[:], reads=("gneg_fm",),
            writes=("scr0",))
        dma("sp_f", scr_ap[16:32, :], s_g_fm[:], reads=("g_fm",),
            writes=("scr1",))
        dma("sp_f", scr_ap[32:48, :], s_gnl_fm[:], reads=("gnl_fm",),
            writes=("scr2",))

        dma("sp_f", s_glast[0:1, 0:NCH * 16].rearrange(
                "a (i h) -> a i h", h=16),
            scr_ap[16:32, 63::64].rearrange("h i -> () i h"),
            reads=("scr1",), writes=("glast",),
            allow_slow_non_contiguous=True)

        for i in range(NCH):
            tt, half = divmod(i, 2)
            grows = s_g_tm[half * 64:(half + 1) * 64, tt * 16:(tt + 1) * 16]
            brows = s_beta[half * 64:(half + 1) * 64, tt * 16:(tt + 1) * 16]
            op("dve", lambda v, grows=grows: v.tensor_copy(s_gch[:], grows),
               reads=("g_tm",), writes=("gch",))
            op("dve", lambda v, brows=brows: v.tensor_copy(s_bch[:], brows),
               reads=("beta",), writes=("bch",))
            op("act", lambda s, i=i: s.activation(
                s_eg[i][:], s_gch[:], AF.Exp), reads=("gch",), writes=(f"eg{i}",))
            op("pool", lambda g, i=i: g.tensor_mul(
                s_bg[i][:], s_eg[i][:], s_bch[:]),
                reads=(f"eg{i}", "bch"), writes=(f"bg{i}",))
            op("pe", lambda t, i=i: t.matmul(
                PS[4][0:64, 0:16], s_ones1[0:1, 0:64],
                s_glast[0:1, i * 16:(i + 1) * 16], start=True, stop=True),
                reads=("ones1", "glast"), writes=("ps4",))
            op("act", lambda s, i=i: s.activation(
                s_egl[i][:], PS[4][0:64, 0:16], AF.Exp),
                reads=("ps4",), writes=(f"egl{i}",))
            op("dve", lambda v, i=i: v.tensor_sub(
                s_el[i][:], PS[4][0:64, 0:16], s_gch[:]),
                reads=("ps4", "gch"), writes=(f"el{i}",))
            op("act", lambda s, i=i: s.activation(
                s_el[i][:], s_el[i][:], AF.Exp),
                reads=(f"el{i}",), writes=(f"el{i}",))

        for pp, pname in enumerate(("q", "k", "v", "gate")):
            issue_slabs(seg * 26 + 2 * pp + 3)
            for tt in range(NTT):
                pbase = (0, 5, 3)[tt % 3]
                for half in range(2):
                    sl, sln = slab(seg, 2 * pp + half)
                    for k in range(NKC):
                        op("pe", lambda t, k=k, tt=tt, half=half, sl=sl,
                           pbase=pbase: t.matmul(
                            PS[pbase + half][:, 0:512],
                            s_xn_fm[k][:, tt * 128:(tt + 1) * 128],
                            sl[:, k * 512:(k + 1) * 512],
                            start=(k == 0), stop=(k == NKC - 1)),
                            reads=(f"xnfm{k}", sln), writes=(f"ps{pbase + half}",))
                if pp in (0, 1):
                    for half in range(2):
                        bi = 2 * (tt % 2) + half
                        mb = s_m13[bi]
                        mbn = f"m13_{bi}"
                        gsl = s_gf[:, half * 512:(half + 1) * 512]
                        op("act", lambda s, half=half, pbase=pbase, gsl=gsl:
                           s.activation(
                            gsl, PS[pbase + half][:, 0:512], AF.Square),
                            reads=(f"ps{pbase + half}",), writes=(f"gf{half}",))
                        op("dve", lambda v, mb=mb, gsl=gsl: v.reduce_sum(
                            mb[:, 0:8],
                            gsl.rearrange("p (h d) -> p h d", d=64), axis=AX.X),
                            reads=(f"gf{half}",), writes=(mbn,))
                        op("dve", lambda v, mb=mb: v.tensor_scalar(
                            mb[:, 0:8], mb[:, 0:8], 1e-6, None, ALU.add),
                            reads=(mbn,), writes=(mbn,))
                        op("act", lambda s, mb=mb: s.activation(
                            mb[:, 0:8], mb[:, 0:8], AF.Ln),
                            reads=(mbn,), writes=(mbn,))
                        op("dve", lambda v, pp=pp, mb=mb: v.tensor_scalar(
                            mb[:, 0:8], mb[:, 0:8], -0.5,
                            (LN8 if pp == 0 else 0.0), ALU.mult, ALU.add),
                            reads=(mbn,), writes=(mbn,))
                        op("act", lambda s, mb=mb: s.activation(
                            mb[:, 0:8], mb[:, 0:8], AF.Exp),
                            reads=(mbn,), writes=(mbn,))
                        dst = s_qp[tt] if pp == 0 else s_kp2[tt]
                        dn = f"{pname}p{tt}"
                        op("dve", lambda v, dst=dst, half=half, pbase=pbase,
                           mb=mb, mbn=mbn: v.tensor_tensor(
                            dst[:, half * 512:(half + 1) * 512].rearrange(
                                "p (h d) -> p h d", d=64),
                            PS[pbase + half][:, 0:512].rearrange(
                                "p (h d) -> p h d", d=64),
                            mb[:, 0:8].rearrange(
                                "p (h o) -> p h o", o=1).broadcast_to(
                                [128, 8, 64]), ALU.mult),
                            reads=(f"ps{pbase + half}", mbn), writes=(dn,))
                elif pp == 2:
                    for half in range(2):
                        op("dve", lambda v, tt=tt, half=half, pbase=pbase:
                           v.tensor_copy(
                            s_vp[tt][:, half * 512:(half + 1) * 512],
                            PS[pbase + half][:, 0:512]),
                            reads=(f"ps{pbase + half}",), writes=(f"vp{tt}",))
                else:
                    for half in range(2):
                        op("act", lambda s, tt=tt, half=half, pbase=pbase:
                           s.activation(
                            s_gate[tt][:, half * 512:(half + 1) * 512],
                            PS[pbase + half][:, 0:512], AF.Copy),
                            reads=(f"ps{pbase + half}",), writes=(f"gate{tt}",))

        if seg == 0:
            op("dve", lambda v: v.memset(s_S[:], 0.0), writes=("S",))
            op("pool", lambda g: g.memset(s_Sb[:], 0.0), writes=("Sb",))

        def emit_pre1(i, piece):
            p = i % 2
            tt, half2 = divmod(i, 2)
            rlo, rhi = half2 * 64, (half2 + 1) * 64
            if piece == 0:
                for r, arow in ((0, 64), (2, 65), (1, 66)):
                    dma("sp_f", s_nega[arow:arow + 1, :].rearrange(
                            "a (h c) -> a h c", c=64),
                        scr_ap[r * 16:(r + 1) * 16, i * 64:(i + 1) * 64],
                        reads=(f"scr{r}",), writes=("negaA",))
                op("dve", lambda v, tt=tt, rlo=rlo, rhi=rhi: v.tensor_copy(
                    s_qb[0][:], s_qp[tt][rlo:rhi, :]),
                    reads=(f"qp{tt}",), writes=("qb0",))
                op("act", lambda s, tt=tt, rlo=rlo, rhi=rhi: s.activation(
                    s_kb[0][:], s_kp2[tt][rlo:rhi, :], AF.Copy),
                    reads=(f"kp{tt}",), writes=("kb0",))
                op("act", lambda s, tt=tt, rlo=rlo, rhi=rhi: s.activation(
                    s_vb[0][:], s_vp[tt][rlo:rhi, :], AF.Copy),
                    reads=(f"vp{tt}",), writes=("vb0",))
                op("dve", lambda v, tt=tt, rlo=rlo, rhi=rhi: v.tensor_copy(
                    s_bch[:], s_beta[rlo:rhi, tt * 16:(tt + 1) * 16]),
                    reads=("beta",), writes=("bch",))
            elif piece in (1, 2):
                src2, srcn, dst, dn = (
                    (s_qb[0], "qb0", s_q_fm[p], f"q_fm{p}") if piece == 1 else
                    (s_kb[0], "kb0", s_k_fm[p], f"k_fm{p}"))
                for h in range(H):
                    op("pe", lambda t, src2=src2, h=h: t.transpose(
                        PB[0][0:64, h * 64:(h + 1) * 64],
                        src2[:, h * 64:(h + 1) * 64], s_ident_bf[0:64, 0:64]),
                        reads=(srcn, "ident_bf"), writes=("pb0",))
                op("dve", lambda v, dst=dst: v.tensor_copy(
                    dst[:], PB[0][0:64, :]), reads=("pb0",), writes=(dn,))
            elif piece in (3, 4):
                if piece == 3:
                    lset, lname, k16, k16n, tri, trin = (
                        s_L1[0], "L1_0", s_gnl_fm, "gnl_fm", s_tril, "tril")
                else:
                    lset, lname, k16, k16n, tri, trin = (
                        s_L3[p], f"L3_{p}", s_gneg_fm, "gneg_fm", s_triu1,
                        "triu1")
                for half in range(2):
                    op("pe", lambda t, i=i, half=half, k16=k16: t.matmul(
                        PS[2][0:64, 0:512], k16[0:16, i * 64:(i + 1) * 64],
                        s_erep[0:16, half * 512:(half + 1) * 512],
                        start=True, stop=False),
                        reads=(k16n, "erep"), writes=("ps2",))
                    op("pe", lambda t, half=half, tri=tri: t.matmul(
                        PS[2][0:64, 0:512], tri[:, 0:64],
                        s_nega[:, half * 512:(half + 1) * 512],
                        start=False, stop=True),
                        reads=(trin, "nega0", "negaA"), writes=("ps2",))
                    op("act", lambda s, half=half, lset=lset: s.activation(
                        lset[:, half * 512:(half + 1) * 512],
                        PS[2][0:64, 0:512], AF.Exp),
                        reads=("ps2",),
                        writes=((lname,) if lset is s_L1[0] else
                                (f"{lname}h{half}",)))

        def emit_pre2(i):
            p = i % 2
            op("dve", lambda v: v.tensor_tensor(
                s_R[0][:].rearrange("c (h w) -> c h w", w=128)[:, :, 0:64],
                s_vb[0][:].rearrange("c (h d) -> c h d", d=64),
                s_bch[:].rearrange("c (h o) -> c h o", o=1).broadcast_to(
                    [64, 16, 64]), ALU.mult),
                reads=("vb0", "bch"), writes=("R0",))
            op("dve", lambda v, i=i: v.tensor_tensor(
                s_R[0][:].rearrange("c (h w) -> c h w", w=128)[:, :, 64:128],
                s_kb[0][:].rearrange("c (h d) -> c h d", d=64),
                s_bg[i][:].rearrange("c (h o) -> c h o", o=1).broadcast_to(
                    [64, 16, 64]), ALU.mult),
                reads=("kb0", f"bg{i}"), writes=("R0",))
            for h in range(H):
                ps = PS[5 + h // 8]
                op("pe", lambda t, p=p, h=h, ps=ps: t.matmul(
                    ps[0:64, (h % 8) * 64:(h % 8) * 64 + 64],
                    s_k_fm[p][:, h * 64:(h + 1) * 64],
                    s_k_fm[p][:, h * 64:(h + 1) * 64],
                    start=True, stop=True),
                    reads=(f"k_fm{p}",), writes=(f"ps{5 + h // 8}",))
            for hh in range(2):
                op("dve", lambda v, hh=hh: v.tensor_tensor(
                    s_M[0][:, hh * 512:(hh + 1) * 512],
                    PS[5 + hh][0:64, 0:512],
                    s_L1[0][:, hh * 512:(hh + 1) * 512], ALU.mult),
                    reads=(f"ps{5 + hh}", "L1_0"),
                    writes=(f"M0q{2 * hh}", f"M0q{2 * hh + 1}"))
            for h in range(H):
                op("pe", lambda t, h=h: t.transpose(
                    PB[0][0:64, h * 64:(h + 1) * 64],
                    s_M[0][:, h * 64:(h + 1) * 64], s_ident_bf[0:64, 0:64]),
                    reads=(f"M0q{h // 4}", "ident_bf"), writes=("pb0",))
            op("act", lambda s: s.activation(
                s_MT[0][:], PB[0][0:64, :], AF.Copy),
                reads=("pb0",), writes=("MT0q0", "MT0q1", "MT0q2", "MT0q3"))
            for q in range(4):
                op("pool", lambda g, q=q: g.tensor_add(
                    s_Ah[0][:, q * 256:(q + 1) * 256],
                    s_MT[0][:, q * 256:(q + 1) * 256],
                    s_irep[:, q * 256:(q + 1) * 256]),
                    reads=(f"MT0q{q}", "irep"), writes=(f"Ah0q{q}",))
            op("dve", lambda v, i=i: v.tensor_tensor(
                s_wtm[:].rearrange("c (h d) -> c h d", d=64),
                s_qb[0][:].rearrange("c (h d) -> c h d", d=64),
                s_eg[i][:].rearrange("c (h o) -> c h o", o=1).broadcast_to(
                    [64, 16, 64]), ALU.mult),
                reads=("qb0", f"eg{i}"), writes=("wtm",))
            for h in range(H):
                op("pe", lambda t, h=h: t.transpose(
                    PB[0][0:64, h * 64:(h + 1) * 64],
                    s_wtm[:, h * 64:(h + 1) * 64], s_ident_bf[0:64, 0:64]),
                    reads=("wtm", "ident_bf"), writes=("pb0",))
            op("dve", lambda v, p=p: v.tensor_copy(s_qe_fm[p][:], PB[0][0:64, :]),
               reads=("pb0",), writes=(f"qe_fm{p}",))
            op("pool", lambda g, i=i, p=p: g.tensor_tensor(
                s_kpp[p][:].rearrange("c (h d) -> c h d", d=64),
                s_kb[0][:].rearrange("c (h d) -> c h d", d=64),
                s_el[i][:].rearrange("c (h o) -> c h o", o=1).broadcast_to(
                    [64, 16, 64]), ALU.mult),
                reads=("kb0", f"el{i}"), writes=(f"kpp{p}",))

        def emitA_iter(i, j):
            for grp in range(4):
                ps = PS[3 + grp % 2]
                pn = f"ps{3 + grp % 2}"
                xsrcn = ("R0" if j == 0 else f"X0g{grp}",)
                xsrc = s_R[0] if j == 0 else s_X[0]
                for hh in range(4):
                    h = grp * 4 + hh
                    op("pe", lambda t, h=h, hh=hh, xsrc=xsrc, ps=ps, grp=grp:
                       t.matmul(
                        ps[0:64, hh * 128:(hh + 1) * 128],
                        s_Ah[0][:, h * 64:(h + 1) * 64],
                        xsrc[:, h * 128:(h + 1) * 128],
                        start=True, stop=True),
                        reads=(f"Ah0q{grp}",) + xsrcn, writes=(pn,))
                if grp % 2 == 0:
                    op("dve", lambda v, grp=grp, ps=ps: v.tensor_copy(
                        s_X[0][:, grp * 512:(grp + 1) * 512], ps[0:64, 0:512]),
                        reads=(pn,), writes=(f"X0g{grp}",))
                else:
                    op("act", lambda s, grp=grp, ps=ps: s.activation(
                        s_X[0][:, grp * 512:(grp + 1) * 512], ps[0:64, 0:512],
                        AF.Copy), reads=(pn,), writes=(f"X0g{grp}",))
            if j == 5:
                return
            for h in range(H):
                ps = PS[5 + h // 8]
                op("pe", lambda t, h=h, ps=ps: t.matmul(
                    ps[0:64, (h % 8) * 64:(h % 8) * 64 + 64],
                    s_MT[0][:, h * 64:(h + 1) * 64],
                    s_M[0][:, h * 64:(h + 1) * 64],
                    start=True, stop=True),
                    reads=(f"MT0q{h // 4}", f"M0q{h // 4}"),
                    writes=(f"ps{5 + h // 8}",))
            for h in range(H):
                ps = PS[3 + h // 8]
                op("pe", lambda t, h=h, ps=ps: t.matmul(
                    ps[0:64, (h % 8) * 64:(h % 8) * 64 + 64],
                    s_M[0][:, h * 64:(h + 1) * 64],
                    s_MT[0][:, h * 64:(h + 1) * 64],
                    start=True, stop=True),
                    reads=(f"M0q{h // 4}", f"MT0q{h // 4}"),
                    writes=(f"ps{3 + h // 8}",))
            for q in range(4):
                op("act", lambda s, q=q: s.activation(
                    s_MT[0][:, q * 256:(q + 1) * 256],
                    PS[3 + q // 2][0:64, (q % 2) * 256:(q % 2) * 256 + 256],
                    AF.Copy),
                    reads=(f"ps{3 + q // 2}",), writes=(f"MT0q{q}",))
                op("pool", lambda g, q=q: g.tensor_add(
                    s_Ah[0][:, q * 256:(q + 1) * 256],
                    s_MT[0][:, q * 256:(q + 1) * 256],
                    s_irep[:, q * 256:(q + 1) * 256]),
                    reads=(f"MT0q{q}", "irep"), writes=(f"Ah0q{q}",))
                op("dve", lambda v, q=q: v.tensor_copy(
                    s_M[0][:, q * 256:(q + 1) * 256],
                    PS[5 + q // 2][0:64, (q % 2) * 256:(q % 2) * 256 + 256]),
                    reads=(f"ps{5 + q // 2}",), writes=(f"M0q{q}",))

        def emitA_post(i):
            p = i % 2
            for pi in range(2):
                op("dve", lambda v, p=p, pi=pi: v.tensor_copy(
                    s_u[p][:, pi * 512:(pi + 1) * 512].rearrange(
                        "c (h d) -> c h d", d=64),
                    s_X[0][:, pi * 1024:(pi + 1) * 1024].rearrange(
                        "c (h w) -> c h w", w=128)[:, :, 0:64]),
                    reads=(f"X0g{2 * pi}", f"X0g{2 * pi + 1}"),
                    writes=(f"u{p}h{pi}",))
            op("dve", lambda v: v.tensor_copy(
                s_wtm[:].rearrange("c (h d) -> c h d", d=64),
                s_X[0][:].rearrange("c (h w) -> c h w", w=128)[:, :, 64:128]),
                reads=("X0g0", "X0g1", "X0g2", "X0g3"), writes=("wtm",))
            for h in range(H):
                op("pe", lambda t, h=h: t.transpose(
                    PB[0][0:64, h * 64:(h + 1) * 64],
                    s_wtm[:, h * 64:(h + 1) * 64], s_ident_bf[0:64, 0:64]),
                    reads=("wtm", "ident_bf"), writes=("pb0",))
            op("dve", lambda v, p=p: v.tensor_copy(s_wT[p][:], PB[0][0:64, :]),
               reads=("pb0",), writes=(f"wT{p}",))

        def emitB(i, piece):
            p = i % 2
            tt, half2 = divmod(i, 2)
            if piece == 0:
                for h in range(H):
                    ps = PS[h // 8]
                    op("pe", lambda t, p=p, h=h, ps=ps: t.matmul(
                        ps[0:64, (h % 8) * 64:(h % 8) * 64 + 64],
                        s_k_fm[p][:, h * 64:(h + 1) * 64],
                        s_q_fm[p][:, h * 64:(h + 1) * 64],
                        start=True, stop=True),
                        reads=(f"k_fm{p}", f"q_fm{p}"), writes=(f"ps{h // 8}",))
                for hh in range(2):
                    op("dve", lambda v, p=p, hh=hh: v.tensor_tensor(
                        s_aT[:, hh * 512:(hh + 1) * 512],
                        PS[hh][0:64, 0:512],
                        s_L3[p][:, hh * 512:(hh + 1) * 512], ALU.mult),
                        reads=(f"ps{hh}", f"L3_{p}h{hh}"), writes=(f"aTh{hh}",))
            elif piece == 1:
                for h in range(H):
                    ps = PS[h // 8]
                    op("pe", lambda t, p=p, h=h, ps=ps: t.matmul(
                        ps[0:64, (h % 8) * 64:(h % 8) * 64 + 64],
                        s_wT[p][:, h * 64:(h + 1) * 64],
                        s_Sb[:, h * 64:(h + 1) * 64],
                        start=True, stop=True),
                        reads=(f"wT{p}", "Sb"), writes=(f"ps{h // 8}",))
                for pi in range(2):
                    op("dve", lambda v, p=p, pi=pi: v.tensor_sub(
                        s_vn[:, pi * 512:(pi + 1) * 512],
                        s_u[p][:, pi * 512:(pi + 1) * 512],
                        PS[pi][0:64, 0:512]),
                        reads=(f"u{p}h{pi}", f"ps{pi}"), writes=(f"vnh{pi}",))
            elif piece == 2:
                for h in range(H):
                    ps = PS[h // 8]
                    op("pe", lambda t, p=p, h=h, ps=ps: t.matmul(
                        ps[0:64, (h % 8) * 64:(h % 8) * 64 + 64],
                        s_qe_fm[p][:, h * 64:(h + 1) * 64],
                        s_Sb[:, h * 64:(h + 1) * 64],
                        start=True, stop=False),
                        reads=(f"qe_fm{p}", "Sb"), writes=(f"ps{h // 8}",))
                    op("pe", lambda t, h=h, ps=ps: t.matmul(
                        ps[0:64, (h % 8) * 64:(h % 8) * 64 + 64],
                        s_aT[:, h * 64:(h + 1) * 64],
                        s_vn[:, h * 64:(h + 1) * 64],
                        start=False, stop=True),
                        reads=(f"aTh{h // 8}", f"vnh{h // 8}"),
                        writes=(f"ps{h // 8}",))
                for pi in range(2):
                    op("act", lambda s, tt=tt, half2=half2, pi=pi: s.activation(
                        s_o[tt][half2 * 64:(half2 + 1) * 64,
                                pi * 512:(pi + 1) * 512],
                        PS[pi][0:64, 0:512], AF.Copy),
                        reads=(f"ps{pi}",), writes=(f"o{tt}",))
            elif piece == 3:
                for h in range(H):
                    ps = PS[h // 8]
                    op("pe", lambda t, p=p, h=h, ps=ps: t.matmul(
                        ps[0:64, (h % 8) * 64:(h % 8) * 64 + 64],
                        s_kpp[p][:, h * 64:(h + 1) * 64],
                        s_vn[:, h * 64:(h + 1) * 64],
                        start=True, stop=True),
                        reads=(f"kpp{p}", f"vnh{h // 8}"), writes=(f"ps{h // 8}",))
            else:
                op("pool", lambda g, i=i: g.tensor_tensor(
                    s_S[:].rearrange("c (h d) -> c h d", d=64),
                    s_S[:].rearrange("c (h d) -> c h d", d=64),
                    s_egl[i][:].rearrange("c (h o) -> c h o", o=1).broadcast_to(
                        [64, 16, 64]), ALU.mult),
                    reads=("S", f"egl{i}"), writes=("S",))
                for hh in range(2):
                    op("dve", lambda v, hh=hh: v.tensor_add(
                        s_S[:, hh * 512:(hh + 1) * 512],
                        s_S[:, hh * 512:(hh + 1) * 512],
                        PS[hh][0:64, 0:512]),
                        reads=("S", f"ps{hh}"), writes=("S",))
                op("dve", lambda v: v.tensor_copy(s_Sb[:], s_S[:]),
                   reads=("S",), writes=("Sb",))

        for pc in range(5):
            emit_pre1(0, pc)
        emit_pre2(0)
        for i in range(NCH):
            for j in range(6):
                emitA_iter(i, j)
                if j < 5 and i + 1 < NCH:
                    emit_pre1(i + 1, j)
                if j < 5 and i >= 1:
                    emitB(i - 1, j)
            emitA_post(i)
            if i + 1 < NCH:
                emit_pre2(i + 1)
        for j in range(5):
            emitB(NCH - 1, j)

        def emit_oepi(tt):
            op("act", lambda s, tt=tt: s.activation(
                s_gf[:], s_o[tt][:], AF.Square), reads=(f"o{tt}",), writes=("gf",))
            op("dve", lambda v: v.reduce_sum(
                s_stat[:, 0:8],
                s_gf[:, 0:512].rearrange("p (h d) -> p h d", d=64), axis=AX.X),
                reads=("gf",), writes=("stat",))
            op("dve", lambda v: v.reduce_sum(
                s_m12[:, 0:8],
                s_gf[:, 512:1024].rearrange("p (h d) -> p h d", d=64), axis=AX.X),
                reads=("gf",), writes=("m12",))
            for half in range(2):
                src = s_stat[:, 0:8] if half == 0 else s_m12[:, 0:8]
                sn = "stat" if half == 0 else "m12"
                op("dve", lambda v, src=src: v.tensor_scalar(
                    src, src, 1.0 / 64, 1e-5, ALU.mult, ALU.add),
                    reads=(sn,), writes=(sn,))
                op("act", lambda s, src=src: s.activation(
                    src, src, AF.Ln), reads=(sn,), writes=(sn,))
                op("dve", lambda v, src=src: v.tensor_scalar(
                    src, src, -0.5, None, ALU.mult), reads=(sn,), writes=(sn,))
                op("act", lambda s, src=src: s.activation(
                    src, src, AF.Exp), reads=(sn,), writes=(sn,))
            op("act", lambda s, tt=tt: s.activation(
                s_gf[:], s_gate[tt][:], AF.Silu), reads=(f"gate{tt}",),
                writes=("gf",))
            for half in range(2):
                src = s_stat[:, 0:8] if half == 0 else s_m12[:, 0:8]
                sn = "stat" if half == 0 else "m12"
                op("dve", lambda v, tt=tt, half=half, src=src: v.tensor_tensor(
                    s_sout[tt][:, half * 512:(half + 1) * 512].rearrange(
                        "p (h d) -> p h d", d=64),
                    s_o[tt][:, half * 512:(half + 1) * 512].rearrange(
                        "p (h d) -> p h d", d=64),
                    src.rearrange("p (h o) -> p h o", o=1).broadcast_to(
                        [128, 8, 64]), ALU.mult),
                    reads=(f"o{tt}", sn), writes=(f"sout{tt}",))
            op("dve", lambda v, tt=tt: v.tensor_mul(
                s_sout[tt][:], s_sout[tt][:], s_nwrep[:]),
                reads=(f"sout{tt}", "nwrep"), writes=(f"sout{tt}",))
            op("dve", lambda v, tt=tt: v.tensor_mul(
                s_sout[tt][:], s_sout[tt][:], s_gf[:]),
                reads=(f"sout{tt}", "gf"), writes=(f"sout{tt}",))

        def emit_wo(tt):
            pbase = (0, 5, 3)[tt % 3]

            def tpose(k):
                pr = k % 2
                op("pe", lambda t, k=k, tt=tt: t.transpose(
                    PB[0][:, 0:128], s_sout[tt][:, k * 128:(k + 1) * 128],
                    s_ident_bf[:]), reads=(f"sout{tt}", "ident_bf"),
                    writes=("pb0",))
                op("dve", lambda v, pr=pr: v.tensor_copy(
                    s_ofm[pr][:], PB[0][:, 0:128]), reads=("pb0",),
                    writes=(f"ofm{pr}",))

            tpose(0)
            tpose(1)
            for k in range(NKC):
                pr = k % 2
                for half in range(2):
                    sl, sln = slab(seg, 24 + half)
                    op("pe", lambda t, k=k, pr=pr, half=half, sl=sl,
                       pbase=pbase: t.matmul(
                        PS[pbase + half][:, 0:512], s_ofm[pr][:],
                        sl[:, k * 512:(k + 1) * 512],
                        start=(k == 0), stop=(k == NKC - 1)),
                        reads=(f"ofm{pr}", sln), writes=(f"ps{pbase + half}",))
                if k + 2 < NKC:
                    tpose(k + 2)
            for half in range(2):
                op("act", lambda s, tt=tt, half=half, pbase=pbase: s.activation(
                    s_sout[tt][:, half * 512:(half + 1) * 512],
                    PS[pbase + half][:, 0:512], AF.Copy),
                    reads=(f"ps{pbase + half}",), writes=(f"sout{tt}",))

        def emit_combine(tt):
            op("act", lambda s, tt=tt: s.activation(
                s_gf[:], s_acc[tt][:], AF.Silu),
                reads=(f"acc{tt}",), writes=("gf",))
            op("pool", lambda g, tt=tt: g.tensor_mul(
                s_og[:], s_sout[tt][:], s_gf[:]),
                reads=(f"sout{tt}", "gf"), writes=("og",))
            dma("sp_c", s_acc[tt][:],
                x_ap[t0 + tt * 128:t0 + (tt + 1) * 128, :],
                writes=(f"acc{tt}",))
            op("dve", lambda v, tt=tt: v.tensor_add(
                s_xc[:], s_acc[tt][:], s_og[:]),
                reads=(f"acc{tt}", "og"), writes=("xc",))
            dma("actq", out_ap[t0 + tt * 128:t0 + (tt + 1) * 128, :],
                s_xc[:], reads=("xc",))

        issue_slabs(seg * 26 + 11)
        for tt in range(NTT):
            for k in range(NKC):
                op("pe", lambda t, k=k, tt=tt: t.matmul(
                    PS[4][:, 0:8], s_xn_fm[k][:, tt * 128:(tt + 1) * 128],
                    s_wgate[:, k * 8:(k + 1) * 8], start=(k == 0), stop=False),
                    reads=(f"xnfm{k}", "wgate"), writes=("ps4",))
            op("pe", lambda t: t.matmul(
                PS[4][:, 0:8], s_ones1[0:1, :], s_bgate[0:1, :],
                start=False, stop=True),
                reads=("ones1", "bgate"), writes=("ps4",))
            op("dve", lambda v: v.tensor_copy(s_logit[:], PS[4][:, 0:8]),
               reads=("ps4",), writes=("logit",))
            op("dve", lambda v: v.reduce_max(s_m12[:, 0:1], s_logit[:], axis=AX.X),
               reads=("logit",), writes=("m12",))
            op("dve", lambda v: v.tensor_scalar(
                s_m12[:, 2:3], s_m12[:, 0:1], -1.0, None, ALU.mult),
                reads=("m12",), writes=("m12",))
            op("dve", lambda v: v.tensor_scalar(
                s_stat[:, 0:8], s_logit[:], s_m12[:, 0:1], None, ALU.is_ge),
                reads=("logit", "m12"), writes=("stat",))
            op("dve", lambda v: v.scalar_tensor_tensor(
                s_stat[:, 0:8], s_stat[:, 0:8], NEG, s_logit[:],
                ALU.mult, ALU.add), reads=("stat", "logit"), writes=("stat",))
            op("dve", lambda v: v.reduce_max(s_m12[:, 1:2], s_stat[:, 0:8],
                                             axis=AX.X),
               reads=("stat",), writes=("m12",))
            op("act", lambda s, tt=tt: s.activation(
                s_score[tt][:], s_logit[:], AF.Exp, bias=s_m12[:, 2:3]),
                reads=("logit", "m12"), writes=(f"score{tt}",))
            op("dve", lambda v: v.tensor_scalar(
                s_stat[:, 0:8], s_logit[:], s_m12[:, 1:2], None, ALU.is_ge),
                reads=("logit", "m12"), writes=("stat",))
            op("dve", lambda v, tt=tt: v.tensor_tensor(
                s_score[tt][:], s_score[tt][:], s_stat[:, 0:8], ALU.mult),
                reads=(f"score{tt}", "stat"), writes=(f"score{tt}",))
            op("dve", lambda v, tt=tt: v.reduce_sum(
                s_m12[:, 3:4], s_score[tt][:], axis=AX.X),
                reads=(f"score{tt}",), writes=("m12",))
            op("dve", lambda v: v.reciprocal(s_m12[:, 4:5], s_m12[:, 3:4]),
               reads=("m12",), writes=("m12",))
            op("dve", lambda v, tt=tt: v.tensor_scalar(
                s_score[tt][:], s_score[tt][:], s_m12[:, 4:5], None, ALU.mult),
                reads=(f"score{tt}", "m12"), writes=(f"score{tt}",))
        for ei in range(8):
            issue_slabs(seg * 26 + 11 + 2 * ei)
            if ei == 0:
                for tt in range(NTT):
                    op("dve", lambda v, tt=tt: v.tensor_copy(
                        s_etmp[:, 0:8], s_score[tt][:]),
                        reads=(f"score{tt}",), writes=("etmp",))
                    op("pe", lambda t: t.transpose(
                        PB[0][0:8, 0:128], s_etmp[:, 0:8], s_ident_bf[:]),
                        reads=("etmp", "ident_bf"), writes=("pb0",))
                    op("dve", lambda v, tt=tt: v.tensor_copy(
                        s_scT[tt][:], PB[0][0:8, 0:128]), reads=("pb0",),
                        writes=(f"scT{tt}",))
            for half in range(2):
                sl, sln = slab(seg, 8 + 2 * ei + half)
                for tt in range(NTT):
                    pbase = (0, 5, 3)[tt % 3]
                    for k in range(NKC):
                        op("pe", lambda t, k=k, tt=tt, half=half, sl=sl,
                           pbase=pbase: t.matmul(
                            PS[pbase + half][:, 0:512],
                            s_xn_fm[k][:, tt * 128:(tt + 1) * 128],
                            sl[:, k * 512:(k + 1) * 512],
                            start=(k == 0), stop=(k == NKC - 1)),
                            reads=(f"xnfm{k}", sln), writes=(f"ps{pbase + half}",))
                    if ei == 0:
                        op("pe", lambda t, tt=tt, half=half: t.matmul(
                            PS[2][:, 0:512], s_scT[tt][:],
                            s_bexp[:, half * 512:(half + 1) * 512],
                            start=True, stop=True),
                            reads=(f"scT{tt}", "bexp"), writes=("ps2",))
                        op("dve", lambda v, tt=tt, half=half: v.tensor_copy(
                            s_acc[tt][:, half * 512:(half + 1) * 512],
                            PS[2][:, 0:512]), reads=("ps2",),
                            writes=(f"acc{tt}",))
                    op("act", lambda s, tt=tt, half=half, ei=ei, pbase=pbase:
                       s.activation(
                        s_etmp[:, half * 512:(half + 1) * 512],
                        PS[pbase + half][:, 0:512], AF.Copy,
                        scale=s_score[tt][:, ei:ei + 1]),
                        reads=(f"ps{pbase + half}", f"score{tt}"), writes=("etmp",))
                    op("dve", lambda v, tt=tt, half=half: v.tensor_add(
                        s_acc[tt][:, half * 512:(half + 1) * 512],
                        s_acc[tt][:, half * 512:(half + 1) * 512],
                        s_etmp[:, half * 512:(half + 1) * 512]),
                        reads=(f"acc{tt}", "etmp"), writes=(f"acc{tt}",))
            if ei < NTT:
                emit_oepi(ei)

        issue_slabs(seg * 26 + 26)
        for tt in range(NTT):
            emit_wo(tt)
            if tt >= 1:
                emit_combine(tt - 1)
        emit_combine(NTT - 1)

    ctx.finish()
    block = e(nc.Block())
    ctx.finalize(block)
    stack.close()
    return nc


_NC_CACHE = {}


def kernel(L0_attn_x, **kw):
    inp = {"L0_attn_x": L0_attn_x, **kw}
    T = L0_attn_x.shape[1]
    if T not in _NC_CACHE:
        _NC_CACHE[T] = build_program(T)
    nc = _NC_CACHE[T]
    lay = layout_inputs(inp)
    consts = host_constants()
    base = {**lay, **consts}
    x = np.ascontiguousarray(np.asarray(L0_attn_x, np.float32))
    in_maps = []
    for b in range(x.shape[0]):
        m = dict(base)
        m["x"] = x[b]
        in_maps.append(m)
    res = run_bass_kernel_spmd(nc, in_maps, core_ids=list(range(x.shape[0])))
    return np.stack([r["out"] for r in res.results], axis=0).astype(np.float32)


# version 25
# speedup vs baseline: 1.0125x; 1.0035x over previous
import numpy as np
import sys

sys.path.insert(0, "/opt/trn_rl_repo")
import ml_dtypes
import concourse.bass as bass
import concourse.mybir as mybir
from concourse.bass_utils import run_bass_kernel_spmd
from contextlib import ExitStack

F32 = mybir.dt.float32
BF16 = mybir.dt.bfloat16
AF = mybir.ActivationFunctionType
ALU = mybir.AluOpType
AX = mybir.AxisListType

D = 1024
H = 16
NKC = 8
NEG = -1e9
LN8 = float(np.log(0.125))


class Ctx:

    DMAQ = {"sp_c": "sp", "sp_x": "sp", "sp_f": "sp", "actq": "act",
            "poolq": "pool", "pq0": "pool", "pq1": "pool", "pq2": "pool",
            "pq3": "pool", "pq4": "pool",
            "sq0": "sp", "sq1": "sp", "sq2": "sp"}

    def __init__(self, nc, stack):
        self.nc = nc
        e = stack.enter_context
        names = ("pe", "dve", "act", "pool", "sp") + tuple(self.DMAQ)
        self.sems = {n: e(nc.semaphore(f"sem_{n}")) for n in names}
        self.count = {n: 0 for n in names}
        self.waited = {n: {} for n in ("pe", "dve", "act", "pool", "sp")}
        self.queues = {n: [] for n in ("pe", "dve", "act", "pool", "sp")}
        self.bufs = {}

    def _buf(self, name):
        return self.bufs.setdefault(name, {"w": None, "r": {}})

    def _wait(self, xeng, sem_name, val):
        if xeng == "pe" and sem_name == "pe":
            return
        hw = self.waited[xeng].get(sem_name, 0)
        if val <= hw:
            return
        self.waited[xeng][sem_name] = val
        sem = self.sems[sem_name]
        self.queues[xeng].append(lambda eng, s=sem, v=val: eng.wait_ge(s, v))

    def _dma_adj(self, sname, v):
        if sname in self.DMAQ:
            return self.count[sname]
        return v

    def _hazards(self, xeng, reads, writes):
        for b in reads:
            rec = self._buf(b)
            if rec["w"]:
                sname, v = rec["w"]
                self._wait(xeng, sname, self._dma_adj(sname, v))
        for b in writes:
            rec = self._buf(b)
            if rec["w"]:
                sname, v = rec["w"]
                self._wait(xeng, sname, self._dma_adj(sname, v))
            for sname, v in rec["r"].items():
                self._wait(xeng, sname, self._dma_adj(sname, v))

    def _record(self, key, reads, writes, val):
        for b in reads:
            self._buf(b)["r"][key] = max(self._buf(b)["r"].get(key, 0), val)
        for b in writes:
            rec = self._buf(b)
            rec["w"] = (key, val)
            rec["r"] = {}

    def op(self, eng, fn, reads=(), writes=()):
        self._hazards(eng, reads, writes)
        sem = self.sems[eng]
        self.count[eng] += 1
        val = self.count[eng]

        def emit(engine, fn=fn, sem=sem):
            fn(engine).then_inc(sem, 1)
        self.queues[eng].append(emit)
        self._record(eng, reads, writes, val)
        return val

    def dma(self, q, out_ap, in_ap, reads=(), writes=(), **kw):
        xeng = self.DMAQ[q]
        self._hazards(xeng, reads, writes)
        sem = self.sems[q]
        self.count[q] += 16
        val = self.count[q]

        def emit(engine, out_ap=out_ap, in_ap=in_ap, sem=sem, kw=kw):
            engine.dma_start(out_ap, in_ap, **kw).then_inc(sem, 16)
        self.queues[xeng].append(emit)
        self._record(q, reads, writes, val)
        return val

    def finish(self):
        for q in self.DMAQ:
            sem, v = self.sems[q], self.count[q]
            if v:
                self.queues["sp"].append(
                    lambda eng, s=sem, v=v: eng.wait_ge(s, v))

    def finalize(self, block):
        qs = self.queues

        @block.tensor
        def _(eng):
            for f in qs["pe"]:
                f(eng)

        @block.vector
        def _(eng):
            for f in qs["dve"]:
                f(eng)

        @block.scalar
        def _(eng):
            for f in qs["act"]:
                f(eng)

        @block.gpsimd
        def _(eng):
            for f in qs["pool"]:
                f(eng)

        @block.sync
        def _(eng):
            for f in qs["sp"]:
                f(eng)


def host_constants():
    c = {}
    c["c_ident_bf"] = np.eye(128, dtype=ml_dtypes.bfloat16)
    c["c_ident_f32"] = np.eye(128, dtype=np.float32)
    c["c_ones1"] = np.ones((1, 128), dtype=np.float32)
    tri = np.triu(np.ones((64, 64), np.float32))
    blk = np.zeros((128, 128), np.float32)
    blk[:64, :64] = tri
    blk[64:, 64:] = tri
    c["c_cumsum"] = blk.astype(ml_dtypes.bfloat16)
    er = np.zeros((16, 1024), np.float32)
    for h in range(16):
        er[h, h * 64:(h + 1) * 64] = 1.0
    c["c_erep"] = er
    c["c_erep_bf"] = er.astype(ml_dtypes.bfloat16)
    def aug(mat, row):
        m = np.zeros((67, 64), np.float32)
        m[0:64] = mat
        m[row] = 1.0
        return m
    c["c_tril"] = aug(np.tril(np.ones((64, 64), np.float32)), 64)
    c["c_triu"] = aug(np.triu(np.ones((64, 64), np.float32)), 65)
    c["c_triu1"] = aug(np.triu(np.ones((64, 64), np.float32), 1), 66)
    negI = np.zeros((64, 1024), np.float32)
    irep = np.zeros((64, 1024), np.float32)
    for h in range(16):
        negI[:, h * 64:(h + 1) * 64] = NEG * np.eye(64, dtype=np.float32)
        irep[:, h * 64:(h + 1) * 64] = np.eye(64)
    c["c_negI"] = negI
    c["c_irep_bf"] = irep.astype(ml_dtypes.bfloat16)
    return c


def layout_inputs(inp):
    d = {}
    d["w_qkvg"] = np.ascontiguousarray(np.concatenate(
        [inp["L0_attn_Wq"], inp["L0_attn_Wk"], inp["L0_attn_Wv"],
         inp["L0_attn_Wg"]], axis=1), dtype=np.float32)
    d["w_o"] = np.ascontiguousarray(inp["L0_attn_Wo"], dtype=np.float32)
    d["w_ab"] = np.ascontiguousarray(np.concatenate(
        [inp["L0_attn_Wb"], inp["L0_attn_Wa"]], axis=1), dtype=np.float32)
    d["dtb_row"] = np.concatenate(
        [np.zeros(16, np.float32),
         np.asarray(inp["L0_attn_dt_bias"], np.float32)]).reshape(1, 32)
    d["a_log"] = np.asarray(inp["L0_attn_A_log"], np.float32).reshape(16, 1)
    d["nw_rep"] = np.ascontiguousarray(np.broadcast_to(
        np.tile(np.asarray(inp["L0_attn_norm_w"], np.float32), 16),
        (128, 1024)))
    d["w_gate"] = np.ascontiguousarray(
        np.asarray(inp["L0_ffn_gate_w"], np.float32).T)
    d["b_gate"] = np.asarray(inp["L0_ffn_gate_b"], np.float32).reshape(1, 8)
    d["w_exp"] = np.ascontiguousarray(inp["L0_ffn_expert_w"], dtype=np.float32)
    d["b_exp"] = np.ascontiguousarray(inp["L0_ffn_expert_b"], dtype=np.float32)
    return d


def build_program(T):
    SEG = min(512, T)
    NSEG = T // SEG
    NTT = SEG // 128
    NCH = SEG // 64
    nc = bass.Bass("TRN2", target_bir_lowering=False)

    x_ap = nc.dram_tensor("x", [T, D], F32, kind="ExternalInput").ap()
    out_ap = nc.dram_tensor("out", [T, D], F32, kind="ExternalOutput").ap()
    wqkvg_ap = nc.dram_tensor("w_qkvg", [D, 4096], F32, kind="ExternalInput").ap()
    wo_ap = nc.dram_tensor("w_o", [D, D], F32, kind="ExternalInput").ap()
    wab_ap = nc.dram_tensor("w_ab", [D, 32], F32, kind="ExternalInput").ap()
    dtb_ap = nc.dram_tensor("dtb_row", [1, 32], F32, kind="ExternalInput").ap()
    alog_ap = nc.dram_tensor("a_log", [16, 1], F32, kind="ExternalInput").ap()
    nwrep_ap = nc.dram_tensor("nw_rep", [128, D], F32, kind="ExternalInput").ap()
    wgate_ap = nc.dram_tensor("w_gate", [D, 8], F32, kind="ExternalInput").ap()
    bgate_ap = nc.dram_tensor("b_gate", [1, 8], F32, kind="ExternalInput").ap()
    wexp_ap = nc.dram_tensor("w_exp", [8, D, D], F32, kind="ExternalInput").ap()
    bexp_ap = nc.dram_tensor("b_exp", [8, D], F32, kind="ExternalInput").ap()
    consts = host_constants()
    c_ap = {k: nc.dram_tensor(k, list(v.shape),
                              BF16 if v.dtype == ml_dtypes.bfloat16 else F32,
                              kind="ExternalInput").ap()
            for k, v in consts.items()}
    scr_ap = nc.dram_tensor("scr", [64, SEG], F32, kind="Internal").ap()

    stack = ExitStack()
    e = stack.enter_context
    ctx = Ctx(nc, stack)

    PS = [e(nc.psum_tensor(f"ps{i}", [128, 512], F32)) for i in range(7)]
    PB = [e(nc.psum_tensor(f"pb{i}", [128, 1024], BF16)) for i in range(1)]

    s_ident_bf = e(nc.sbuf_tensor("s_ident_bf", [128, 128], BF16))
    s_ident_f32 = e(nc.sbuf_tensor("s_ident_f32", [128, 128], F32))
    s_ones1 = e(nc.sbuf_tensor("s_ones1", [1, 128], F32))
    s_cumsum = e(nc.sbuf_tensor("s_cumsum", [128, 128], BF16))
    s_decb = e(nc.sbuf_tensor("s_decb", [128, 16], BF16))
    s_erep = e(nc.sbuf_tensor("s_erep", [16, 1024], F32))
    s_tril = e(nc.sbuf_tensor("s_tril", [67, 64], F32))
    s_triu = e(nc.sbuf_tensor("s_triu", [67, 64], F32))
    s_triu1 = e(nc.sbuf_tensor("s_triu1", [67, 64], F32))
    s_nega = e(nc.sbuf_tensor("s_nega", [67, 1024], F32))
    s_irep = e(nc.sbuf_tensor("s_irep", [64, 1024], BF16))
    s_nwrep = e(nc.sbuf_tensor("s_nwrep", [128, D], BF16))
    s_dtb = e(nc.sbuf_tensor("s_dtb", [1, 32], F32))
    s_alog = e(nc.sbuf_tensor("s_alog", [16, 1], F32))
    s_nAb = e(nc.sbuf_tensor("s_nAb", [128, 16], F32))
    s_wab = e(nc.sbuf_tensor("s_wab", [128, 256], BF16))
    s_wgate = e(nc.sbuf_tensor("s_wgate", [128, 64], BF16))
    s_bgate = e(nc.sbuf_tensor("s_bgate", [1, 8], F32))
    s_bexp = e(nc.sbuf_tensor("s_bexp", [8, D], BF16))
    s_tmprow = e(nc.sbuf_tensor("s_tmprow", [1, 16], F32))

    NSL = 3
    s_slab = [e(nc.sbuf_tensor(f"s_slab{j}", [128, 4096], BF16)) for j in range(NSL)]

    s_x = [e(nc.sbuf_tensor("s_x0", [128, D], F32))] * 2
    s_xc = e(nc.sbuf_tensor("s_xc", [128, D], F32))
    s_stat = e(nc.sbuf_tensor("s_stat", [128, 8], F32))
    s_xn_tm = e(nc.sbuf_tensor("s_xn_tm", [128, D], BF16))
    s_xn_fm = [e(nc.sbuf_tensor(f"s_xn_fm{k}", [128, SEG], BF16)) for k in range(NKC)]

    s_ab = e(nc.sbuf_tensor("s_ab", [128, NTT * 32], F32))
    s_beta = e(nc.sbuf_tensor("s_beta", [128, NTT * 16], F32))
    s_sp = e(nc.sbuf_tensor("s_sp", [128, NTT * 16], F32))
    s_dec = e(nc.sbuf_tensor("s_dec", [128, NTT * 16], F32))
    s_g_tm = e(nc.sbuf_tensor("s_g_tm", [128, NTT * 16], F32))
    s_lnb_tm = e(nc.sbuf_tensor("s_lnb_tm", [128, NTT * 16], F32))
    s_g_fm = e(nc.sbuf_tensor("s_g_fm", [16, SEG], F32))
    s_gnl_fm = e(nc.sbuf_tensor("s_gnl_fm", [16, SEG], F32))
    s_gneg_fm = e(nc.sbuf_tensor("s_gneg_fm", [16, SEG], F32))
    s_lnb_fm = e(nc.sbuf_tensor("s_lnb_fm", [16, SEG], F32))
    s_glast = e(nc.sbuf_tensor("s_glast", [1, max(NCH * 16, 16)], F32))

    def chunk_set(name, dt=F32, w=16):
        return [e(nc.sbuf_tensor(f"{name}{i}", [64, w], dt)) for i in range(NCH)]

    s_qp = [e(nc.sbuf_tensor(f"s_qp{t}", [128, D], BF16)) for t in range(NTT)]
    s_kp2 = [e(nc.sbuf_tensor(f"s_kp2{t}", [128, D], BF16)) for t in range(NTT)]
    s_vp = [e(nc.sbuf_tensor(f"s_vp{t}", [128, D], BF16)) for t in range(NTT)]
    s_eg = chunk_set("s_eg")
    s_bg = chunk_set("s_bg")
    s_el = chunk_set("s_el")
    s_egl = chunk_set("s_egl")

    def par(name, w=H * 64, dt=BF16):
        t = e(nc.sbuf_tensor(f"{name}0", [64, w], dt))
        return [t, t]

    def par2(name, w=H * 64, dt=BF16):
        return [e(nc.sbuf_tensor(f"{name}{p}", [64, w], dt)) for p in range(2)]

    s_vb = par("s_vb")
    s_bch = e(nc.sbuf_tensor("s_bch", [64, 16], F32))
    s_gch = e(nc.sbuf_tensor("s_gch", [64, 16], F32))
    s_qb = par("s_qb")
    s_kb = par("s_kb")
    s_q_fm = par2("s_q_fm")
    s_k_fm = par2("s_k_fm")
    s_u = par2("s_u")
    s_wT = par2("s_wT")
    s_L1 = par("s_L1_")
    s_L3 = par2("s_L3_")
    s_R = par("s_R", w=H * 128)
    s_M = par("s_M")
    s_MT = par("s_MT")
    s_Ah = par("s_Ah")
    s_X = par("s_X", w=H * 128)

    s_S = e(nc.sbuf_tensor("s_S", [64, H * 64], F32))
    s_Sb = e(nc.sbuf_tensor("s_Sb", [64, H * 64], BF16))
    s_aT = e(nc.sbuf_tensor("s_aT", [64, H * 64], BF16))
    s_wtm = e(nc.sbuf_tensor("s_wtm", [64, H * 64], BF16))
    s_vn = e(nc.sbuf_tensor("s_vn", [64, H * 64], BF16))
    s_kpp = [e(nc.sbuf_tensor(f"s_kpp{p}", [64, H * 64], BF16)) for p in range(2)]
    s_qe_tm = e(nc.sbuf_tensor("s_qe_tm", [64, H * 64], BF16))
    s_qe_fm = [e(nc.sbuf_tensor(f"s_qe_fm{p}", [64, H * 64], BF16)) for p in range(2)]

    s_o = [e(nc.sbuf_tensor(f"s_o{t}", [128, D], BF16)) for t in range(NTT)]
    s_gate = [e(nc.sbuf_tensor(f"s_gate{t}", [128, D], BF16)) for t in range(NTT)]
    s_og = e(nc.sbuf_tensor("s_og", [128, D], BF16))
    s_ofm = [e(nc.sbuf_tensor(f"s_ofm{p}", [128, 128], BF16)) for p in range(2)]
    s_sout = [e(nc.sbuf_tensor(f"s_sout{t}", [128, D], BF16)) for t in range(NTT)]
    s_logit = e(nc.sbuf_tensor("s_logit", [128, 8], F32))
    s_m12 = e(nc.sbuf_tensor("s_m12", [128, 8], F32))
    s_m13 = [e(nc.sbuf_tensor(f"s_m13_{j}", [128, 8], F32)) for j in range(4)]
    s_score = [e(nc.sbuf_tensor(f"s_score{t}", [128, 8], F32)) for t in range(NTT)]
    s_scT = [e(nc.sbuf_tensor(f"s_scT{t}", [8, 128], BF16)) for t in range(NTT)]
    s_acc = [e(nc.sbuf_tensor(f"s_acc{t}", [128, D], F32)) for t in range(NTT)]
    s_etmp = e(nc.sbuf_tensor("s_etmp", [128, D], BF16))
    s_gf = e(nc.sbuf_tensor("s_gf", [128, D], BF16))

    op, dma = ctx.op, ctx.dma

    for dst, src, name in (
            (s_ident_bf, c_ap["c_ident_bf"], "ident_bf"),
            (s_ident_f32, c_ap["c_ident_f32"], "ident_f32"),
            (s_ones1, c_ap["c_ones1"], "ones1"),
            (s_cumsum, c_ap["c_cumsum"], "cumsum"),
            (s_erep, c_ap["c_erep"], "erep"),
            (s_tril, c_ap["c_tril"], "tril"),
            (s_triu, c_ap["c_triu"], "triu"),
            (s_triu1, c_ap["c_triu1"], "triu1"),
            (s_irep, c_ap["c_irep_bf"], "irep"),
            (s_dtb, dtb_ap, "dtb"),
            (s_alog, alog_ap, "alog"),
            (s_bgate, bgate_ap, "bgate")):
        dma("sp_c", dst[:], src, writes=(name,))
    dma("sp_c", s_nega[0:64, :], c_ap["c_negI"], writes=("nega0",))
    dma("poolq", s_nwrep[:], nwrep_ap, writes=("nwrep",))
    dma("poolq", s_wab[:].rearrange("p (kc a) -> p kc a", a=32),
        wab_ap.rearrange("(kc p) a -> p kc a", p=128), writes=("wab",))
    dma("poolq", s_wgate[:].rearrange("p (kc a) -> p kc a", a=8),
        wgate_ap.rearrange("(kc p) a -> p kc a", p=128), writes=("wgate",))
    dma("poolq", s_bexp[:], bexp_ap, writes=("bexp",))

    op("act", lambda s: s.activation(s_alog[:], s_alog[:], AF.Exp),
       reads=("alog",), writes=("alog",))
    op("pe", lambda t: t.transpose(PS[4][0:1, 0:16], s_alog[:], s_ident_f32[0:16, 0:16]),
       reads=("alog", "ident_f32"), writes=("ps4",))
    op("dve", lambda v: v.tensor_copy(s_tmprow[:], PS[4][0:1, 0:16]),
       reads=("ps4",), writes=("tmprow",))
    op("pe", lambda t: t.matmul(PS[4][:, 0:16], s_ones1[0:1, :], s_tmprow[:],
                                start=True, stop=True),
       reads=("ones1", "tmprow"), writes=("ps4",))
    op("dve", lambda v: v.tensor_scalar(s_nAb[:], PS[4][:, 0:16], -1.0, None,
                                        ALU.mult),
       reads=("ps4",), writes=("nAb",))

    def slab_src(idx):
        if idx < 8:
            return wqkvg_ap[:, idx * 512:(idx + 1) * 512].rearrange(
                "(kc p) n -> p kc n", p=128)
        if idx < 24:
            ei, hh = divmod(idx - 8, 2)
            return wexp_ap[ei, :, hh * 512:(hh + 1) * 512].rearrange(
                "(kc p) n -> p kc n", p=128)
        h = idx - 24
        return wo_ap[:, h * 512:(h + 1) * 512].rearrange(
            "(kc p) n -> p kc n", p=128)

    scrw_ap = nc.dram_tensor("scr_w", [26, 128, 4096], BF16,
                             kind="Internal").ap()

    slab_issued = [0]

    def slab_name(gidx):
        return f"slab{gidx % NSL}"

    def issue_slabs(upto):
        while slab_issued[0] < upto:
            g = slab_issued[0]
            seg, idx = divmod(g, 26)
            if seg >= NSEG:
                break
            if seg == 0:
                dma(f"pq{g % NSL}", s_slab[g % NSL][:].rearrange(
                    "p (kc n) -> p kc n", n=512), slab_src(idx),
                    writes=(slab_name(g),))
                if NSEG > 1:
                    dma("pq4", scrw_ap[idx], s_slab[g % NSL][:],
                        reads=(slab_name(g),), writes=(f"scrw{idx}",))
            else:
                dma(f"sq{g % NSL}", s_slab[g % NSL][:], scrw_ap[idx],
                    reads=(f"scrw{idx}",), writes=(slab_name(g),))
            slab_issued[0] += 1

    def slab(seg, idx):
        g = seg * 26 + idx
        return s_slab[g % NSL], slab_name(g)

    issue_slabs(3)

    for seg in range(NSEG):
        t0 = seg * SEG
        for tt in range(NTT):
            xn = "x0"
            xt = s_x[0]
            mb = s_m13[tt % 4]
            mbn = f"m13_{tt % 4}"
            dma("sp_x", xt[:], x_ap[t0 + tt * 128:t0 + (tt + 1) * 128, :],
                writes=(xn,))
            op("dve", lambda v, xt=xt, mb=mb: v.reduce_sum(
                mb[:, 0:1], xt[:], axis=AX.X),
                reads=(xn,), writes=(mbn,))
            op("act", lambda s, xt=xt, mb=mb: s.activation(
                s_gf[:], xt[:], AF.Square, accum_out=mb[:, 2:3]),
                reads=(xn,), writes=("gf", mbn))
            op("dve", lambda v, mb=mb: v.tensor_scalar(
                mb[:, 1:2], mb[:, 0:1], 1.0 / D, None, ALU.mult),
                reads=(mbn,), writes=(mbn,))
            op("dve", lambda v, mb=mb: v.tensor_scalar(
                mb[:, 5:6], mb[:, 1:2], mb[:, 1:2], None, ALU.mult),
                reads=(mbn,), writes=(mbn,))
            op("dve", lambda v, mb=mb: v.tensor_scalar(
                mb[:, 3:4], mb[:, 2:3], 1.0 / D, 1e-5,
                ALU.mult, ALU.add), reads=(mbn,), writes=(mbn,))
            op("dve", lambda v, mb=mb: v.tensor_tensor(
                mb[:, 3:4], mb[:, 3:4], mb[:, 5:6], ALU.subtract),
                reads=(mbn,), writes=(mbn,))
            op("act", lambda s, mb=mb: s.activation(
                mb[:, 3:4], mb[:, 3:4], AF.Ln),
                reads=(mbn,), writes=(mbn,))
            op("dve", lambda v, mb=mb: v.tensor_scalar(
                mb[:, 4:5], mb[:, 3:4], -0.5, None, ALU.mult),
                reads=(mbn,), writes=(mbn,))
            op("act", lambda s, mb=mb: s.activation(
                mb[:, 4:5], mb[:, 4:5], AF.Exp),
                reads=(mbn,), writes=(mbn,))
            op("dve", lambda v, mb=mb: v.tensor_scalar(
                mb[:, 5:6], mb[:, 4:5], mb[:, 1:2], None, ALU.mult),
                reads=(mbn,), writes=(mbn,))
            op("dve", lambda v, xt=xt, mb=mb: v.tensor_scalar(
                s_xn_tm[:], xt[:], mb[:, 4:5], mb[:, 5:6],
                ALU.mult, ALU.subtract),
                reads=(xn, mbn), writes=("xn_tm",))
            for k in range(NKC):
                op("pe", lambda t, k=k: t.transpose(
                    PB[0][:, 0:128], s_xn_tm[:, k * 128:(k + 1) * 128],
                    s_ident_bf[:]),
                    reads=("xn_tm", "ident_bf"), writes=("pb0",))
                op("dve", lambda v, k=k, tt=tt: v.tensor_copy(
                    s_xn_fm[k][:, tt * 128:(tt + 1) * 128], PB[0][:, 0:128]),
                    reads=("pb0",), writes=(f"xnfm{k}",))
            for k in range(NKC):
                op("pe", lambda t, k=k, tt=tt: t.matmul(
                    PS[4][:, 0:32], s_xn_fm[k][:, tt * 128:(tt + 1) * 128],
                    s_wab[:, k * 32:(k + 1) * 32], start=(k == 0), stop=False),
                    reads=(f"xnfm{k}", "wab"), writes=("ps4",))
            op("pe", lambda t: t.matmul(
                PS[4][:, 0:32], s_ones1[0:1, :], s_dtb[0:1, :],
                start=False, stop=True),
                reads=("ones1", "dtb"), writes=("ps4",))
            op("dve", lambda v, tt=tt: v.tensor_copy(
                s_ab[:, tt * 32:(tt + 1) * 32], PS[4][:, 0:32]),
                reads=("ps4",), writes=("ab",))

        for tt in range(NTT):
            zb = s_ab[:, tt * 32:tt * 32 + 16]
            za = s_ab[:, tt * 32 + 16:tt * 32 + 32]
            bcol = s_beta[:, tt * 16:(tt + 1) * 16]
            spc = s_sp[:, tt * 16:(tt + 1) * 16]
            dcol = s_dec[:, tt * 16:(tt + 1) * 16]
            lcol = s_lnb_tm[:, tt * 16:(tt + 1) * 16]
            op("act", lambda s, zb=zb, bcol=bcol: s.activation(
                bcol, zb, AF.Sigmoid), reads=("ab",), writes=("beta",))
            op("act", lambda s, bcol=bcol, lcol=lcol: s.activation(
                lcol, bcol, AF.Ln), reads=("beta",), writes=("lnb_tm",))
            op("dve", lambda v, za=za, dcol=dcol: v.tensor_scalar(
                dcol, za, -1.0, None, ALU.mult), reads=("ab",), writes=("dec",))
            op("dve", lambda v, za=za, spc=spc, dcol=dcol: v.tensor_tensor(
                spc, za, dcol, ALU.min), reads=("ab", "dec"), writes=("spx",))
            op("act", lambda s, spc=spc: s.activation(
                spc, spc, AF.Exp), reads=("spx",), writes=("spx",))
            op("dve", lambda v, spc=spc: v.tensor_scalar(
                spc, spc, 1.0, None, ALU.add), reads=("spx",), writes=("spx",))
            op("act", lambda s, spc=spc: s.activation(
                spc, spc, AF.Ln), reads=("spx",), writes=("spx",))
            op("dve", lambda v, za=za, dcol=dcol: v.tensor_scalar(
                dcol, za, 0.0, None, ALU.max), reads=("ab",), writes=("dec",))
            op("dve", lambda v, spc=spc, dcol=dcol: v.tensor_tensor(
                dcol, dcol, spc, ALU.add), reads=("dec", "spx"), writes=("dec",))
            op("dve", lambda v, dcol=dcol: v.tensor_tensor(
                dcol, dcol, s_nAb[:, 0:16], ALU.mult),
                reads=("dec", "nAb"), writes=("dec",))
            op("dve", lambda v, tt=tt: v.tensor_copy(
                s_decb[:], s_dec[:, tt * 16:(tt + 1) * 16]),
                reads=("dec",), writes=("decb",))
            op("pe", lambda t: t.matmul(
                PS[4][:, 0:16], s_cumsum[:, 0:128],
                s_decb[:], start=True, stop=True),
                reads=("cumsum", "decb"), writes=("ps4",))
            op("dve", lambda v, tt=tt: v.tensor_copy(
                s_g_tm[:, tt * 16:(tt + 1) * 16], PS[4][:, 0:16]),
                reads=("ps4",), writes=("g_tm",))
            op("pe", lambda t, tt=tt: t.transpose(
                PS[4][0:16, 0:128], s_g_tm[:, tt * 16:(tt + 1) * 16],
                s_ident_f32[:]),
                reads=("g_tm", "ident_f32"), writes=("ps4",))
            op("dve", lambda v, tt=tt: v.tensor_copy(
                s_g_fm[:, tt * 128:(tt + 1) * 128], PS[4][0:16, 0:128]),
                reads=("ps4",), writes=("g_fm",))
            op("pe", lambda t, tt=tt: t.transpose(
                PS[4][0:16, 0:128], s_lnb_tm[:, tt * 16:(tt + 1) * 16],
                s_ident_f32[:]),
                reads=("lnb_tm", "ident_f32"), writes=("ps4",))
            op("dve", lambda v, tt=tt: v.tensor_copy(
                s_lnb_fm[:, tt * 128:(tt + 1) * 128], PS[4][0:16, 0:128]),
                reads=("ps4",), writes=("lnb_fm",))
        op("act", lambda s: s.activation(
            s_gneg_fm[:], s_g_fm[:], AF.Copy, scale=-1.0),
            reads=("g_fm",), writes=("gneg_fm",))
        op("pool", lambda g: g.tensor_add(s_gnl_fm[:], s_g_fm[:], s_lnb_fm[:]),
           reads=("g_fm", "lnb_fm"), writes=("gnl_fm",))
        dma("sp_f", scr_ap[0:16, :], s_gneg_fm[:], reads=("gneg_fm",),
            writes=("scr0",))
        dma("sp_f", scr_ap[16:32, :], s_g_fm[:], reads=("g_fm",),
            writes=("scr1",))
        dma("sp_f", scr_ap[32:48, :], s_gnl_fm[:], reads=("gnl_fm",),
            writes=("scr2",))

        dma("sp_f", s_glast[0:1, 0:NCH * 16].rearrange(
                "a (i h) -> a i h", h=16),
            scr_ap[16:32, 63::64].rearrange("h i -> () i h"),
            reads=("scr1",), writes=("glast",),
            allow_slow_non_contiguous=True)

        for i in range(NCH):
            tt, half = divmod(i, 2)
            grows = s_g_tm[half * 64:(half + 1) * 64, tt * 16:(tt + 1) * 16]
            brows = s_beta[half * 64:(half + 1) * 64, tt * 16:(tt + 1) * 16]
            op("dve", lambda v, grows=grows: v.tensor_copy(s_gch[:], grows),
               reads=("g_tm",), writes=("gch",))
            op("dve", lambda v, brows=brows: v.tensor_copy(s_bch[:], brows),
               reads=("beta",), writes=("bch",))
            op("act", lambda s, i=i: s.activation(
                s_eg[i][:], s_gch[:], AF.Exp), reads=("gch",), writes=(f"eg{i}",))
            op("pool", lambda g, i=i: g.tensor_mul(
                s_bg[i][:], s_eg[i][:], s_bch[:]),
                reads=(f"eg{i}", "bch"), writes=(f"bg{i}",))
            op("pe", lambda t, i=i: t.matmul(
                PS[4][0:64, 0:16], s_ones1[0:1, 0:64],
                s_glast[0:1, i * 16:(i + 1) * 16], start=True, stop=True),
                reads=("ones1", "glast"), writes=("ps4",))
            op("act", lambda s, i=i: s.activation(
                s_egl[i][:], PS[4][0:64, 0:16], AF.Exp),
                reads=("ps4",), writes=(f"egl{i}",))
            op("dve", lambda v, i=i: v.tensor_sub(
                s_el[i][:], PS[4][0:64, 0:16], s_gch[:]),
                reads=("ps4", "gch"), writes=(f"el{i}",))
            op("act", lambda s, i=i: s.activation(
                s_el[i][:], s_el[i][:], AF.Exp),
                reads=(f"el{i}",), writes=(f"el{i}",))

        for pp, pname in enumerate(("q", "k", "v", "gate")):
            issue_slabs(seg * 26 + 2 * pp + 3)
            for tt in range(NTT):
                pbase = 0 if tt % 2 == 0 else 5
                for half in range(2):
                    sl, sln = slab(seg, 2 * pp + half)
                    for k in range(NKC):
                        op("pe", lambda t, k=k, tt=tt, half=half, sl=sl,
                           pbase=pbase: t.matmul(
                            PS[pbase + half][:, 0:512],
                            s_xn_fm[k][:, tt * 128:(tt + 1) * 128],
                            sl[:, k * 512:(k + 1) * 512],
                            start=(k == 0), stop=(k == NKC - 1)),
                            reads=(f"xnfm{k}", sln), writes=(f"ps{pbase + half}",))
                if pp in (0, 1):
                    for half in range(2):
                        bi = 2 * (tt % 2) + half
                        mb = s_m13[bi]
                        mbn = f"m13_{bi}"
                        gsl = s_gf[:, half * 512:(half + 1) * 512]
                        op("act", lambda s, half=half, pbase=pbase, gsl=gsl:
                           s.activation(
                            gsl, PS[pbase + half][:, 0:512], AF.Square),
                            reads=(f"ps{pbase + half}",), writes=(f"gf{half}",))
                        op("dve", lambda v, mb=mb, gsl=gsl: v.reduce_sum(
                            mb[:, 0:8],
                            gsl.rearrange("p (h d) -> p h d", d=64), axis=AX.X),
                            reads=(f"gf{half}",), writes=(mbn,))
                        op("dve", lambda v, mb=mb: v.tensor_scalar(
                            mb[:, 0:8], mb[:, 0:8], 1e-6, None, ALU.add),
                            reads=(mbn,), writes=(mbn,))
                        op("act", lambda s, mb=mb: s.activation(
                            mb[:, 0:8], mb[:, 0:8], AF.Ln),
                            reads=(mbn,), writes=(mbn,))
                        op("dve", lambda v, pp=pp, mb=mb: v.tensor_scalar(
                            mb[:, 0:8], mb[:, 0:8], -0.5,
                            (LN8 if pp == 0 else 0.0), ALU.mult, ALU.add),
                            reads=(mbn,), writes=(mbn,))
                        op("act", lambda s, mb=mb: s.activation(
                            mb[:, 0:8], mb[:, 0:8], AF.Exp),
                            reads=(mbn,), writes=(mbn,))
                        dst = s_qp[tt] if pp == 0 else s_kp2[tt]
                        dn = f"{pname}p{tt}"
                        op("dve", lambda v, dst=dst, half=half, pbase=pbase,
                           mb=mb, mbn=mbn: v.tensor_tensor(
                            dst[:, half * 512:(half + 1) * 512].rearrange(
                                "p (h d) -> p h d", d=64),
                            PS[pbase + half][:, 0:512].rearrange(
                                "p (h d) -> p h d", d=64),
                            mb[:, 0:8].rearrange(
                                "p (h o) -> p h o", o=1).broadcast_to(
                                [128, 8, 64]), ALU.mult),
                            reads=(f"ps{pbase + half}", mbn), writes=(dn,))
                elif pp == 2:
                    for half in range(2):
                        op("dve", lambda v, tt=tt, half=half, pbase=pbase:
                           v.tensor_copy(
                            s_vp[tt][:, half * 512:(half + 1) * 512],
                            PS[pbase + half][:, 0:512]),
                            reads=(f"ps{pbase + half}",), writes=(f"vp{tt}",))
                else:
                    for half in range(2):
                        op("act", lambda s, tt=tt, half=half, pbase=pbase:
                           s.activation(
                            s_gate[tt][:, half * 512:(half + 1) * 512],
                            PS[pbase + half][:, 0:512], AF.Copy),
                            reads=(f"ps{pbase + half}",), writes=(f"gate{tt}",))

        if seg == 0:
            op("dve", lambda v: v.memset(s_S[:], 0.0), writes=("S",))
            op("pool", lambda g: g.memset(s_Sb[:], 0.0), writes=("Sb",))

        def emit_pre1(i, piece):
            p = i % 2
            tt, half2 = divmod(i, 2)
            rlo, rhi = half2 * 64, (half2 + 1) * 64
            if piece == 0:
                for r, arow in ((0, 64), (2, 65), (1, 66)):
                    dma("sp_f", s_nega[arow:arow + 1, :].rearrange(
                            "a (h c) -> a h c", c=64),
                        scr_ap[r * 16:(r + 1) * 16, i * 64:(i + 1) * 64],
                        reads=(f"scr{r}",), writes=("negaA",))
                op("dve", lambda v, tt=tt, rlo=rlo, rhi=rhi: v.tensor_copy(
                    s_qb[0][:], s_qp[tt][rlo:rhi, :]),
                    reads=(f"qp{tt}",), writes=("qb0",))
                op("act", lambda s, tt=tt, rlo=rlo, rhi=rhi: s.activation(
                    s_kb[0][:], s_kp2[tt][rlo:rhi, :], AF.Copy),
                    reads=(f"kp{tt}",), writes=("kb0",))
                op("act", lambda s, tt=tt, rlo=rlo, rhi=rhi: s.activation(
                    s_vb[0][:], s_vp[tt][rlo:rhi, :], AF.Copy),
                    reads=(f"vp{tt}",), writes=("vb0",))
                op("dve", lambda v, tt=tt, rlo=rlo, rhi=rhi: v.tensor_copy(
                    s_bch[:], s_beta[rlo:rhi, tt * 16:(tt + 1) * 16]),
                    reads=("beta",), writes=("bch",))
            elif piece in (1, 2):
                src2, srcn, dst, dn = (
                    (s_qb[0], "qb0", s_q_fm[p], f"q_fm{p}") if piece == 1 else
                    (s_kb[0], "kb0", s_k_fm[p], f"k_fm{p}"))
                for h in range(H):
                    op("pe", lambda t, src2=src2, h=h: t.transpose(
                        PB[0][0:64, h * 64:(h + 1) * 64],
                        src2[:, h * 64:(h + 1) * 64], s_ident_bf[0:64, 0:64]),
                        reads=(srcn, "ident_bf"), writes=("pb0",))
                op("dve", lambda v, dst=dst: v.tensor_copy(
                    dst[:], PB[0][0:64, :]), reads=("pb0",), writes=(dn,))
            elif piece in (3, 4):
                if piece == 3:
                    lset, lname, k16, k16n, tri, trin = (
                        s_L1[0], "L1_0", s_gnl_fm, "gnl_fm", s_tril, "tril")
                else:
                    lset, lname, k16, k16n, tri, trin = (
                        s_L3[p], f"L3_{p}", s_gneg_fm, "gneg_fm", s_triu1,
                        "triu1")
                for half in range(2):
                    op("pe", lambda t, i=i, half=half, k16=k16: t.matmul(
                        PS[2][0:64, 0:512], k16[0:16, i * 64:(i + 1) * 64],
                        s_erep[0:16, half * 512:(half + 1) * 512],
                        start=True, stop=False),
                        reads=(k16n, "erep"), writes=("ps2",))
                    op("pe", lambda t, half=half, tri=tri: t.matmul(
                        PS[2][0:64, 0:512], tri[:, 0:64],
                        s_nega[:, half * 512:(half + 1) * 512],
                        start=False, stop=True),
                        reads=(trin, "nega0", "negaA"), writes=("ps2",))
                    op("act", lambda s, half=half, lset=lset: s.activation(
                        lset[:, half * 512:(half + 1) * 512],
                        PS[2][0:64, 0:512], AF.Exp),
                        reads=("ps2",),
                        writes=((lname,) if lset is s_L1[0] else
                                (f"{lname}h{half}",)))

        def emit_pre2(i):
            p = i % 2
            op("dve", lambda v: v.tensor_tensor(
                s_R[0][:].rearrange("c (h w) -> c h w", w=128)[:, :, 0:64],
                s_vb[0][:].rearrange("c (h d) -> c h d", d=64),
                s_bch[:].rearrange("c (h o) -> c h o", o=1).broadcast_to(
                    [64, 16, 64]), ALU.mult),
                reads=("vb0", "bch"), writes=("R0",))
            op("dve", lambda v, i=i: v.tensor_tensor(
                s_R[0][:].rearrange("c (h w) -> c h w", w=128)[:, :, 64:128],
                s_kb[0][:].rearrange("c (h d) -> c h d", d=64),
                s_bg[i][:].rearrange("c (h o) -> c h o", o=1).broadcast_to(
                    [64, 16, 64]), ALU.mult),
                reads=("kb0", f"bg{i}"), writes=("R0",))
            for h in range(H):
                ps = PS[5 + h // 8]
                op("pe", lambda t, p=p, h=h, ps=ps: t.matmul(
                    ps[0:64, (h % 8) * 64:(h % 8) * 64 + 64],
                    s_k_fm[p][:, h * 64:(h + 1) * 64],
                    s_k_fm[p][:, h * 64:(h + 1) * 64],
                    start=True, stop=True),
                    reads=(f"k_fm{p}",), writes=(f"ps{5 + h // 8}",))
            for hh in range(2):
                op("dve", lambda v, hh=hh: v.tensor_tensor(
                    s_M[0][:, hh * 512:(hh + 1) * 512],
                    PS[5 + hh][0:64, 0:512],
                    s_L1[0][:, hh * 512:(hh + 1) * 512], ALU.mult),
                    reads=(f"ps{5 + hh}", "L1_0"),
                    writes=(f"M0q{2 * hh}", f"M0q{2 * hh + 1}"))
            for h in range(H):
                op("pe", lambda t, h=h: t.transpose(
                    PB[0][0:64, h * 64:(h + 1) * 64],
                    s_M[0][:, h * 64:(h + 1) * 64], s_ident_bf[0:64, 0:64]),
                    reads=(f"M0q{h // 4}", "ident_bf"), writes=("pb0",))
            op("act", lambda s: s.activation(
                s_MT[0][:], PB[0][0:64, :], AF.Copy),
                reads=("pb0",), writes=("MT0q0", "MT0q1", "MT0q2", "MT0q3"))
            for q in range(4):
                op("pool", lambda g, q=q: g.tensor_add(
                    s_Ah[0][:, q * 256:(q + 1) * 256],
                    s_MT[0][:, q * 256:(q + 1) * 256],
                    s_irep[:, q * 256:(q + 1) * 256]),
                    reads=(f"MT0q{q}", "irep"), writes=(f"Ah0q{q}",))
            op("dve", lambda v, i=i: v.tensor_tensor(
                s_wtm[:].rearrange("c (h d) -> c h d", d=64),
                s_qb[0][:].rearrange("c (h d) -> c h d", d=64),
                s_eg[i][:].rearrange("c (h o) -> c h o", o=1).broadcast_to(
                    [64, 16, 64]), ALU.mult),
                reads=("qb0", f"eg{i}"), writes=("wtm",))
            for h in range(H):
                op("pe", lambda t, h=h: t.transpose(
                    PB[0][0:64, h * 64:(h + 1) * 64],
                    s_wtm[:, h * 64:(h + 1) * 64], s_ident_bf[0:64, 0:64]),
                    reads=("wtm", "ident_bf"), writes=("pb0",))
            op("dve", lambda v, p=p: v.tensor_copy(s_qe_fm[p][:], PB[0][0:64, :]),
               reads=("pb0",), writes=(f"qe_fm{p}",))
            op("pool", lambda g, i=i, p=p: g.tensor_tensor(
                s_kpp[p][:].rearrange("c (h d) -> c h d", d=64),
                s_kb[0][:].rearrange("c (h d) -> c h d", d=64),
                s_el[i][:].rearrange("c (h o) -> c h o", o=1).broadcast_to(
                    [64, 16, 64]), ALU.mult),
                reads=("kb0", f"el{i}"), writes=(f"kpp{p}",))

        def emitA_iter(i, j):
            for grp in range(4):
                ps = PS[3 + grp % 2]
                pn = f"ps{3 + grp % 2}"
                xsrcn = ("R0" if j == 0 else f"X0g{grp}",)
                xsrc = s_R[0] if j == 0 else s_X[0]
                for hh in range(4):
                    h = grp * 4 + hh
                    op("pe", lambda t, h=h, hh=hh, xsrc=xsrc, ps=ps, grp=grp:
                       t.matmul(
                        ps[0:64, hh * 128:(hh + 1) * 128],
                        s_Ah[0][:, h * 64:(h + 1) * 64],
                        xsrc[:, h * 128:(h + 1) * 128],
                        start=True, stop=True),
                        reads=(f"Ah0q{grp}",) + xsrcn, writes=(pn,))
                if grp % 2 == 0:
                    op("dve", lambda v, grp=grp, ps=ps: v.tensor_copy(
                        s_X[0][:, grp * 512:(grp + 1) * 512], ps[0:64, 0:512]),
                        reads=(pn,), writes=(f"X0g{grp}",))
                else:
                    op("act", lambda s, grp=grp, ps=ps: s.activation(
                        s_X[0][:, grp * 512:(grp + 1) * 512], ps[0:64, 0:512],
                        AF.Copy), reads=(pn,), writes=(f"X0g{grp}",))
            if j == 5:
                return
            for h in range(H):
                ps = PS[5 + h // 8]
                op("pe", lambda t, h=h, ps=ps: t.matmul(
                    ps[0:64, (h % 8) * 64:(h % 8) * 64 + 64],
                    s_MT[0][:, h * 64:(h + 1) * 64],
                    s_M[0][:, h * 64:(h + 1) * 64],
                    start=True, stop=True),
                    reads=(f"MT0q{h // 4}", f"M0q{h // 4}"),
                    writes=(f"ps{5 + h // 8}",))
            for h in range(H):
                ps = PS[3 + h // 8]
                op("pe", lambda t, h=h, ps=ps: t.matmul(
                    ps[0:64, (h % 8) * 64:(h % 8) * 64 + 64],
                    s_M[0][:, h * 64:(h + 1) * 64],
                    s_MT[0][:, h * 64:(h + 1) * 64],
                    start=True, stop=True),
                    reads=(f"M0q{h // 4}", f"MT0q{h // 4}"),
                    writes=(f"ps{3 + h // 8}",))
            for q in range(4):
                op("act", lambda s, q=q: s.activation(
                    s_MT[0][:, q * 256:(q + 1) * 256],
                    PS[3 + q // 2][0:64, (q % 2) * 256:(q % 2) * 256 + 256],
                    AF.Copy),
                    reads=(f"ps{3 + q // 2}",), writes=(f"MT0q{q}",))
                op("pool", lambda g, q=q: g.tensor_add(
                    s_Ah[0][:, q * 256:(q + 1) * 256],
                    s_MT[0][:, q * 256:(q + 1) * 256],
                    s_irep[:, q * 256:(q + 1) * 256]),
                    reads=(f"MT0q{q}", "irep"), writes=(f"Ah0q{q}",))
                op("dve", lambda v, q=q: v.tensor_copy(
                    s_M[0][:, q * 256:(q + 1) * 256],
                    PS[5 + q // 2][0:64, (q % 2) * 256:(q % 2) * 256 + 256]),
                    reads=(f"ps{5 + q // 2}",), writes=(f"M0q{q}",))

        def emitA_post(i):
            p = i % 2
            for pi in range(2):
                op("dve", lambda v, p=p, pi=pi: v.tensor_copy(
                    s_u[p][:, pi * 512:(pi + 1) * 512].rearrange(
                        "c (h d) -> c h d", d=64),
                    s_X[0][:, pi * 1024:(pi + 1) * 1024].rearrange(
                        "c (h w) -> c h w", w=128)[:, :, 0:64]),
                    reads=(f"X0g{2 * pi}", f"X0g{2 * pi + 1}"),
                    writes=(f"u{p}h{pi}",))
            op("dve", lambda v: v.tensor_copy(
                s_wtm[:].rearrange("c (h d) -> c h d", d=64),
                s_X[0][:].rearrange("c (h w) -> c h w", w=128)[:, :, 64:128]),
                reads=("X0g0", "X0g1", "X0g2", "X0g3"), writes=("wtm",))
            for h in range(H):
                op("pe", lambda t, h=h: t.transpose(
                    PB[0][0:64, h * 64:(h + 1) * 64],
                    s_wtm[:, h * 64:(h + 1) * 64], s_ident_bf[0:64, 0:64]),
                    reads=("wtm", "ident_bf"), writes=("pb0",))
            op("dve", lambda v, p=p: v.tensor_copy(s_wT[p][:], PB[0][0:64, :]),
               reads=("pb0",), writes=(f"wT{p}",))

        def emitB(i, piece):
            p = i % 2
            tt, half2 = divmod(i, 2)
            if piece == 0:
                for h in range(H):
                    ps = PS[h // 8]
                    op("pe", lambda t, p=p, h=h, ps=ps: t.matmul(
                        ps[0:64, (h % 8) * 64:(h % 8) * 64 + 64],
                        s_k_fm[p][:, h * 64:(h + 1) * 64],
                        s_q_fm[p][:, h * 64:(h + 1) * 64],
                        start=True, stop=True),
                        reads=(f"k_fm{p}", f"q_fm{p}"), writes=(f"ps{h // 8}",))
                for hh in range(2):
                    op("dve", lambda v, p=p, hh=hh: v.tensor_tensor(
                        s_aT[:, hh * 512:(hh + 1) * 512],
                        PS[hh][0:64, 0:512],
                        s_L3[p][:, hh * 512:(hh + 1) * 512], ALU.mult),
                        reads=(f"ps{hh}", f"L3_{p}h{hh}"), writes=(f"aTh{hh}",))
            elif piece == 1:
                for h in range(H):
                    ps = PS[h // 8]
                    op("pe", lambda t, p=p, h=h, ps=ps: t.matmul(
                        ps[0:64, (h % 8) * 64:(h % 8) * 64 + 64],
                        s_wT[p][:, h * 64:(h + 1) * 64],
                        s_Sb[:, h * 64:(h + 1) * 64],
                        start=True, stop=True),
                        reads=(f"wT{p}", "Sb"), writes=(f"ps{h // 8}",))
                for pi in range(2):
                    op("dve", lambda v, p=p, pi=pi: v.tensor_sub(
                        s_vn[:, pi * 512:(pi + 1) * 512],
                        s_u[p][:, pi * 512:(pi + 1) * 512],
                        PS[pi][0:64, 0:512]),
                        reads=(f"u{p}h{pi}", f"ps{pi}"), writes=(f"vnh{pi}",))
            elif piece == 2:
                for h in range(H):
                    ps = PS[h // 8]
                    op("pe", lambda t, p=p, h=h, ps=ps: t.matmul(
                        ps[0:64, (h % 8) * 64:(h % 8) * 64 + 64],
                        s_qe_fm[p][:, h * 64:(h + 1) * 64],
                        s_Sb[:, h * 64:(h + 1) * 64],
                        start=True, stop=False),
                        reads=(f"qe_fm{p}", "Sb"), writes=(f"ps{h // 8}",))
                    op("pe", lambda t, h=h, ps=ps: t.matmul(
                        ps[0:64, (h % 8) * 64:(h % 8) * 64 + 64],
                        s_aT[:, h * 64:(h + 1) * 64],
                        s_vn[:, h * 64:(h + 1) * 64],
                        start=False, stop=True),
                        reads=(f"aTh{h // 8}", f"vnh{h // 8}"),
                        writes=(f"ps{h // 8}",))
                for pi in range(2):
                    op("act", lambda s, tt=tt, half2=half2, pi=pi: s.activation(
                        s_o[tt][half2 * 64:(half2 + 1) * 64,
                                pi * 512:(pi + 1) * 512],
                        PS[pi][0:64, 0:512], AF.Copy),
                        reads=(f"ps{pi}",), writes=(f"o{tt}",))
            elif piece == 3:
                for h in range(H):
                    ps = PS[h // 8]
                    op("pe", lambda t, p=p, h=h, ps=ps: t.matmul(
                        ps[0:64, (h % 8) * 64:(h % 8) * 64 + 64],
                        s_kpp[p][:, h * 64:(h + 1) * 64],
                        s_vn[:, h * 64:(h + 1) * 64],
                        start=True, stop=True),
                        reads=(f"kpp{p}", f"vnh{h // 8}"), writes=(f"ps{h // 8}",))
            else:
                op("pool", lambda g, i=i: g.tensor_tensor(
                    s_S[:].rearrange("c (h d) -> c h d", d=64),
                    s_S[:].rearrange("c (h d) -> c h d", d=64),
                    s_egl[i][:].rearrange("c (h o) -> c h o", o=1).broadcast_to(
                        [64, 16, 64]), ALU.mult),
                    reads=("S", f"egl{i}"), writes=("S",))
                for hh in range(2):
                    op("dve", lambda v, hh=hh: v.tensor_add(
                        s_S[:, hh * 512:(hh + 1) * 512],
                        s_S[:, hh * 512:(hh + 1) * 512],
                        PS[hh][0:64, 0:512]),
                        reads=("S", f"ps{hh}"), writes=("S",))
                op("dve", lambda v: v.tensor_copy(s_Sb[:], s_S[:]),
                   reads=("S",), writes=("Sb",))

        for pc in range(5):
            emit_pre1(0, pc)
        emit_pre2(0)
        for i in range(NCH):
            for j in range(6):
                emitA_iter(i, j)
                if j < 5 and i + 1 < NCH:
                    emit_pre1(i + 1, j)
                if j < 5 and i >= 1:
                    emitB(i - 1, j)
            emitA_post(i)
            if i + 1 < NCH:
                emit_pre2(i + 1)
        for j in range(5):
            emitB(NCH - 1, j)

        def emit_oepi(tt):
            op("act", lambda s, tt=tt: s.activation(
                s_gf[:], s_o[tt][:], AF.Square), reads=(f"o{tt}",), writes=("gf",))
            op("dve", lambda v: v.reduce_sum(
                s_stat[:, 0:8],
                s_gf[:, 0:512].rearrange("p (h d) -> p h d", d=64), axis=AX.X),
                reads=("gf",), writes=("stat",))
            op("dve", lambda v: v.reduce_sum(
                s_m12[:, 0:8],
                s_gf[:, 512:1024].rearrange("p (h d) -> p h d", d=64), axis=AX.X),
                reads=("gf",), writes=("m12",))
            for half in range(2):
                src = s_stat[:, 0:8] if half == 0 else s_m12[:, 0:8]
                sn = "stat" if half == 0 else "m12"
                op("dve", lambda v, src=src: v.tensor_scalar(
                    src, src, 1.0 / 64, 1e-5, ALU.mult, ALU.add),
                    reads=(sn,), writes=(sn,))
                op("act", lambda s, src=src: s.activation(
                    src, src, AF.Ln), reads=(sn,), writes=(sn,))
                op("dve", lambda v, src=src: v.tensor_scalar(
                    src, src, -0.5, None, ALU.mult), reads=(sn,), writes=(sn,))
                op("act", lambda s, src=src: s.activation(
                    src, src, AF.Exp), reads=(sn,), writes=(sn,))
            op("act", lambda s, tt=tt: s.activation(
                s_gf[:], s_gate[tt][:], AF.Silu), reads=(f"gate{tt}",),
                writes=("gf",))
            for half in range(2):
                src = s_stat[:, 0:8] if half == 0 else s_m12[:, 0:8]
                sn = "stat" if half == 0 else "m12"
                op("dve", lambda v, tt=tt, half=half, src=src: v.tensor_tensor(
                    s_sout[tt][:, half * 512:(half + 1) * 512].rearrange(
                        "p (h d) -> p h d", d=64),
                    s_o[tt][:, half * 512:(half + 1) * 512].rearrange(
                        "p (h d) -> p h d", d=64),
                    src.rearrange("p (h o) -> p h o", o=1).broadcast_to(
                        [128, 8, 64]), ALU.mult),
                    reads=(f"o{tt}", sn), writes=(f"sout{tt}",))
            op("dve", lambda v, tt=tt: v.tensor_mul(
                s_sout[tt][:], s_sout[tt][:], s_nwrep[:]),
                reads=(f"sout{tt}", "nwrep"), writes=(f"sout{tt}",))
            op("dve", lambda v, tt=tt: v.tensor_mul(
                s_sout[tt][:], s_sout[tt][:], s_gf[:]),
                reads=(f"sout{tt}", "gf"), writes=(f"sout{tt}",))

        def emit_wo(tt):
            pbase = (0, 5, 3)[tt % 3]

            def tpose(k):
                pr = k % 2
                op("pe", lambda t, k=k, tt=tt: t.transpose(
                    PB[0][:, 0:128], s_sout[tt][:, k * 128:(k + 1) * 128],
                    s_ident_bf[:]), reads=(f"sout{tt}", "ident_bf"),
                    writes=("pb0",))
                op("dve", lambda v, pr=pr: v.tensor_copy(
                    s_ofm[pr][:], PB[0][:, 0:128]), reads=("pb0",),
                    writes=(f"ofm{pr}",))

            tpose(0)
            tpose(1)
            for k in range(NKC):
                pr = k % 2
                for half in range(2):
                    sl, sln = slab(seg, 24 + half)
                    op("pe", lambda t, k=k, pr=pr, half=half, sl=sl,
                       pbase=pbase: t.matmul(
                        PS[pbase + half][:, 0:512], s_ofm[pr][:],
                        sl[:, k * 512:(k + 1) * 512],
                        start=(k == 0), stop=(k == NKC - 1)),
                        reads=(f"ofm{pr}", sln), writes=(f"ps{pbase + half}",))
                if k + 2 < NKC:
                    tpose(k + 2)
            for half in range(2):
                op("act", lambda s, tt=tt, half=half, pbase=pbase: s.activation(
                    s_sout[tt][:, half * 512:(half + 1) * 512],
                    PS[pbase + half][:, 0:512], AF.Copy),
                    reads=(f"ps{pbase + half}",), writes=(f"sout{tt}",))

        def emit_combine(tt):
            op("act", lambda s, tt=tt: s.activation(
                s_gf[:], s_acc[tt][:], AF.Silu),
                reads=(f"acc{tt}",), writes=("gf",))
            op("pool", lambda g, tt=tt: g.tensor_mul(
                s_og[:], s_sout[tt][:], s_gf[:]),
                reads=(f"sout{tt}", "gf"), writes=("og",))
            dma("sp_c", s_acc[tt][:],
                x_ap[t0 + tt * 128:t0 + (tt + 1) * 128, :],
                writes=(f"acc{tt}",))
            op("dve", lambda v, tt=tt: v.tensor_add(
                s_xc[:], s_acc[tt][:], s_og[:]),
                reads=(f"acc{tt}", "og"), writes=("xc",))
            dma("actq", out_ap[t0 + tt * 128:t0 + (tt + 1) * 128, :],
                s_xc[:], reads=("xc",))

        issue_slabs(seg * 26 + 11)
        for tt in range(NTT):
            for k in range(NKC):
                op("pe", lambda t, k=k, tt=tt: t.matmul(
                    PS[4][:, 0:8], s_xn_fm[k][:, tt * 128:(tt + 1) * 128],
                    s_wgate[:, k * 8:(k + 1) * 8], start=(k == 0), stop=False),
                    reads=(f"xnfm{k}", "wgate"), writes=("ps4",))
            op("pe", lambda t: t.matmul(
                PS[4][:, 0:8], s_ones1[0:1, :], s_bgate[0:1, :],
                start=False, stop=True),
                reads=("ones1", "bgate"), writes=("ps4",))
            op("dve", lambda v: v.tensor_copy(s_logit[:], PS[4][:, 0:8]),
               reads=("ps4",), writes=("logit",))
            op("dve", lambda v: v.reduce_max(s_m12[:, 0:1], s_logit[:], axis=AX.X),
               reads=("logit",), writes=("m12",))
            op("dve", lambda v: v.tensor_scalar(
                s_m12[:, 2:3], s_m12[:, 0:1], -1.0, None, ALU.mult),
                reads=("m12",), writes=("m12",))
            op("dve", lambda v: v.tensor_scalar(
                s_stat[:, 0:8], s_logit[:], s_m12[:, 0:1], None, ALU.is_ge),
                reads=("logit", "m12"), writes=("stat",))
            op("dve", lambda v: v.scalar_tensor_tensor(
                s_stat[:, 0:8], s_stat[:, 0:8], NEG, s_logit[:],
                ALU.mult, ALU.add), reads=("stat", "logit"), writes=("stat",))
            op("dve", lambda v: v.reduce_max(s_m12[:, 1:2], s_stat[:, 0:8],
                                             axis=AX.X),
               reads=("stat",), writes=("m12",))
            op("act", lambda s, tt=tt: s.activation(
                s_score[tt][:], s_logit[:], AF.Exp, bias=s_m12[:, 2:3]),
                reads=("logit", "m12"), writes=(f"score{tt}",))
            op("dve", lambda v: v.tensor_scalar(
                s_stat[:, 0:8], s_logit[:], s_m12[:, 1:2], None, ALU.is_ge),
                reads=("logit", "m12"), writes=("stat",))
            op("dve", lambda v, tt=tt: v.tensor_tensor(
                s_score[tt][:], s_score[tt][:], s_stat[:, 0:8], ALU.mult),
                reads=(f"score{tt}", "stat"), writes=(f"score{tt}",))
            op("dve", lambda v, tt=tt: v.reduce_sum(
                s_m12[:, 3:4], s_score[tt][:], axis=AX.X),
                reads=(f"score{tt}",), writes=("m12",))
            op("dve", lambda v: v.reciprocal(s_m12[:, 4:5], s_m12[:, 3:4]),
               reads=("m12",), writes=("m12",))
            op("dve", lambda v, tt=tt: v.tensor_scalar(
                s_score[tt][:], s_score[tt][:], s_m12[:, 4:5], None, ALU.mult),
                reads=(f"score{tt}", "m12"), writes=(f"score{tt}",))
        for ei in range(8):
            issue_slabs(seg * 26 + 11 + 2 * ei)
            if ei == 0:
                for tt in range(NTT):
                    op("dve", lambda v, tt=tt: v.tensor_copy(
                        s_etmp[:, 0:8], s_score[tt][:]),
                        reads=(f"score{tt}",), writes=("etmp",))
                    op("pe", lambda t: t.transpose(
                        PB[0][0:8, 0:128], s_etmp[:, 0:8], s_ident_bf[:]),
                        reads=("etmp", "ident_bf"), writes=("pb0",))
                    op("dve", lambda v, tt=tt: v.tensor_copy(
                        s_scT[tt][:], PB[0][0:8, 0:128]), reads=("pb0",),
                        writes=(f"scT{tt}",))
            for half in range(2):
                sl, sln = slab(seg, 8 + 2 * ei + half)
                for tt in range(NTT):
                    pbase = (0, 5, 3)[tt % 3]
                    for k in range(NKC):
                        op("pe", lambda t, k=k, tt=tt, half=half, sl=sl,
                           pbase=pbase: t.matmul(
                            PS[pbase + half][:, 0:512],
                            s_xn_fm[k][:, tt * 128:(tt + 1) * 128],
                            sl[:, k * 512:(k + 1) * 512],
                            start=(k == 0), stop=(k == NKC - 1)),
                            reads=(f"xnfm{k}", sln), writes=(f"ps{pbase + half}",))
                    if ei == 0:
                        op("pe", lambda t, tt=tt, half=half: t.matmul(
                            PS[2][:, 0:512], s_scT[tt][:],
                            s_bexp[:, half * 512:(half + 1) * 512],
                            start=True, stop=True),
                            reads=(f"scT{tt}", "bexp"), writes=("ps2",))
                        op("dve", lambda v, tt=tt, half=half: v.tensor_copy(
                            s_acc[tt][:, half * 512:(half + 1) * 512],
                            PS[2][:, 0:512]), reads=("ps2",),
                            writes=(f"acc{tt}",))
                    op("act", lambda s, tt=tt, half=half, ei=ei, pbase=pbase:
                       s.activation(
                        s_etmp[:, half * 512:(half + 1) * 512],
                        PS[pbase + half][:, 0:512], AF.Copy,
                        scale=s_score[tt][:, ei:ei + 1]),
                        reads=(f"ps{pbase + half}", f"score{tt}"), writes=("etmp",))
                    op("dve", lambda v, tt=tt, half=half: v.tensor_add(
                        s_acc[tt][:, half * 512:(half + 1) * 512],
                        s_acc[tt][:, half * 512:(half + 1) * 512],
                        s_etmp[:, half * 512:(half + 1) * 512]),
                        reads=(f"acc{tt}", "etmp"), writes=(f"acc{tt}",))
            if 2 <= ei < 2 + NTT:
                emit_oepi(ei - 2)

        issue_slabs(seg * 26 + 26)
        for tt in range(NTT):
            emit_wo(tt)
            if tt >= 1:
                emit_combine(tt - 1)
        emit_combine(NTT - 1)

    ctx.finish()
    block = e(nc.Block())
    ctx.finalize(block)
    stack.close()
    return nc


_NC_CACHE = {}


def kernel(L0_attn_x, **kw):
    inp = {"L0_attn_x": L0_attn_x, **kw}
    T = L0_attn_x.shape[1]
    if T not in _NC_CACHE:
        _NC_CACHE[T] = build_program(T)
    nc = _NC_CACHE[T]
    lay = layout_inputs(inp)
    consts = host_constants()
    base = {**lay, **consts}
    x = np.ascontiguousarray(np.asarray(L0_attn_x, np.float32))
    in_maps = []
    for b in range(x.shape[0]):
        m = dict(base)
        m["x"] = x[b]
        in_maps.append(m)
    res = run_bass_kernel_spmd(nc, in_maps, core_ids=list(range(x.shape[0])))
    return np.stack([r["out"] for r in res.results], axis=0).astype(np.float32)
